# version 02
# baseline (speedup 1.0000x reference)
import contextlib
import numpy as np
import concourse.bass as bass
import concourse.mybir as mybir
from concourse.bass_utils import run_bass_kernel_spmd

F32 = mybir.dt.float32
F16 = mybir.dt.float16
BF16 = mybir.dt.bfloat16
AT = mybir.ActivationFunctionType
OP = mybir.AluOpType

T = 2048
D = 1024
NST = 64
NE = 8
FI = 4096
RNE_C = 1.5 * 2.0 ** 23
DEBUG_TAPS = ()


class P:

    ENGS = ("sync", "gpsimd", "vector", "scalar", "tensor")

    def __init__(self, nc):
        self.nc = nc
        self.ops = {k: [] for k in self.ENGS}
        self.semh = {}
        self.cnt = {}
        for name in ("s_sy", "s_gp", "s_ve", "s_ac", "s_te",
                     "d_sy", "d_gp", "d_ac", "d_gt", "d_st"):
            self.cnt[name] = 0
        self.echo_sb = {}
        self.echo_dr = None

    def make_sems(self, stack):
        for name in list(self.cnt):
            self.semh[name] = stack.enter_context(self.nc.semaphore(name))

    def em(self, eng, fn, is_dma=False):
        self.ops[eng].append(fn)
        self._last_dma = getattr(self, "_last_dma", {})
        self._last_dma[eng] = is_dma
        self._last_marked = getattr(self, "_last_marked", {})
        self._last_marked[eng] = False

    def inc(self, name, amt=1):
        self.cnt[name] += amt
        return self.cnt[name]

    def mark(self, eng):
        sname = {"sync": "s_sy", "gpsimd": "s_gp", "vector": "s_ve",
                 "scalar": "s_ac", "tensor": "s_te"}[eng]
        tv = self.inc(sname)
        h = self.semh[sname]
        already = getattr(self, "_last_marked", {}).get(eng, False)
        self._last_marked = getattr(self, "_last_marked", {})
        last_dma = getattr(self, "_last_dma", {}).get(eng, False)
        if self.ops[eng] and not already and not last_dma:
            prev = self.ops[eng].pop()
            self.em(eng, lambda e, h=h, prev=prev: prev(e).then_inc(h, 1))
        elif last_dma or not self.ops[eng]:
            self.em(eng, lambda e, h=h: e.nop().then_inc(h, 1))
        else:
            self.em(eng, lambda e, h=h: e.drain().then_inc(h, 1))
        self._last_marked[eng] = True
        return sname, tv

    SEM_ENG = {"s_sy": "sync", "s_gp": "gpsimd", "s_ve": "vector",
               "s_ac": "scalar", "s_te": "tensor",
               "d_sy": "sync", "d_gp": "gpsimd", "d_ac": "scalar"}

    def wait(self, eng, tok):
        sname, tv = tok
        h = self.semh[sname]
        box = [max(tv, self.cnt[sname])]
        self.pending = getattr(self, "pending", {})
        self.pending.setdefault(self.SEM_ENG[sname], []).append((sname, box))
        self.em(eng, lambda e, h=h, box=box: e.wait_ge(h, box[0]))
        self._flush(eng)

    def _flush(self, eng):
        self.pending = getattr(self, "pending", {})
        for sname, box in self.pending.pop(eng, []):
            box[0] = max(box[0], self.cnt[sname])

    def finalize_waits(self):
        self.pending = getattr(self, "pending", {})
        for eng in list(self.pending):
            self._flush(eng)

    def selfsync(self, eng):
        self.wait(eng, self.mark(eng))

    def barrier(self, engs=None):
        engs = engs or self.ENGS
        for _phase in range(2):
            toks = {eng: self.mark(eng) for eng in engs}
            for eng in engs:
                for other, tok in toks.items():
                    if other != eng:
                        self.wait(eng, tok)

    ECHO_THRESH = 600_000

    def dma(self, qeng, dst, src, **kw):
        sname = {"sync": "d_sy", "gpsimd": "d_gp", "scalar": "d_ac"}[qeng]
        h = self.semh[sname]

        def nbytes(ap):
            n = 1
            for s_ in ap.shape:
                n *= s_
            return n * mybir.dt.size(ap.dtype)

        if max(nbytes(dst), nbytes(src)) > self.ECHO_THRESH:
            ht = self.semh["d_gt" if qeng == "gpsimd" else "d_st"]
            self.em(qeng, lambda e, ht=ht, dst=dst, src=src, kw=kw:
                    e.dma_start(dst, src, **kw).then_inc(ht, 16), is_dma=True)
            tv = self.inc(sname, 16)
            ed, esr = self.echo_sb[qeng], self.echo_dr
            self.em(qeng, lambda e, h=h, ed=ed, esr=esr:
                    e.dma_start(ed, esr).then_inc(h, 16), is_dma=True)
            return sname, tv
        tv = self.inc(sname, 16)
        self.em(qeng, lambda e, h=h, dst=dst, src=src, kw=kw:
                e.dma_start(dst, src, **kw).then_inc(h, 16), is_dma=True)
        return sname, tv


def fmv(ap, dh=8):
    return ap.rearrange("q (dh t) -> q dh t", dh=dh)


def emit_program(p, A, taps):
    nc = p.nc
    sb = {}

    def S(name, shape, dtype):
        sb[name] = nc.alloc_sbuf_tensor("sb_" + name, shape, dtype).ap()
        return sb[name]

    VECS = ["ln_g", "ln_b", "conv_b", "Dp", "b_dt", "ln1_w", "ln1_b",
            "ln2_w", "ln2_b", "decay_v", "u_v", "b_proj"]
    for n in VECS:
        S(n, [128, 8], F32)
    S("b_fc_fm", [128, 32], F32)
    S("cw", [128, 32], F32)
    S("A_row", [1, 64], F32)
    S("Amov", [64, 16], F32)
    S("A_cm", [128, 16], F32)
    S("S1c", [64, 128], F32)
    S("S2c", [64, 16], F32)
    S("Ssum", [128, 32], BF16)
    S("Ssel3", [64, 2048], BF16)
    S("eyef", [128, 128], F32)
    S("eye", [128, 128], BF16)
    S("iota_s", [128, 512], F32)
    S("ones_c", [128, 1], F32)
    S("eps_c", [128, 1], F32)
    S("ones_r1f", [1, 512], F32)
    S("ones_r1b", [1, 16], BF16)
    S("ew", [128, 8], F32)
    S("eu", [128, 8], F32)
    S("row1", [1, 512], F32)
    S("row2", [1, 512], F32)
    S("row3", [1, 512], F32)
    S("aff", [8, 2048], F32)
    S("E8", [8, 512], F32)
    S("lo8", [8, 1], F32)
    S("hi8", [8, 1], F32)
    S("mid8", [8, 1], F32)
    S("cnt8", [8, 1], F32)
    S("tmp8a", [8, 1], F32)
    S("tmp8b", [8, 1], F32)
    S("slotcol", [128, 128], F32)
    S("affcol", [128, 128], F32)
    S("echo_g", [128, 16], F32)
    S("echo_s", [128, 16], F32)
    p.echo_sb = {"gpsimd": sb["echo_g"], "sync": sb["echo_s"],
                 "scalar": sb["echo_s"]}
    p.echo_dr = A["echo_d"]
    S("carry", [128, 512], F32)
    S("wa", [128, 16384], BF16)
    S("wb", [128, 16384], BF16)
    S("wc", [128, 16384], BF16)
    S("fa", [128, 8192], F32)
    S("fb", [128, 4096], F32)
    S("ga", [128, 8192], BF16)
    S("gb", [128, 8192], BF16)
    sb["msk"] = sb["fa"][0:8, 0:2048]
    sb["slotx"] = sb["fa"][0:8, 2048:4096]
    sb["iot8"] = sb["fa"][0:8, 4096:6144]

    ps = [nc.alloc_psum_tensor(f"ps{i}", [128, 512], F32).ap() for i in range(8)]

    def TE(fn):
        p.em("tensor", fn)

    def VE(fn, nd=False):
        if not nd:
            p.em("vector", lambda v: v.drain())
        p.em("vector", fn)

    def AC(fn, nd=False):
        if not nd:
            p.em("scalar", lambda s: s.drain())
        p.em("scalar", fn)

    def GP(fn):
        p.em("gpsimd", fn)

    def tapout(name, src_ap):
        if name in taps:
            p.barrier()
            tok = p.dma("sync", A["tap_" + name], src_ap)
            for eng in P.ENGS:
                p.wait(eng, tok)

    def tapout_d(name, dram_ap):
        tapout(name, dram_ap)


    toks = []
    for n in VECS:
        toks.append(p.dma("gpsimd", sb[n],
                          A[n].rearrange("a (dh q) -> a q dh", q=128),
                          allow_slow_non_contiguous=True))
    toks.append(p.dma("gpsimd", sb["b_fc_fm"],
                      A["b_fc"].rearrange("a (dh q) -> a q dh", q=128),
                      allow_slow_non_contiguous=True))
    toks.append(p.dma("gpsimd", sb["cw"].rearrange("q (k dh) -> q k dh", k=4),
                      A["conv_wT"].rearrange("k (dh q) -> q k dh", q=128),
                      allow_slow_non_contiguous=True))
    toks.append(p.dma("sync", sb["A_row"], A["A_log"]))
    toks.append(p.dma("sync", sb["S1c"], A["S1c"]))
    toks.append(p.dma("sync", sb["S2c"], A["S2c"]))
    toks.append(p.dma("sync", sb["Ssum"], A["Ssum"]))
    toks.append(p.dma("sync", sb["Ssel3"], A["Ssel3"]))
    toks.append(p.dma("sync", sb["eyef"], A["eyef"]))
    toks.append(p.dma("sync", sb["eye"], A["eye"]))
    for eng in ("vector", "scalar", "tensor", "gpsimd"):
        for tok in toks:
            p.wait(eng, tok)
    VE(lambda v: v.memset(sb["ones_c"], 1.0))
    VE(lambda v: v.memset(sb["eps_c"], 1e-5))
    VE(lambda v: v.memset(sb["ones_r1f"], 1.0))
    VE(lambda v: v.memset(sb["ones_r1b"], 1.0))
    VE(lambda v: v.memset(sb["carry"], 0.0))
    GP(lambda g: g.iota(sb["iota_s"], [[1, 512]], base=0, channel_multiplier=0,
                        allow_small_or_imprecise_dtypes=True))
    GP(lambda g: g.iota(sb["iot8"], [[1, 2048]], base=0, channel_multiplier=0,
                        allow_small_or_imprecise_dtypes=True))
    AC(lambda s: s.activation(sb["A_row"], sb["A_row"], AT.Exp))
    t_ac = p.mark("scalar")
    p.wait("vector", t_ac)
    VE(lambda v: v.tensor_scalar(sb["A_row"], sb["A_row"], -1.0, None, OP.mult))
    AC(lambda s: s.activation(sb["ew"], sb["decay_v"], AT.Exp))
    t_ac = p.mark("scalar")
    p.wait("vector", t_ac)
    VE(lambda v: v.tensor_scalar(sb["ew"], sb["ew"], -1.0, None, OP.mult))
    t_ve = p.mark("vector")
    p.wait("scalar", t_ve)
    AC(lambda s: s.activation(sb["ew"], sb["ew"], AT.Exp))
    AC(lambda s: s.activation(sb["eu"], sb["u_v"], AT.Exp))
    t_ve2 = p.mark("vector")
    p.wait("tensor", t_ve2)
    TE(lambda t: t.matmul(ps[0][0:64, 0:1], sb["A_row"],
                          sb["ones_r1f"][0:1, 0:1], start=True, stop=True))
    t_te = p.mark("tensor")
    p.wait("vector", t_te)
    VE(lambda v: v.tensor_copy(sb["fa"][0:64, 0:1], ps[0][0:64, 0:1]))
    p.selfsync("vector")
    VE(lambda v: v.tensor_scalar(sb["Amov"], sb["S2c"], sb["fa"][0:64, 0:1],
                                 None, OP.mult))
    t_ve = p.mark("vector")
    p.wait("tensor", t_ve)
    TE(lambda t: t.matmul(ps[1][:, 0:16], sb["S1c"], sb["Amov"],
                          start=True, stop=True))
    t_te = p.mark("tensor")
    p.wait("vector", t_te)
    VE(lambda v: v.tensor_copy(sb["A_cm"], ps[1][:, 0:16]))
    p.barrier()
    tapout("A_cm", sb["A_cm"])

    xTv = A["xT"].rearrange("(dh q) t -> q dh t", q=128)

    def ln_stats_apply(tt, src_f32_4096, dst_xn_slice, dst_xnn_slice,
                       gcol, bcol, want_xnn=True):
        xt = src_f32_4096
        xsq = sb["fa"][:, 4096:8192]
        VE(lambda v: v.tensor_tensor(xsq, xt, xt, OP.mult))
        t_ve = p.mark("vector")
        p.wait("tensor", t_ve)
        for dh in range(8):
            TE(lambda t, dh=dh: t.matmul(
                ps[0][0:1, :], sb["ones_c"], fmv(xt)[:, dh, :],
                start=(dh == 0), stop=(dh == 7)))
        for dh in range(8):
            TE(lambda t, dh=dh: t.matmul(
                ps[1][0:1, :], sb["ones_c"], fmv(xsq, 8)[:, dh, :],
                start=(dh == 0), stop=(dh == 7)))
        t_te = p.mark("tensor")
        p.wait("vector", t_te)
        r1, r2, r3 = sb["row1"], sb["row2"], sb["row3"]
        sl = slice(0, 512)
        VE(lambda v: v.tensor_scalar(r1[0:1, sl], ps[0][0:1, :], 1.0 / D, None, OP.mult))
        VE(lambda v: v.tensor_scalar(r2[0:1, sl], ps[1][0:1, :], 1.0 / D, None, OP.mult))
        VE(lambda v: v.tensor_tensor(r3[0:1, sl], r1[0:1, sl], r1[0:1, sl], OP.mult))
        VE(lambda v: v.tensor_tensor(r2[0:1, sl], r2[0:1, sl], r3[0:1, sl], OP.subtract))
        t_ve = p.mark("vector")
        p.wait("scalar", t_ve)
        AC(lambda s: s.activation(r3[0:1, sl], r2[0:1, sl], AT.Sqrt, bias=sb["eps_c"][0:1, :]))
        t_ac = p.mark("scalar")
        p.wait("vector", t_ac)
        VE(lambda v: v.reciprocal(r3[0:1, sl], r3[0:1, sl]))
        if want_xnn:
            VE(lambda v: v.tensor_tensor(r2[0:1, sl], r2[0:1, sl], r3[0:1, sl], OP.mult))
            VE(lambda v: v.tensor_tensor(r2[0:1, sl], r2[0:1, sl], r3[0:1, sl], OP.mult))
            t_ve = p.mark("vector")
            p.wait("scalar", t_ve)
            AC(lambda s: s.activation(r2[0:1, sl], r2[0:1, sl], AT.Sqrt, bias=sb["eps_c"][0:1, :]))
            t_ac = p.mark("scalar")
            p.wait("vector", t_ac)
            VE(lambda v: v.reciprocal(r2[0:1, sl], r2[0:1, sl]))
            VE(lambda v: v.tensor_tensor(r2[0:1, sl], r2[0:1, sl], r3[0:1, sl], OP.mult))
        t_ve = p.mark("vector")
        p.wait("tensor", t_ve)
        TE(lambda t: t.matmul(ps[2], sb["ones_r1f"][0:1, 0:128], r1[0:1, sl],
                              start=True, stop=True))
        TE(lambda t: t.matmul(ps[3], sb["ones_r1f"][0:1, 0:128], r3[0:1, sl],
                              start=True, stop=True))
        if want_xnn:
            TE(lambda t: t.matmul(ps[4], sb["ones_r1f"][0:1, 0:128], r2[0:1, sl],
                                  start=True, stop=True))
        t_te = p.mark("tensor")
        p.wait("vector", t_te)
        xc = sb["fa"]
        for dh in range(8):
            VE(lambda v, dh=dh: v.tensor_tensor(
                fmv(xc, 16)[:, dh, :], fmv(xt)[:, dh, :], ps[2], OP.subtract))
        if dst_xn_slice is not None:
            for dh in range(8):
                VE(lambda v, dh=dh: v.tensor_tensor(
                    dst_xn_slice(dh), fmv(xc, 16)[:, dh, :], ps[3], OP.mult))
        if want_xnn:
            for dh in range(8):
                VE(lambda v, dh=dh: v.tensor_tensor(
                    fmv(xc, 16)[:, 8 + dh, :], fmv(xc, 16)[:, dh, :], ps[4], OP.mult))
            for dh in range(8):
                VE(lambda v, dh=dh: v.tensor_scalar(
                    dst_xnn_slice(dh), fmv(xc, 16)[:, 8 + dh, :],
                    gcol[:, dh:dh + 1], bcol[:, dh:dh + 1], OP.mult, OP.add))
        t_ve = p.mark("vector")
        p.wait("tensor", t_ve)
        p.wait("scalar", t_ve)
        return t_ve

    tok_win = p.dma("gpsimd", sb["wa"],
                    A["WinT"].rearrange("(kh q) o -> q kh o", q=128))
    tok_gw = p.dma("gpsimd", sb["ga"][:, 0:64],
                   A["gate_wT"].rearrange("(kh q) o -> q kh o", q=128))

    prev_mark = None
    for tt in range(4):
        xt = sb["fb"]
        if prev_mark is not None:
            p.wait("sync", prev_mark)
        tok = p.dma("sync", fmv(xt), xTv[:, :, tt * 512:(tt + 1) * 512])
        p.wait("vector", tok)
        prev_mark = ln_stats_apply(
            tt, xt,
            lambda dh, tt=tt: fmv(sb["wb"])[:, dh, tt * 512:(tt + 1) * 512],
            lambda dh, tt=tt: fmv(sb["wc"])[:, dh, tt * 512:(tt + 1) * 512],
            sb["ln_g"], sb["ln_b"])
    p.wait("sync", prev_mark)
    tok_xn = p.dma("sync", A["xn_fm"], sb["wb"])
    tapout("xn_bf", sb["wb"])
    tapout("xnn_bf", sb["wc"])

    p.wait("tensor", tok_gw)
    gwv = sb["ga"][:, 0:64].rearrange("q (kh o) -> q kh o", kh=8)
    for tt in range(4):
        slt = slice(tt * 512, (tt + 1) * 512)
        for kh in range(8):
            TE(lambda t, kh=kh, slt=slt: t.matmul(
                ps[7][0:8, :], gwv[:, kh, :], fmv(sb["wb"])[:, kh, slt],
                start=(kh == 0), stop=(kh == 7)))
        t_te = p.mark("tensor")
        p.wait("scalar", t_te)
        AC(lambda s: s.activation(
            sb["E8"], ps[7][0:8, :], AT.Exp))
        t_ac = p.mark("scalar")
        p.wait("tensor", t_ac)
        p.wait("vector", t_ac)
        TE(lambda t: t.matmul(
            ps[0][0:1, :], sb["ones_c"][0:8, :], sb["E8"],
            start=True, stop=True))
        t_te = p.mark("tensor")
        p.wait("vector", t_te)
        VE(lambda v: v.reciprocal(sb["row1"], ps[0][0:1, :]))
        t_ve = p.mark("vector")
        p.wait("tensor", t_ve)
        TE(lambda t: t.matmul(
            ps[1][0:8, :], sb["ones_r1f"][0:1, 0:8], sb["row1"],
            start=True, stop=True))
        t_te = p.mark("tensor")
        p.wait("vector", t_te)
        VE(lambda v, slt=slt: v.tensor_tensor(
            sb["aff"][:, slt], sb["E8"], ps[1][0:8, :], OP.mult))
        prev_aff = p.mark("vector")
        p.wait("tensor", prev_aff)
    tapout("aff", sb["aff"])

    xntm_img = A["xn_tm"].rearrange("(tc q) d -> tc q d", q=128)
    prev_tm = None
    for tc in range(16):
        if prev_tm is not None:
            p.wait("scalar", prev_tm)
        for dh in range(8):
            TE(lambda t, tc=tc, dh=dh: t.transpose(
                ps[6].bitcast(BF16)[:, 0:128],
                fmv(sb["wb"])[:, dh, tc * 128:(tc + 1) * 128], sb["eye"]))
            t_te = p.mark("tensor")
            p.wait("scalar", t_te)
            AC(lambda s, dh=dh: s.copy(
                sb["ga"][:, 1024 + dh * 128:1024 + (dh + 1) * 128],
                ps[6].bitcast(BF16)[:, 0:128]))
            t_ac = p.mark("scalar")
            p.wait("tensor", t_ac)
        t_ac = p.mark("scalar")
        p.wait("sync", t_ac)
        prev_tm = p.dma("sync", xntm_img[tc], sb["ga"][:, 1024:2048])

    p.wait("tensor", tok_win)
    p.wait("tensor", tok_xn)
    wav = fmv(sb["wa"])
    for tt in range(4):
        slt = slice(tt * 512, (tt + 1) * 512)
        for oc in range(16):
            bank = oc % 2 + 5
            for kh in range(8):
                TE(lambda t, oc=oc, kh=kh, bank=bank, slt=slt: t.matmul(
                    ps[bank], wav[:, kh, oc * 128:(oc + 1) * 128],
                    fmv(sb["wc"])[:, kh, slt], start=(kh == 0), stop=(kh == 7)))
            t_te = p.mark("tensor")
            if oc < 8:
                p.wait("vector", t_te)
                VE(lambda v, oc=oc, tt=tt, bank=bank: v.tensor_copy(
                    fmv(sb["wb"])[:, oc, tt * 512:(tt + 1) * 512], ps[bank]))
                t_ve = p.mark("vector")
                p.wait("tensor", t_ve)
            else:
                p.wait("scalar", t_te)
                dst = sb["ga"] if oc < 12 else sb["gb"]
                AC(lambda s, oc=oc, tt=tt, bank=bank, dst=dst: s.activation(
                    fmv(dst, 4)[:, (oc - 8) % 4, tt * 512:(tt + 1) * 512],
                    ps[bank], AT.Silu))
                t_ac = p.mark("scalar")
                p.wait("tensor", t_ac)
    p.barrier()
    tok_sza = p.dma("sync", fmv(A["siluz_fm"])[:, 0:4, :], fmv(sb["ga"], 4))
    tok_szb = p.dma("sync", fmv(A["siluz_fm"])[:, 4:8, :], fmv(sb["gb"], 4))
    if "siluz" in taps:
        p.barrier()
        ta = p.dma("sync", fmv(A["tap_siluz"])[:, 0:4, :], fmv(sb["ga"], 4))
        tb = p.dma("sync", fmv(A["tap_siluz"])[:, 4:8, :], fmv(sb["gb"], 4))
        for eng in P.ENGS:
            p.wait(eng, tb)
    tapout("x_main", sb["wb"])

    for dh in range(8):
        accs = sb["fa"][:, 0:2048]
        VE(lambda v, dh=dh, accs=accs: v.tensor_scalar(
            accs, fmv(sb["wb"])[:, dh, :], sb["cw"][:, 24 + dh:25 + dh], None, OP.mult))
        for k in range(3):
            sh = 3 - k
            VE(lambda v, dh=dh, k=k, sh=sh, accs=accs: v.scalar_tensor_tensor(
                accs[:, sh:2048], fmv(sb["wb"])[:, dh, 0:2048 - sh],
                sb["cw"][:, k * 8 + dh:k * 8 + dh + 1], accs[:, sh:2048],
                OP.mult, OP.add))
        t_ve = p.mark("vector")
        p.wait("scalar", t_ve)
        AC(lambda s, dh=dh, accs=accs: s.activation(
            fmv(sb["wc"])[:, dh, :], accs,
            AT.Silu, bias=sb["conv_b"][:, dh:dh + 1]))
        t_ac = p.mark("scalar")
        p.wait("vector", t_ac)
    p.barrier()
    tok_xc = p.dma("sync", A["xconv_fm"], sb["wc"])
    tapout("x_conv", sb["wc"])

    tok_wdt = p.dma("gpsimd", sb["wa"][:, 0:8192],
                    A["WdtT"].rearrange("(kh q) o -> q kh o", q=128))
    p.wait("tensor", tok_wdt)
    wdtv = sb["wa"][:, 0:8192].rearrange("q (kh o) -> q kh o", kh=8)
    for tt in range(4):
        slt = slice(tt * 512, (tt + 1) * 512)
        for oc in range(8):
            bank = oc % 2 + 5
            for kh in range(8):
                TE(lambda t, oc=oc, kh=kh, bank=bank, slt=slt: t.matmul(
                    ps[bank], wdtv[:, kh, oc * 128:(oc + 1) * 128],
                    fmv(sb["wc"])[:, kh, slt], start=(kh == 0), stop=(kh == 7)))
            t_te = p.mark("tensor")
            p.wait("scalar", t_te)
            AC(lambda s, oc=oc, bank=bank: s.activation(
                sb["fa"][:, 0:512], ps[bank], AT.Exp,
                bias=sb["b_dt"][:, oc:oc + 1]))
            AC(lambda s, oc=oc, tt=tt: s.activation(
                fmv(sb["wb"].bitcast(F16))[:, oc, tt * 512:(tt + 1) * 512],
                sb["fa"][:, 0:512], AT.Ln, bias=sb["ones_c"]))
            t_ac = p.mark("scalar")
            p.wait("tensor", t_ac)
    p.barrier()
    tok_dt = p.dma("sync", A["dt_fm"], sb["wb"].bitcast(F16))
    tapout("dt", sb["wb"].bitcast(F16))

    p.wait("gpsimd", tok_sza)
    tok_wB = p.dma("gpsimd", sb["ga"][:, 0:512],
                   A["WBT"].rearrange("(kh q) o -> q kh o", q=128))
    tok_wC = p.dma("gpsimd", sb["ga"][:, 512:1024],
                   A["WCT"].rearrange("(kh q) o -> q kh o", q=128))
    p.wait("tensor", tok_wB)
    p.wait("tensor", tok_wC)
    for tt in range(4):
        slt = slice(tt * 512, (tt + 1) * 512)
        for which, base in (("B", 0), ("C", 512)):
            bank = 5 if which == "B" else 6
            stat = sb["ga"][:, base:base + 512].rearrange("q (kh o) -> q kh o", kh=8)
            for kh in range(8):
                TE(lambda t, kh=kh, bank=bank, slt=slt, stat=stat: t.matmul(
                    ps[bank][0:64, :], stat[:, kh, 0:64],
                    fmv(sb["wc"])[:, kh, slt], start=(kh == 0), stop=(kh == 7)))
            t_te = p.mark("tensor")
            p.wait("vector", t_te)
            fbb = sb["fb"].bitcast(BF16)
            dst = fbb[0:64, 0:2048] if which == "B" else fbb[0:64, 2048:4096]
            VE(lambda v, dst=dst, tt=tt, bank=bank: v.tensor_copy(
                dst[:, tt * 512:(tt + 1) * 512], ps[bank][0:64, :]))
            t_ve = p.mark("vector")
            p.wait("tensor", t_ve)
    p.barrier()
    tapout("Bm", sb["fb"].bitcast(BF16)[0:64, 0:2048])
    tapout("Cm", sb["fb"].bitcast(BF16)[0:64, 2048:4096])


    p.barrier()
    msk, slx, io8 = sb["msk"], sb["slotx"], sb["iot8"]
    VE(lambda v: v.memset(sb["lo8"], 0.0))
    VE(lambda v: v.memset(sb["hi8"], 1.0))
    for it in range(30):
        VE(lambda v: v.tensor_tensor(sb["mid8"], sb["lo8"], sb["hi8"], OP.add))
        VE(lambda v: v.tensor_scalar(sb["mid8"], sb["mid8"], 0.5, None, OP.mult))
        VE(lambda v: v.tensor_scalar(msk, sb["aff"], sb["mid8"], None, OP.is_ge))
        VE(lambda v: v.tensor_reduce(sb["cnt8"], msk, mybir.AxisListType.X, OP.add))
        VE(lambda v: v.tensor_scalar(sb["cnt8"], sb["cnt8"], 512.0, None, OP.is_ge))
        VE(lambda v: v.tensor_tensor(sb["tmp8a"], sb["mid8"], sb["lo8"], OP.subtract))
        VE(lambda v: v.tensor_tensor(sb["tmp8a"], sb["tmp8a"], sb["cnt8"], OP.mult))
        VE(lambda v: v.tensor_tensor(sb["lo8"], sb["lo8"], sb["tmp8a"], OP.add))
        VE(lambda v: v.tensor_tensor(sb["tmp8b"], sb["hi8"], sb["mid8"], OP.subtract))
        VE(lambda v: v.tensor_tensor(sb["tmp8b"], sb["tmp8b"], sb["cnt8"], OP.mult))
        VE(lambda v: v.tensor_tensor(sb["hi8"], sb["mid8"], sb["tmp8b"], OP.add))
    VE(lambda v: v.tensor_scalar(msk, sb["aff"], sb["lo8"], None, OP.is_ge))
    GP(lambda g: g.iota(io8, [[1, 2048]], base=0, channel_multiplier=0,
                        allow_small_or_imprecise_dtypes=True))
    t_gp = p.mark("gpsimd")
    p.wait("vector", t_gp)
    VE(lambda v: v.memset(sb["fa"][0:8, 6144:8192], 1.0))
    VE(lambda v: v.tensor_tensor_scan(slx, sb["fa"][0:8, 6144:8192], msk,
                                      0.0, OP.mult, OP.add))
    VE(lambda v: v.tensor_tensor(slx, slx, msk, OP.mult))
    VE(lambda v: v.tensor_scalar(slx, slx, 1.0, None, OP.subtract))
    t_ve = p.mark("vector")
    p.wait("tensor", t_ve)
    for c in range(16):
        TE(lambda t, c=c: t.transpose(ps[6][:, 0:8], slx[:, c * 128:(c + 1) * 128],
                                      sb["eyef"][0:8, 0:8]))
        TE(lambda t, c=c: t.transpose(ps[7][:, 0:8], sb["aff"][:, c * 128:(c + 1) * 128],
                                      sb["eyef"][0:8, 0:8]))
        t_te = p.mark("tensor")
        p.wait("vector", t_te)
        VE(lambda v, c=c: v.tensor_copy(sb["slotcol"][:, c * 8:(c + 1) * 8], ps[6][:, 0:8]))
        VE(lambda v, c=c: v.tensor_copy(sb["affcol"][:, c * 8:(c + 1) * 8], ps[7][:, 0:8]))
        t_ve = p.mark("vector")
        p.wait("tensor", t_ve)
    p.barrier()

    tapout("slotx", sb["slotx"])
    tapout("thr", sb["lo8"])
    tapout("slotcol", sb["slotcol"])
    tapout_d("xntm", A["xn_tm"])

    xe_img = A["xe_tm"].rearrange("(e q) f -> e q f", q=128)
    wgv = A["wgT"].rearrange("(e kh q) o -> e q kh o", q=128, kh=8)
    wuv = A["wuT"].rearrange("(e kh q) o -> e q kh o", q=128, kh=8)
    wdv = A["wdT"].rearrange("(e2 ic q) o -> e2 ic q o", q=128, ic=32)
    xnv = A["xn_tm"].rearrange("(tc q) (dc w) -> q tc dc w", q=128, w=128)
    for e in range(NE):
        for c in range(16):
            VE(lambda v, c=c, e=e: v.tensor_scalar(
                sb["ga"].rearrange("q (c s) -> q c s", c=16)[:, c, :],
                sb["iota_s"], sb["slotcol"][:, c * 8 + e:c * 8 + e + 1], None,
                OP.is_equal))
        t_ve = p.mark("vector")
        p.wait("tensor", t_ve)
        p.wait("gpsimd", t_ve)
        st_slot = [0, 4096]
        toks = [None] * 8
        marks = [None] * 8
        toks[0] = p.dma("gpsimd", sb["gb"][:, 0:2048], xnv[:, :, 0, :])
        for dc in range(8):
            if dc + 1 < 8:
                if dc >= 1:
                    p.wait("gpsimd", marks[dc - 1])
                sl = st_slot[(dc + 1) % 2]
                toks[dc + 1] = p.dma("gpsimd", sb["gb"][:, sl:sl + 2048],
                                     xnv[:, :, dc + 1, :])
            bank = dc % 2
            sl = st_slot[dc % 2]
            p.wait("tensor", toks[dc])
            for tc in range(16):
                TE(lambda t, tc=tc, bank=bank, sl=sl: t.matmul(
                    ps[bank], sb["gb"][:, sl:sl + 2048].rearrange("q (tc w) -> q tc w", tc=16)[:, tc, :],
                    sb["ga"].rearrange("q (c s) -> q c s", c=16)[:, tc, :],
                    start=(tc == 0), stop=(tc == 15)))
            marks[dc] = t_te = p.mark("tensor")
            p.wait("scalar", t_te)
            AC(lambda s, dc=dc, bank=bank: s.copy(
                sb["wc"].rearrange("q (dc s) -> q dc s", dc=32)[:, dc, :], ps[bank]))
            t_ac = p.mark("scalar")
            p.wait("tensor", t_ac)
        p.wait("gpsimd", t_ac)
        if e == 0:
            tapout("xin0", sb["wc"][:, 0:4096])
        gu_slots = [(5120, 6144), (8192, 9216)]
        gtoks = [None] * 32
        gmarks = [None] * 32

        def gu_load(ic):
            sg, su = gu_slots[ic % 2]
            p.dma("gpsimd", sb["wc"][:, sg:sg + 1024],
                  wgv[e, :, :, ic * 128:(ic + 1) * 128])
            gtoks[ic] = p.dma("gpsimd", sb["wc"][:, su:su + 1024],
                              wuv[e, :, :, ic * 128:(ic + 1) * 128])
        gu_load(0)
        for ic in range(32):
            if ic + 1 < 32:
                if ic >= 1:
                    p.wait("gpsimd", gmarks[ic - 1])
                gu_load(ic + 1)
            sg, su = gu_slots[ic % 2]
            p.wait("tensor", gtoks[ic])
            for kh in range(8):
                TE(lambda t, kh=kh, sg=sg: t.matmul(
                    ps[2], sb["wc"][:, sg:sg + 1024].rearrange("q (kh w) -> q kh w", kh=8)[:, kh, :],
                    sb["wc"].rearrange("q (dc s) -> q dc s", dc=32)[:, kh, :],
                    start=(kh == 0), stop=(kh == 7)))
            for kh in range(8):
                TE(lambda t, kh=kh, su=su: t.matmul(
                    ps[3], sb["wc"][:, su:su + 1024].rearrange("q (kh w) -> q kh w", kh=8)[:, kh, :],
                    sb["wc"].rearrange("q (dc s) -> q dc s", dc=32)[:, kh, :],
                    start=(kh == 0), stop=(kh == 7)))
            gmarks[ic] = t_te = p.mark("tensor")
            p.wait("scalar", t_te)
            AC(lambda s: s.activation(sb["gb"][:, 2048:2560], ps[2], AT.Silu))
            t_ac = p.mark("scalar")
            p.wait("vector", t_ac)
            VE(lambda v, ic=ic: v.tensor_tensor(
                sb["wa"].rearrange("q (ic s) -> q ic s", ic=32)[:, ic, :],
                sb["gb"][:, 2048:2560], ps[3], OP.mult))
            t_ve = p.mark("vector")
            p.wait("tensor", t_ve)
        if e == 0:
            tapout("hi0", sb["wa"][:, 0:16384])
        dn_slots = [10240, 11264]
        dtoks = [None] * 32
        dmarks = [None] * 32
        dtoks[0] = p.dma("gpsimd", sb["wc"][:, 10240:11264], wdv[e, 0, :, :])
        for ic in range(32):
            if ic + 1 < 32:
                if ic >= 1:
                    p.wait("gpsimd", dmarks[ic - 1])
                sl = dn_slots[(ic + 1) % 2]
                dtoks[ic + 1] = p.dma("gpsimd", sb["wc"][:, sl:sl + 1024],
                                      wdv[e, ic + 1, :, :])
            sl = dn_slots[ic % 2]
            p.wait("tensor", dtoks[ic])
            for sc in range(4):
                for dh2 in range(2):
                    TE(lambda t, ic=ic, sc=sc, dh2=dh2, sl=sl: t.matmul(
                        ps[sc * 2 + dh2],
                        sb["wa"].rearrange("q (ic s) -> q ic s", ic=32)[:, ic, sc * 128:(sc + 1) * 128],
                        sb["wc"][:, sl + dh2 * 512:sl + (dh2 + 1) * 512],
                        start=(ic == 0), stop=(ic == 31)))
            dmarks[ic] = p.mark("tensor")
        t_te = p.mark("tensor")
        p.wait("scalar", t_te)
        for sc in range(4):
            for dh2 in range(2):
                AC(lambda s, sc=sc, dh2=dh2: s.copy(
                    sb["gb"][:, sc * 1024 + dh2 * 512: sc * 1024 + (dh2 + 1) * 512],
                    ps[sc * 2 + dh2]))
        t_ac = p.mark("scalar")
        p.wait("tensor", t_ac)
        p.wait("gpsimd", t_ac)
        p.wait("sync", t_ac)
        tok = p.dma("sync", xe_img[e], sb["gb"][:, 0:4096])
        p.wait("vector", tok)
        p.wait("scalar", tok)
        p.wait("gpsimd", tok)
        if e == 0:
            tapout("xe0", sb["gb"][:, 0:4096])
    p.barrier()

    wcf16 = sb["wc"].bitcast(F16)
    fbb = sb["fb"].bitcast(BF16)
    dtimg = A["dt_fm"].rearrange("q (dh t) -> q dh t", dh=8)
    xcimg = A["xconv_fm"].rearrange("q (dh t) -> q dh t", dh=8)
    szimg = A["siluz_fm"].rearrange("q (dh t) -> q dh t", dh=8)
    spimg = A["spre_fm"].rearrange("q (dh t) -> q dh t", dh=8)
    DTS = [0, 1024]
    UDS = [2048, 3072]
    BBS = [4096, 5120]
    SZS = 6144
    SPS = 7168
    for th in range(2):
        prev_ac = None
        for q in range(16):
            for hh in range(2):
                lo = th * 1024 + hh * 512
                TE(lambda t, q=q, lo=lo: t.matmul(
                    ps[6], sb["Ssel3"][:, q * 128:(q + 1) * 128],
                    fbb[0:64, lo:lo + 512], start=True, stop=True))
                TE(lambda t, q=q, lo=lo: t.matmul(
                    ps[7], sb["Ssel3"][:, q * 128:(q + 1) * 128],
                    fbb[0:64, 2048 + lo:2048 + lo + 512], start=True, stop=True))
                t_te = p.mark("tensor")
                p.wait("scalar", t_te)
                AC(lambda s, q=q, hh=hh: s.copy(
                    fmv(sb["wa"], 16)[:, q, hh * 512:(hh + 1) * 512], ps[6]))
                AC(lambda s, q=q, hh=hh: s.copy(
                    fmv(sb["wb"], 16)[:, q, hh * 512:(hh + 1) * 512], ps[7]))
                prev_ac = p.mark("scalar")
                p.wait("tensor", prev_ac)
        p.wait("vector", prev_ac)
        p.wait("sync", prev_ac)
        prev_ep = None
        prev_sp_tok = None
        for g in range(32):
            dh, dl0 = g // 4, (g % 4) * 32
            gs = g % 2
            tsl = slice(th * 1024, (th + 1) * 1024)
            if prev_ep is not None:
                p.wait("sync", prev_ep)
            for pg in range(4):
                p.dma("sync", wcf16[pg * 32:(pg + 1) * 32, DTS[gs]:DTS[gs] + 1024],
                      dtimg[dl0:dl0 + 32, dh, tsl])
                p.dma("sync", sb["wc"][pg * 32:(pg + 1) * 32, UDS[gs]:UDS[gs] + 1024],
                      xcimg[dl0:dl0 + 32, dh, tsl])
            tok_dup = p.dma("sync", sb["wc"][0:32, SZS:SZS + 1024],
                            szimg[dl0:dl0 + 32, dh, tsl])
            p.wait("scalar", tok_dup)
            p.wait("vector", tok_dup)
            for q in range(16):
                asl = (q % 2) * 1024
                hbuf = sb["ga"] if q < 8 else sb["gb"]
                hsl = (q % 8) * 1024
                AC(lambda s, q=q, asl=asl, gs=gs: s.activation(
                    sb["fa"][:, asl:asl + 1024], wcf16[:, DTS[gs]:DTS[gs] + 1024],
                    AT.Exp, scale=sb["A_cm"][:, q:q + 1]))
                t_ac = p.mark("scalar")
                VE(lambda v, q=q, gs=gs: v.tensor_tensor(
                    sb["wc"][:, BBS[q % 2]:BBS[q % 2] + 1024],
                    sb["wc"][:, UDS[gs]:UDS[gs] + 1024],
                    fmv(sb["wa"], 16)[:, q, :], OP.mult))
                p.wait("vector", t_ac)
                VE(lambda v, q=q, g=g, asl=asl, hbuf=hbuf, hsl=hsl: v.tensor_tensor_scan(
                    hbuf[:, hsl:hsl + 1024], sb["fa"][:, asl:asl + 1024],
                    sb["wc"][:, BBS[q % 2]:BBS[q % 2] + 1024],
                    sb["carry"][:, g * 16 + q:g * 16 + q + 1],
                    OP.mult, OP.add), nd=True)
                t_ve = p.mark("vector")
                p.wait("scalar", t_ve)
            VE(lambda v, g=g: v.tensor_copy(
                sb["carry"][:, g * 16:g * 16 + 8],
                sb["ga"].rearrange("q (a w) -> q a w", a=8)[:, :, 1023]))
            VE(lambda v, g=g: v.tensor_copy(
                sb["carry"][:, g * 16 + 8:g * 16 + 16],
                sb["gb"].rearrange("q (a w) -> q a w", a=8)[:, :, 1023]))
            for q in range(16):
                hbuf = sb["ga"] if q < 8 else sb["gb"]
                hsl = (q % 8) * 1024
                VE(lambda v, q=q, hbuf=hbuf, hsl=hsl: v.tensor_tensor(
                    hbuf[:, hsl:hsl + 1024], hbuf[:, hsl:hsl + 1024],
                    fmv(sb["wb"], 16)[:, q, :], OP.mult))
            t_ve = p.mark("vector")
            p.wait("tensor", t_ve)
            for ti in range(2):
                for q in range(16):
                    hbuf = sb["ga"] if q < 8 else sb["gb"]
                    hsl = (q % 8) * 1024
                    TE(lambda t, q=q, ti=ti, hbuf=hbuf, hsl=hsl: t.matmul(
                        ps[ti][0:32, :], sb["Ssum"],
                        hbuf[:, hsl + ti * 512: hsl + (ti + 1) * 512],
                        start=(q == 0), stop=(q == 15)))
            t_te = p.mark("tensor")
            p.wait("vector", t_te)
            if prev_sp_tok is not None:
                p.wait("vector", prev_sp_tok)
            for ti in range(2):
                VE(lambda v, ti=ti, gs=gs, dl0=dl0, dh=dh: v.tensor_scalar(
                    sb["fa"][0:32, 2048 + ti * 512: 2048 + (ti + 1) * 512],
                    sb["wc"][0:32, UDS[gs] + ti * 512: UDS[gs] + (ti + 1) * 512],
                    sb["Dp"][dl0:dl0 + 32, dh:dh + 1], None, OP.mult))
                VE(lambda v, ti=ti: v.tensor_tensor(
                    sb["fa"][0:32, 2048 + ti * 512: 2048 + (ti + 1) * 512],
                    sb["fa"][0:32, 2048 + ti * 512: 2048 + (ti + 1) * 512],
                    ps[ti][0:32, :], OP.add))
                VE(lambda v, ti=ti: v.tensor_tensor(
                    sb["wc"][0:32, SPS + ti * 512: SPS + (ti + 1) * 512],
                    sb["fa"][0:32, 2048 + ti * 512: 2048 + (ti + 1) * 512],
                    sb["wc"][0:32, SZS + ti * 512: SZS + (ti + 1) * 512], OP.mult))
            prev_ep = p.mark("vector")
            p.wait("tensor", prev_ep)
            p.wait("sync", prev_ep)
            prev_sp_tok = p.dma("sync", spimg[dl0:dl0 + 32, dh, tsl],
                                sb["wc"][0:32, SPS:SPS + 1024])
    p.barrier()
    tapout_d("spre", A["spre_fm"])

    tok_wo = p.dma("gpsimd", sb["wb"][:, 0:8192],
                   A["WoutT"].rearrange("(kh q) o -> q kh o", q=128))
    wov = sb["wb"][:, 0:8192].rearrange("q (kh o) -> q kh o", kh=8)
    xeimg2 = A["xe_tm"].rearrange("(e q) (sh w) -> e q sh w", q=128, sh=4)
    x1img = A["x1_fm"].rearrange("q (dh t) -> q dh t", dh=8)
    xnnimg = A["xnn1_fm"].rearrange("q (dh t) -> q dh t", dh=8)
    p.wait("tensor", tok_wo)
    for tt in range(4):
        ts512 = slice(tt * 512, (tt + 1) * 512)
        for e in range(8):
            for tc in range(4):
                ci = tt * 4 + tc
                VE(lambda v, ci=ci, e=e: v.tensor_scalar(
                    sb["fa"].bitcast(BF16)[:, 0:512], sb["iota_s"],
                    sb["slotcol"][:, ci * 8 + e:ci * 8 + e + 1],
                    sb["affcol"][:, ci * 8 + e:ci * 8 + e + 1],
                    OP.is_equal, OP.mult))
                t_ve = p.mark("vector")
                p.wait("tensor", t_ve)
                for sh in range(4):
                    TE(lambda t, sh=sh: t.transpose(
                        ps[6].bitcast(BF16)[:, 0:128],
                        sb["fa"].bitcast(BF16)[:, sh * 128:(sh + 1) * 128],
                        sb["eye"]))
                    t_te = p.mark("tensor")
                    p.wait("scalar", t_te)
                    dstb = sb["ga"] if e < 4 else sb["wb"][:, 8192:16384]
                    zi = (e * 4 + sh) % 16
                    AC(lambda s, dstb=dstb, zi=zi, tc=tc: s.copy(
                        dstb.rearrange("q (z w) -> q z w", z=16)
                        [:, zi, tc * 128:(tc + 1) * 128],
                        ps[6].bitcast(BF16)[:, 0:128]))
                    t_ac = p.mark("scalar")
                    p.wait("tensor", t_ac)
                    p.wait("vector", t_ac)
        if tt == 0 and "p2g" in taps:
            tapout("p2g", sb["ga"])
            tapout("affcol", sb["affcol"])
        tok_sp = p.dma("sync", sb["wc"][:, 0:4096],
                       A["spre_fm"].rearrange("q (dh t) -> q dh t", dh=8)[:, :, ts512])
        tok_xn2 = p.dma("sync", sb["wc"][:, 4096:8192],
                        A["xn_fm"].rearrange("q (dh t) -> q dh t", dh=8)[:, :, ts512])
        tok_x = p.dma("sync", fmv(sb["fb"]), xTv[:, :, ts512])
        p.wait("tensor", tok_sp)
        p.wait("vector", tok_x)
        dbuf = "gmoe" not in taps

        def st_load(dc2):
            base = 4096 if (dbuf and dc2 % 2) else 0
            tk = None
            for e2 in range(8):
                tk = p.dma("sync", sb["gb"][:, base + e2 * 512:base + (e2 + 1) * 512],
                           xeimg2[e2, :, :, dc2 * 128:(dc2 + 1) * 128])
            return tk
        if tt > 0:
            p.wait("sync", smarks[7])
        stoks = [None] * 8
        smarks = [None] * 8
        if dbuf:
            stoks[0] = st_load(0)
        for dc in range(8):
            bank_g = dc % 2
            bank_s = 2 + dc % 2
            if dbuf:
                if dc + 1 < 8:
                    if dc >= 1:
                        p.wait("sync", smarks[dc - 1])
                    stoks[dc + 1] = st_load(dc + 1)
            else:
                if dc >= 1:
                    p.wait("sync", smarks[dc - 1])
                stoks[dc] = st_load(dc)
            base = 4096 if (dbuf and dc % 2) else 0
            p.wait("tensor", stoks[dc])
            for e in range(8):
                for sh in range(4):
                    srcb = sb["ga"] if e < 4 else sb["wb"][:, 8192:16384]
                    zi = (e * 4 + sh) % 16
                    TE(lambda t, srcb=srcb, zi=zi, bank_g=bank_g, e=e, sh=sh, base=base: t.matmul(
                        ps[bank_g],
                        sb["gb"][:, base + e * 512 + sh * 128:base + e * 512 + (sh + 1) * 128],
                        srcb.rearrange("q (z w) -> q z w", z=16)[:, zi, :],
                        start=(e == 0 and sh == 0), stop=(e == 7 and sh == 3)))
            smarks[dc] = p.mark("tensor")
            for kh in range(8):
                TE(lambda t, kh=kh, dc=dc, bank_s=bank_s: t.matmul(
                    ps[bank_s], wov[:, kh, dc * 128:(dc + 1) * 128],
                    sb["wc"][:, 0:4096].rearrange("q (kh w) -> q kh w", kh=8)[:, kh, :],
                    start=(kh == 0), stop=(kh == 7)))
            t_te = p.mark("tensor")
            p.wait("scalar", t_te)
            p.wait("vector", t_te)
            if "gmoe" in taps:
                AC(lambda s, bank_g=bank_g, dc=dc: s.copy(
                    sb["wc"].rearrange("q (z w) -> q z w", z=32)[:, 16 + dc, :],
                    ps[bank_g]))
            AC(lambda s, bank_g=bank_g: s.activation(
                sb["fa"][:, 0:512], ps[bank_g], AT.Silu))
            t_ac = p.mark("scalar")
            p.wait("vector", t_ac)
            VE(lambda v, dc=dc, bank_s=bank_s: v.tensor_tensor(
                sb["fa"][:, 512:1024], ps[bank_s],
                sb["wc"][:, 4096:8192].rearrange("q (kh w) -> q kh w", kh=8)[:, dc, :],
                OP.add))
            VE(lambda v: v.tensor_tensor(
                sb["fa"][:, 512:1024], sb["fa"][:, 512:1024], sb["fa"][:, 0:512],
                OP.mult))
            VE(lambda v, dc=dc: v.tensor_tensor(
                sb["fa"].rearrange("q (z w) -> q z w", z=16)[:, 8 + dc, :],
                fmv(sb["fb"])[:, dc, :], sb["fa"][:, 512:1024], OP.add))
            VE(lambda v, dc=dc: v.tensor_tensor(
                sb["fa"][:, 0:512],
                sb["fa"].rearrange("q (z w) -> q z w", z=16)[:, 8 + dc, :],
                sb["fa"].rearrange("q (z w) -> q z w", z=16)[:, 8 + dc, :], OP.mult))
            t_ve = p.mark("vector")
            p.wait("tensor", t_ve)
            TE(lambda t, dc=dc: t.matmul(
                ps[4][0:1, :], sb["ones_c"],
                sb["fa"].rearrange("q (z w) -> q z w", z=16)[:, 8 + dc, :],
                start=(dc == 0), stop=(dc == 7)))
            TE(lambda t, dc=dc: t.matmul(
                ps[5][0:1, :], sb["ones_c"], sb["fa"][:, 0:512],
                start=(dc == 0), stop=(dc == 7)))
            t_te = p.mark("tensor")
            p.wait("vector", t_te)
            p.wait("scalar", t_te)
        if "gmoe" in taps:
            t_g = p.mark("scalar")
            p.wait("sync", t_g)
            p.dma("sync",
                  A["tap_gmoe"].rearrange("q (dh t) -> q dh t", dh=8)[:, :, ts512],
                  sb["wc"].rearrange("q (z w) -> q z w", z=32)[:, 16:24, :])
        p.wait("sync", t_ve)
        tok_x1 = p.dma("sync", x1img[:, :, ts512], sb["fa"][:, 4096:8192])
        r1, r2, r3 = sb["row1"], sb["row2"], sb["row3"]
        VE(lambda v: v.tensor_scalar(r1, ps[4][0:1, :], 1.0 / D, None, OP.mult))
        VE(lambda v: v.tensor_scalar(r2, ps[5][0:1, :], 1.0 / D, None, OP.mult))
        VE(lambda v: v.tensor_tensor(r3, r1, r1, OP.mult))
        VE(lambda v: v.tensor_tensor(r2, r2, r3, OP.subtract))
        t_ve = p.mark("vector")
        p.wait("scalar", t_ve)
        AC(lambda s: s.activation(r3, r2, AT.Sqrt, bias=sb["eps_c"][0:1, :]))
        t_ac = p.mark("scalar")
        p.wait("vector", t_ac)
        VE(lambda v: v.reciprocal(r3, r3))
        VE(lambda v: v.tensor_tensor(r2, r2, r3, OP.mult))
        VE(lambda v: v.tensor_tensor(r2, r2, r3, OP.mult))
        t_ve = p.mark("vector")
        p.wait("scalar", t_ve)
        AC(lambda s: s.activation(r2, r2, AT.Sqrt, bias=sb["eps_c"][0:1, :]))
        t_ac = p.mark("scalar")
        p.wait("vector", t_ac)
        VE(lambda v: v.reciprocal(r2, r2))
        VE(lambda v: v.tensor_tensor(r2, r2, r3, OP.mult))
        t_ve = p.mark("vector")
        p.wait("tensor", t_ve)
        TE(lambda t: t.matmul(ps[6], sb["ones_r1f"][0:1, 0:128], r1, start=True, stop=True))
        TE(lambda t: t.matmul(ps[7], sb["ones_r1f"][0:1, 0:128], r3, start=True, stop=True))
        TE(lambda t: t.matmul(ps[4], sb["ones_r1f"][0:1, 0:128], r2, start=True, stop=True))
        t_te = p.mark("tensor")
        p.wait("vector", t_te)
        for dc in range(8):
            x1s = sb["fa"].rearrange("q (z w) -> q z w", z=16)[:, 8 + dc, :]
            VE(lambda v, x1s=x1s: v.tensor_tensor(sb["fa"][:, 0:512], x1s, ps[6], OP.subtract))
            VE(lambda v, dc=dc, ts512=ts512: v.tensor_tensor(
                fmv(sb["wa"])[:, dc, ts512], sb["fa"][:, 0:512], ps[7], OP.mult))
            VE(lambda v: v.tensor_tensor(sb["fa"][:, 512:1024], sb["fa"][:, 0:512], ps[4], OP.mult))
            VE(lambda v, dc=dc: v.tensor_scalar(
                sb["wc"].rearrange("q (z w) -> q z w", z=32)[:, dc, :],
                sb["fa"][:, 512:1024],
                sb["ln1_w"][:, dc:dc + 1], sb["ln1_b"][:, dc:dc + 1], OP.mult, OP.add))
        t_ve = p.mark("vector")
        p.wait("sync", t_ve)
        p.wait("tensor", t_ve)
        tok_xnn = p.dma("sync", xnnimg[:, :, ts512], sb["wc"][:, 0:4096])
        p.wait("vector", tok_xnn)
        p.wait("scalar", tok_xnn)
    p.barrier()
    tapout_d("x1", A["x1_fm"])
    tapout("xn1", sb["wa"])
    tapout_d("xnn1", A["xnn1_fm"])

    fab = sb["fa"].bitcast(BF16)
    fbb16 = sb["fb"].bitcast(BF16)
    for wi, wname in enumerate(["WrT", "WkT", "WvT"]):
        tok_w = p.dma("gpsimd", fab[:, 0:8192],
                      A[wname].rearrange("(kh q) o -> q kh o", q=128))
        p.wait("tensor", tok_w)
        wv = fab[:, 0:8192].rearrange("q (kh o) -> q kh o", kh=8)
        for tt in range(4):
            ts512 = slice(tt * 512, (tt + 1) * 512)
            tok_xnn = p.dma("sync", fbb16[:, 0:4096], xnnimg[:, :, ts512])
            p.wait("tensor", tok_xnn)
            for oc in range(8):
                bank = oc % 2
                for kh in range(8):
                    TE(lambda t, oc=oc, kh=kh, bank=bank: t.matmul(
                        ps[bank], wv[:, kh, oc * 128:(oc + 1) * 128],
                        fbb16[:, 0:4096].rearrange("q (kh w) -> q kh w", kh=8)[:, kh, :],
                        start=(kh == 0), stop=(kh == 7)))
                t_te = p.mark("tensor")
                p.wait("scalar", t_te)
                if wi == 0:
                    AC(lambda s, oc=oc, ts512=ts512, bank=bank: s.activation(
                        fmv(sb["wb"])[:, oc, ts512], ps[bank], AT.Sigmoid))
                elif wi == 1:
                    AC(lambda s, oc=oc, ts512=ts512, bank=bank: s.copy(
                        fmv(sb["wc"].bitcast(F16))[:, oc, ts512], ps[bank]))
                else:
                    AC(lambda s, oc=oc, ts512=ts512, bank=bank: s.copy(
                        (fmv(sb["ga"], 4)[:, oc, ts512] if oc < 4 else
                         fmv(sb["gb"], 4)[:, oc - 4, ts512]), ps[bank]))
                t_ac = p.mark("scalar")
                p.wait("tensor", t_ac)
            t_ac = p.mark("scalar")
            p.wait("sync", t_ac)
        t_mm = p.mark("tensor")
        p.wait("gpsimd", t_mm)
    p.barrier()
    tapout("rr", sb["wb"])
    tapout("kk_", sb["wc"].bitcast(F16))
    tapout("vva", sb["ga"])
    tapout("vvb", sb["gb"])

    VE(lambda v: v.memset(sb["fb"][:, 2048:2049], 0.0))
    VE(lambda v: v.memset(sb["fa"][:, 6144:6145], 0.0))
    for ch in range(8):
        vsl = (fmv(sb["ga"], 4)[:, ch, :] if ch < 4 else
               fmv(sb["gb"], 4)[:, ch - 4, :])
        ksl = fmv(sb["wc"].bitcast(F16))[:, ch, :]
        VE(lambda v: v.memset(sb["fa"][:, 6144:8192], 1.0))
        VE(lambda v, ch=ch: v.tensor_scalar(
            sb["fa"][:, 6144:8192], sb["fa"][:, 6144:8192],
            sb["ew"][:, ch:ch + 1], None, OP.mult))
        for dr in range(2):
            t_vp = p.mark("vector")
            p.wait("scalar", t_vp)
            AC(lambda s, ksl=ksl: s.activation(sb["fa"][:, 0:2048], ksl, AT.Exp))
            t_ac = p.mark("scalar")
            p.wait("vector", t_ac)
            VE(lambda v, vsl=vsl: v.tensor_tensor(
                sb["fa"][:, 2048:4096], sb["fa"][:, 0:2048], vsl, OP.mult))
            ewb = sb["fa"][:, 6144:8192]
            if dr == 0:
                nout = sb["fb"][:, 0:2048]
                dout = sb["fa"][:, 4096:6144]
                bnum = sb["fa"][:, 2048:4096]
                bden = sb["fa"][:, 0:2048]
            else:
                nout = bass.AP(sb["fb"].tensor, sb["fb"].offset + 2047,
                               [list(sb["fb"].ap[0]), [-1, 2048]])
                dout = bass.AP(sb["fa"].tensor, sb["fa"].offset + 4096 + 2047,
                               [list(sb["fa"].ap[0]), [-1, 2048]])
                bnum = bass.AP(sb["fa"].tensor, sb["fa"].offset + 2048 + 2047,
                               [list(sb["fa"].ap[0]), [-1, 2048]])
                bden = bass.AP(sb["fa"].tensor, sb["fa"].offset + 2047,
                               [list(sb["fa"].ap[0]), [-1, 2048]])
            if ch == 0 and dr == 0:
                tapout("p6a", sb["fa"])
            VE(lambda v, nout=nout, bnum=bnum, ewb=ewb: v.tensor_tensor_scan(
                nout, ewb, bnum, 0.0, OP.mult, OP.add))
            VE(lambda v, dout=dout, bden=bden, ewb=ewb: v.tensor_tensor_scan(
                dout, ewb, bden, 0.0, OP.mult, OP.add))
            if ch == 0 and dr == 0:
                tapout("p6b", sb["fb"])
                tapout("p6c", sb["fa"])
            eucol = sb["eu"][:, ch:ch + 1]
            VE(lambda v, eucol=eucol: v.tensor_scalar(
                sb["fa"][:, 2048:4096], sb["fa"][:, 2048:4096], eucol, None,
                OP.mult))
            VE(lambda v, eucol=eucol: v.tensor_scalar(
                sb["fa"][:, 0:2048], sb["fa"][:, 0:2048], eucol, None,
                OP.mult))
            if dr == 0:
                VE(lambda v: v.tensor_tensor(
                    sb["fa"][:, 2049:4096], sb["fa"][:, 2049:4096],
                    sb["fb"][:, 0:2047], OP.add))
                VE(lambda v: v.tensor_tensor(
                    sb["fa"][:, 1:2048], sb["fa"][:, 1:2048],
                    sb["fa"][:, 4096:6143], OP.add))
            else:
                VE(lambda v: v.tensor_tensor(
                    sb["fa"][:, 2048:4095], sb["fa"][:, 2048:4095],
                    sb["fb"][:, 1:2048], OP.add))
                VE(lambda v: v.tensor_tensor(
                    sb["fa"][:, 0:2047], sb["fa"][:, 0:2047],
                    sb["fa"][:, 4097:6144], OP.add))
            VE(lambda v: v.reciprocal(sb["fa"][:, 0:2048], sb["fa"][:, 0:2048]))
            if dr == 0:
                VE(lambda v: v.tensor_tensor(
                    sb["fb"][:, 2048:4096], sb["fa"][:, 2048:4096],
                    sb["fa"][:, 0:2048], OP.mult))
            else:
                VE(lambda v: v.tensor_tensor(
                    sb["fa"][:, 4096:6144], sb["fa"][:, 2048:4096],
                    sb["fa"][:, 0:2048], OP.mult))
        if ch == 0:
            tapout("p6d", sb["fb"])
            tapout("p6e", sb["fa"])
        VE(lambda v: v.tensor_tensor(
            sb["fa"][:, 4096:6144], sb["fa"][:, 4096:6144],
            sb["fb"][:, 2048:4096], OP.add))
        VE(lambda v: v.tensor_scalar(
            sb["fa"][:, 4096:6144], sb["fa"][:, 4096:6144], 0.5, None, OP.mult))
        VE(lambda v, ch=ch: v.tensor_tensor(
            fmv(sb["wb"])[:, ch, :], fmv(sb["wb"])[:, ch, :],
            sb["fa"][:, 4096:6144], OP.mult))
    t_ve = p.mark("vector")
    p.barrier()
    tapout("rw", sb["wb"])

    tok_wo2 = p.dma("gpsimd", fab[:, 0:8192],
                    A["WoT"].rearrange("(kh q) o -> q kh o", q=128))
    p.wait("tensor", tok_wo2)
    wov2 = fab[:, 0:8192].rearrange("q (kh o) -> q kh o", kh=8)
    xmimg = A["xmid_fm"].rearrange("q (dh t) -> q dh t", dh=8)
    for tt in range(4):
        ts512 = slice(tt * 512, (tt + 1) * 512)
        for dc in range(8):
            bank = dc % 2
            for kh in range(8):
                TE(lambda t, dc=dc, kh=kh, bank=bank, ts512=ts512: t.matmul(
                    ps[bank], wov2[:, kh, dc * 128:(dc + 1) * 128],
                    fmv(sb["wb"])[:, kh, ts512], start=(kh == 0), stop=(kh == 7)))
            t_te = p.mark("tensor")
            p.wait("vector", t_te)
            VE(lambda v, dc=dc, ts512=ts512, bank=bank: v.tensor_tensor(
                fmv(sb["fb"])[:, dc, :], ps[bank], fmv(sb["wa"])[:, dc, ts512],
                OP.add))
            VE(lambda v, dc=dc: v.tensor_tensor(
                sb["fa"][:, 4096:4608], fmv(sb["fb"])[:, dc, :],
                fmv(sb["fb"])[:, dc, :], OP.mult))
            t_ve = p.mark("vector")
            p.wait("tensor", t_ve)
            TE(lambda t, dc=dc: t.matmul(
                ps[4][0:1, :], sb["ones_c"], fmv(sb["fb"])[:, dc, :],
                start=(dc == 0), stop=(dc == 7)))
            TE(lambda t, dc=dc: t.matmul(
                ps[5][0:1, :], sb["ones_c"], sb["fa"][:, 4096:4608],
                start=(dc == 0), stop=(dc == 7)))
            t_te = p.mark("tensor")
            p.wait("vector", t_te)
            p.wait("scalar", t_te)
        p.wait("sync", t_ve)
        tok_xm = p.dma("sync", xmimg[:, :, ts512], sb["fb"])
        p.wait("vector", tok_xm)
        r1, r2, r3 = sb["row1"], sb["row2"], sb["row3"]
        VE(lambda v: v.tensor_scalar(r1, ps[4][0:1, :], 1.0 / D, None, OP.mult))
        VE(lambda v: v.tensor_scalar(r2, ps[5][0:1, :], 1.0 / D, None, OP.mult))
        VE(lambda v: v.tensor_tensor(r3, r1, r1, OP.mult))
        VE(lambda v: v.tensor_tensor(r2, r2, r3, OP.subtract))
        t_ve = p.mark("vector")
        p.wait("scalar", t_ve)
        AC(lambda s: s.activation(r3, r2, AT.Sqrt, bias=sb["eps_c"][0:1, :]))
        t_ac = p.mark("scalar")
        p.wait("vector", t_ac)
        VE(lambda v: v.reciprocal(r3, r3))
        t_ve = p.mark("vector")
        p.wait("tensor", t_ve)
        TE(lambda t: t.matmul(ps[6], sb["ones_r1f"][0:1, 0:128], r1, start=True, stop=True))
        TE(lambda t: t.matmul(ps[7], sb["ones_r1f"][0:1, 0:128], r3, start=True, stop=True))
        t_te = p.mark("tensor")
        p.wait("vector", t_te)
        for dc in range(8):
            VE(lambda v, dc=dc: v.tensor_tensor(
                sb["fa"][:, 4096:4608], fmv(sb["fb"])[:, dc, :], ps[6], OP.subtract))
            VE(lambda v: v.tensor_tensor(
                sb["fa"][:, 4608:5120], sb["fa"][:, 4096:4608], ps[7], OP.mult))
            VE(lambda v, dc=dc, ts512=ts512: v.tensor_scalar(
                fmv(sb["wb"])[:, dc, ts512], sb["fa"][:, 4608:5120],
                sb["ln2_w"][:, dc:dc + 1], sb["ln2_b"][:, dc:dc + 1],
                OP.mult, OP.add))
        t_ve = p.mark("vector")
        p.wait("tensor", t_ve)
        p.wait("sync", t_ve)
    p.barrier()
    tapout_d("xmid", A["xmid_fm"])
    tapout("xn2", sb["wb"])

    for wname, dstname, nch, cw in (("WfcT", "WqfcT_d", 8, 4096),
                                    ("WprojT", "WqprojT_d", 32, 1024)):
        VE(lambda v: v.memset(sb["fa"][:, 7168:7169], 0.0))
        t_ve = p.mark("vector")
        p.wait("sync", t_ve)
        wimg = A[wname].rearrange("(n q) o -> n q o", q=128)
        prev = None
        for n in range(nch):
            tok = p.dma("sync", sb["fa"][:, 0:cw], wimg[n])
            p.wait("vector", tok)
            VE(lambda v, cw=cw: v.tensor_reduce(
                sb["fa"][:, 7169:7170], sb["fa"][:, 0:cw], mybir.AxisListType.X,
                OP.max, apply_absolute_value=True))
            VE(lambda v: v.tensor_tensor(
                sb["fa"][:, 7168:7169], sb["fa"][:, 7168:7169],
                sb["fa"][:, 7169:7170], OP.max))
            prev = p.mark("vector")
            p.wait("sync", prev)
        p.wait("tensor", prev)
        TE(lambda t: t.transpose(ps[6][0:1, 0:128], sb["fa"][:, 7168:7169],
                                 sb["eyef"]))
        t_te = p.mark("tensor")
        p.wait("vector", t_te)
        VE(lambda v: v.tensor_reduce(
            sb["fa"][0:1, 7170:7171], ps[6][0:1, 0:128], mybir.AxisListType.X,
            OP.max))
        VE(lambda v: v.tensor_scalar(
            sb["fa"][0:1, 7171:7172], sb["fa"][0:1, 7170:7171], 1.0 / 127.0,
            None, OP.mult))
        VE(lambda v: v.reciprocal(sb["fa"][0:1, 7172:7173], sb["fa"][0:1, 7171:7172]))
        t_ve = p.mark("vector")
        p.wait("tensor", t_ve)
        TE(lambda t: t.matmul(ps[6][:, 0:2], sb["ones_r1f"][0:1, 0:128],
                              sb["fa"][0:1, 7171:7173], start=True, stop=True))
        t_te = p.mark("tensor")
        p.wait("vector", t_te)
        VE(lambda v: v.tensor_copy(sb["fa"][:, 7173:7175], ps[6][:, 0:2]))
        p.selfsync("vector")
        for n in range(nch):
            tok = p.dma("sync", sb["fa"][:, 0:cw], wimg[n])
            p.wait("vector", tok)
            VE(lambda v, cw=cw: v.tensor_scalar(
                sb["fa"][:, 0:cw], sb["fa"][:, 0:cw],
                sb["fa"][:, 7174:7175], RNE_C, OP.mult, OP.add))
            VE(lambda v, cw=cw: v.tensor_scalar(
                sb["fa"][:, 0:cw], sb["fa"][:, 0:cw], -RNE_C, None, OP.add))
            VE(lambda v, cw=cw: v.tensor_scalar(
                sb["fa"][:, 0:cw], sb["fa"][:, 0:cw], 127.0, -128.0,
                OP.min, OP.max))
            VE(lambda v, cw=cw: v.tensor_scalar(
                sb["fb"].bitcast(BF16)[:, 0:cw], sb["fa"][:, 0:cw],
                sb["fa"][:, 7173:7174], None, OP.mult))
            t_ve = p.mark("vector")
            p.wait("sync", t_ve)
            tok2 = p.dma("sync",
                         A[dstname].rearrange("(n q) o -> n q o", q=128)[n],
                         sb["fb"].bitcast(BF16)[:, 0:cw])
            p.wait("vector", tok2)
    p.barrier()

    kkimg = A["kk_fm"].rearrange("q (ic t) -> q ic t", ic=32)
    hfimg = A["hf_fm"].rearrange("q (ic t) -> q ic t", ic=32)
    fktoks = [None] * 32
    fkmarks = [None] * 32

    def fk_load(ic2):
        return p.dma("gpsimd", fab[:, (ic2 % 2) * 1024:(ic2 % 2) * 1024 + 1024],
                     A["WfkT"].rearrange("(kh q) o -> q kh o", q=128)[:, :, ic2 * 128:(ic2 + 1) * 128])
    fktoks[0] = fk_load(0)
    for ic in range(32):
        slot = (ic % 2) * 1024
        if ic + 1 < 32:
            if ic >= 1:
                p.wait("gpsimd", fkmarks[ic - 1])
            fktoks[ic + 1] = fk_load(ic + 1)
        p.wait("tensor", fktoks[ic])
        for tt in range(4):
            ts512 = slice(tt * 512, (tt + 1) * 512)
            bank = tt % 2
            for kh in range(8):
                TE(lambda t, kh=kh, bank=bank, slot=slot, ts512=ts512: t.matmul(
                    ps[bank],
                    fab[:, slot:slot + 1024].rearrange("q (kh w) -> q kh w", kh=8)[:, kh, :],
                    fmv(sb["wb"])[:, kh, ts512], start=(kh == 0), stop=(kh == 7)))
            t_te = p.mark("tensor")
            p.wait("vector", t_te)
            VE(lambda v, bank=bank: v.tensor_scalar(
                sb["fa"][:, 2048:2560], ps[bank], 0.0, None, OP.max))
            VE(lambda v, bank=bank: v.tensor_tensor(
                fbb16[:, 0:512], sb["fa"][:, 2048:2560], ps[bank], OP.mult))
            t_ve = p.mark("vector")
            p.wait("tensor", t_ve)
            p.wait("sync", t_ve)
            tok_kk = p.dma("sync", kkimg[:, ic, ts512], fbb16[:, 0:512])
            p.wait("vector", tok_kk)
        fkmarks[ic] = p.mark("tensor")
    p.barrier()

    tok_afc = p.dma("gpsimd", fab[:, 0:128],
                    A["A_fcT"].rearrange("(kh q) o -> q kh o", q=128))
    tok_bfc = p.dma("gpsimd", sb["gb"][32:48, 0:4096], A["B_fcT"])
    p.wait("tensor", tok_bfc)
    for tt in range(4):
        ts512 = slice(tt * 512, (tt + 1) * 512)
        for kh in range(8):
            TE(lambda t, kh=kh, ts512=ts512: t.matmul(
                ps[2][32:48, :],
                fab[:, 0:128].rearrange("q (kh w) -> q kh w", kh=8)[:, kh, :],
                fmv(sb["wa"])[:, kh, ts512], start=(kh == 0), stop=(kh == 7)))
        t_te = p.mark("tensor")
        p.wait("vector", t_te)
        VE(lambda v, ts512=ts512: v.tensor_scalar(
            sb["gb"][32:48, 4096:6144][:, ts512], ps[2][32:48, :], 2.0, None, OP.mult))
        t_ve = p.mark("vector")
        p.wait("tensor", t_ve)
    fctoks = [None] * 32
    fcmarks = [None] * 32

    def fc_load(ic2):
        return p.dma("gpsimd", fab[:, (ic2 % 2) * 1024:(ic2 % 2) * 1024 + 1024],
                     A["WqfcT_d"].rearrange("(kh q) o -> q kh o", q=128)[:, :, ic2 * 128:(ic2 + 1) * 128])
    fctoks[0] = fc_load(0)
    for ic in range(32):
        slot = (ic % 2) * 1024
        if ic + 1 < 32:
            if ic >= 1:
                p.wait("gpsimd", fcmarks[ic - 1])
            fctoks[ic + 1] = fc_load(ic + 1)
        p.wait("tensor", fctoks[ic])
        for tt in range(4):
            ts512 = slice(tt * 512, (tt + 1) * 512)
            bank = tt % 2
            for kh in range(8):
                TE(lambda t, kh=kh, bank=bank, slot=slot, ts512=ts512: t.matmul(
                    ps[bank],
                    fab[:, slot:slot + 1024].rearrange("q (kh w) -> q kh w", kh=8)[:, kh, :],
                    fmv(sb["wa"])[:, kh, ts512], start=(kh == 0), stop=False))
            TE(lambda t, ic=ic, ts512=ts512, bank=bank: t.matmul(
                ps[bank], sb["gb"][32:48, ic * 128:(ic + 1) * 128],
                sb["gb"][32:48, 4096:6144][:, ts512], start=False, stop=True))
            t_te = p.mark("tensor")
            p.wait("scalar", t_te)
            AC(lambda s, ic=ic, bank=bank: s.activation(
                fbb16[:, 0:512], ps[bank], AT.Gelu,
                bias=sb["b_fc_fm"][:, ic:ic + 1]))
            t_ac = p.mark("scalar")
            p.wait("tensor", t_ac)
            p.wait("sync", t_ac)
            tok_h = p.dma("sync", hfimg[:, ic, ts512], fbb16[:, 0:512])
            p.wait("scalar", tok_h)
        fcmarks[ic] = p.mark("tensor")
    p.barrier()

    tok_fr = p.dma("gpsimd", sb["ga"][:, 0:8192],
                   A["WfrT"].rearrange("(kh q) o -> q kh o", q=128))
    tok_bpj = p.dma("gpsimd", sb["gb"][32:48, 4096:5120], A["B_projT"])
    wcf32 = sb["wc"].bitcast(F32)
    outimg = A["out_fm"].rearrange("q (dh t) -> q dh t", dh=8)
    frv = sb["ga"][:, 0:8192].rearrange("q (kh o) -> q kh o", kh=8)
    p.wait("tensor", tok_bpj)
    for tt in range(4):
        ts512 = slice(tt * 512, (tt + 1) * 512)
        if tt > 0:
            p.wait("gpsimd", pjm[31])
            p.wait("sync", pjm[31])
        kvt = [None] * 32
        kvm = [None] * 32

        def kv_load(ih2):
            tg = p.dma("gpsimd", fbb16[:, (ih2 % 2) * 1024:(ih2 % 2) * 1024 + 1024],
                  A["WfvT"].rearrange("(ih q) o -> q ih o", q=128, ih=32)[:, ih2, :])
            return tg, p.dma("sync", fbb16[:, 2048 + (ih2 % 2) * 512:2048 + (ih2 % 2) * 512 + 512],
                         kkimg[:, ih2, ts512])
        kvt[0] = kv_load(0)
        for ih in range(32):
            wslot = (ih % 2) * 1024
            kslot = 2048 + (ih % 2) * 512
            if ih + 1 < 32:
                if ih >= 1:
                    p.wait("gpsimd", kvm[ih - 1])
                    p.wait("sync", kvm[ih - 1])
                kvt[ih + 1] = kv_load(ih + 1)
            p.wait("tensor", kvt[ih][0])
            p.wait("tensor", kvt[ih][1])
            for dc in range(8):
                TE(lambda t, ih=ih, dc=dc, wslot=wslot, kslot=kslot: t.matmul(
                    ps[dc],
                    fbb16[:, wslot:wslot + 1024].rearrange("q (dc w) -> q dc w", dc=8)[:, dc, :],
                    fbb16[:, kslot:kslot + 512],
                    start=(ih == 0), stop=(ih == 31)))
            kvm[ih] = p.mark("tensor")
        p.wait("gpsimd", kvm[31])
        p.wait("sync", kvm[31])
        t_te = p.mark("tensor")
        p.wait("vector", t_te)
        for dc in range(8):
            VE(lambda v, dc=dc: v.tensor_copy(
                sb["fa"].rearrange("q (z w) -> q z w", z=16)[:, dc, :], ps[dc]))
        t_ve = p.mark("vector")
        p.wait("tensor", t_ve)
        for dc in range(8):
            bank = dc % 2
            for kh in range(8):
                TE(lambda t, dc=dc, kh=kh, bank=bank, ts512=ts512: t.matmul(
                    ps[bank], frv[:, kh, dc * 128:(dc + 1) * 128],
                    fmv(sb["wb"])[:, kh, ts512], start=(kh == 0), stop=(kh == 7)))
            t_te = p.mark("tensor")
            p.wait("scalar", t_te)
            AC(lambda s, dc=dc, bank=bank: s.activation(
                sb["wc"].rearrange("q (z w) -> q z w", z=32)[:, 16 + dc, :],
                ps[bank], AT.Sigmoid))
            t_ac = p.mark("scalar")
            p.wait("tensor", t_ac)
        l2t = [None] * 32
        l2m = [None] * 32

        def l2_load(ih2):
            tg = p.dma("gpsimd", fbb16[:, (ih2 % 2) * 1024:(ih2 % 2) * 1024 + 16],
                  A["A_projT"].rearrange("(ih q) o -> q ih o", q=128, ih=32)[:, ih2, :])
            return tg, p.dma("sync", fbb16[:, 2048 + (ih2 % 2) * 512:2048 + (ih2 % 2) * 512 + 512],
                         hfimg[:, ih2, ts512])
        l2t[0] = l2_load(0)
        for ih in range(32):
            wslot = (ih % 2) * 1024
            kslot = 2048 + (ih % 2) * 512
            if ih + 1 < 32:
                if ih >= 1:
                    p.wait("gpsimd", l2m[ih - 1])
                    p.wait("sync", l2m[ih - 1])
                l2t[ih + 1] = l2_load(ih + 1)
            p.wait("tensor", l2t[ih][0])
            p.wait("tensor", l2t[ih][1])
            TE(lambda t, ih=ih, wslot=wslot, kslot=kslot: t.matmul(
                ps[2][32:48, :], fbb16[:, wslot:wslot + 16],
                fbb16[:, kslot:kslot + 512], start=(ih == 0), stop=(ih == 31)))
            l2m[ih] = p.mark("tensor")
        p.wait("gpsimd", l2m[31])
        p.wait("sync", l2m[31])
        t_te = p.mark("tensor")
        p.wait("vector", t_te)
        VE(lambda v: v.tensor_scalar(
            sb["gb"][32:48, 6144:6656], ps[2][32:48, :], 2.0, None, OP.mult))
        t_ve = p.mark("vector")
        p.wait("tensor", t_ve)
        p.wait("gpsimd", t_ve)
        pjt = [None] * 32
        pjm = [None] * 32

        def pj_load(ih2):
            tg = p.dma("gpsimd", fbb16[:, (ih2 % 2) * 1024:(ih2 % 2) * 1024 + 1024],
                  A["WqprojT_d"].rearrange("(ih q) o -> q ih o", q=128, ih=32)[:, ih2, :])
            return tg, p.dma("sync", fbb16[:, 2048 + (ih2 % 2) * 512:2048 + (ih2 % 2) * 512 + 512],
                         hfimg[:, ih2, ts512])
        pjt[0] = pj_load(0)
        for ih in range(32):
            wslot = (ih % 2) * 1024
            kslot = 2048 + (ih % 2) * 512
            if ih + 1 < 32:
                if ih >= 1:
                    p.wait("gpsimd", pjm[ih - 1])
                    p.wait("sync", pjm[ih - 1])
                pjt[ih + 1] = pj_load(ih + 1)
            p.wait("tensor", pjt[ih][0])
            p.wait("tensor", pjt[ih][1])
            for dc in range(8):
                TE(lambda t, ih=ih, dc=dc, wslot=wslot, kslot=kslot: t.matmul(
                    ps[dc],
                    fbb16[:, wslot:wslot + 1024].rearrange("q (dc w) -> q dc w", dc=8)[:, dc, :],
                    fbb16[:, kslot:kslot + 512],
                    start=(ih == 0), stop=False))
            pjm[ih] = p.mark("tensor")
        for dc in range(8):
            TE(lambda t, dc=dc: t.matmul(
                ps[dc], sb["gb"][32:48, 4096 + dc * 128:4096 + (dc + 1) * 128],
                sb["gb"][32:48, 6144:6656], start=False, stop=True))
        t_te = p.mark("tensor")
        p.wait("scalar", t_te)
        tok_xm2 = p.dma("sync", sb["fa"][:, 4096:8192], xmimg[:, :, ts512])
        tok_x12 = p.dma("sync", wcf32[:, 0:4096], x1img[:, :, ts512])
        p.wait("scalar", tok_xm2)
        p.wait("vector", tok_x12)
        for dc in range(8):
            AC(lambda s, dc=dc: s.activation(
                fbb16[:, 3072 + dc * 512:3072 + (dc + 1) * 512], ps[dc], AT.Silu,
                bias=sb["b_proj"][:, dc:dc + 1]))
        t_ac = p.mark("scalar")
        p.wait("vector", t_ac)
        for dc in range(8):
            VE(lambda v, dc=dc: v.tensor_tensor(
                sb["fa"][:, 0:4096].rearrange("q (z w) -> q z w", z=8)[:, dc, :],
                sb["fa"][:, 0:4096].rearrange("q (z w) -> q z w", z=8)[:, dc, :],
                sb["wc"].rearrange("q (z w) -> q z w", z=32)[:, 16 + dc, :],
                OP.mult))
            VE(lambda v, dc=dc: v.tensor_tensor(
                sb["fa"][:, 0:4096].rearrange("q (z w) -> q z w", z=8)[:, dc, :],
                sb["fa"][:, 0:4096].rearrange("q (z w) -> q z w", z=8)[:, dc, :],
                sb["fa"][:, 4096:8192].rearrange("q (z w) -> q z w", z=8)[:, dc, :],
                OP.add))
            VE(lambda v, dc=dc: v.tensor_tensor(
                sb["fa"][:, 0:4096].rearrange("q (z w) -> q z w", z=8)[:, dc, :],
                sb["fa"][:, 0:4096].rearrange("q (z w) -> q z w", z=8)[:, dc, :],
                fbb16[:, 3072 + dc * 512:3072 + (dc + 1) * 512], OP.mult))
            VE(lambda v, dc=dc: v.tensor_tensor(
                wcf32[:, 0:4096].rearrange("q (z w) -> q z w", z=8)[:, dc, :],
                wcf32[:, 0:4096].rearrange("q (z w) -> q z w", z=8)[:, dc, :],
                sb["fa"][:, 0:4096].rearrange("q (z w) -> q z w", z=8)[:, dc, :],
                OP.add))
        t_ve = p.mark("vector")
        p.wait("sync", t_ve)
        p.wait("tensor", t_ve)
        tok_out = p.dma("sync", outimg[:, :, ts512], wcf32[:, 0:4096])
        p.wait("vector", tok_out)
        p.wait("scalar", tok_out)
    p.barrier()
    return sb, ps


TAP_SHAPES = {
    "A_cm": ([128, 16], F32),
    "xn_bf": ([128, 8 * T], BF16),
    "xnn_bf": ([128, 8 * T], BF16),
    "x_main": ([128, 8 * T], BF16),
    "siluz": ([128, 8 * T], BF16),
    "x_conv": ([128, 8 * T], BF16),
    "dt": ([128, 8 * T], F16),
    "Bm": ([64, T], BF16),
    "Cm": ([64, T], BF16),
    "aff": ([8, T], F32),
    "spre": ([128, 8 * T], BF16),
    "x1": ([128, 8 * T], F32),
    "xn1": ([128, 8 * T], BF16),
    "xnn1": ([128, 8 * T], BF16),
    "rr": ([128, 8 * T], BF16),
    "kk_": ([128, 8 * T], F16),
    "rw": ([128, 8 * T], BF16),
    "xmid": ([128, 8 * T], F32),
    "xn2": ([128, 8 * T], BF16),
    "slotx": ([8, T], F32),
    "thr": ([8, 1], F32),
    "xin0": ([128, 8 * 512], BF16),
    "xe0": ([128, 4 * 1024], BF16),
    "gmoe": ([128, 8 * T], BF16),
    "slotcol": ([128, 128], F32),
    "xntm": ([T, D], BF16),
    "hi0": ([128, 16384], BF16),
    "p2g": ([128, 8192], BF16),
    "affcol": ([128, 128], F32),
    "vva": ([128, 8192], BF16),
    "vvb": ([128, 8192], BF16),
    "p6a": ([128, 8192], F32),
    "p6b": ([128, 4096], F32),
    "p6c": ([128, 8192], F32),
    "p6d": ([128, 4096], F32),
    "p6e": ([128, 8192], F32),
}


def build_nc(taps=()):
    nc = bass.Bass("TRN2", target_bir_lowering=False, debug=False)
    A = {}

    def inp(name, shape, dtype=F32):
        A[name] = nc.dram_tensor(name, shape, dtype, kind="ExternalInput").ap()

    def scratch(name, shape, dtype):
        A[name] = nc.dram_tensor(name, shape, dtype, kind="Internal").ap()

    def outp(name, shape, dtype=F32):
        A[name] = nc.dram_tensor(name, shape, dtype, kind="ExternalOutput").ap()

    inp("xT", [D, T])
    inp("WinT", [D, 2 * D])
    inp("conv_wT", [4, D])
    inp("WdtT", [D, D])
    inp("WBT", [D, NST])
    inp("WCT", [D, NST])
    inp("WoutT", [D, D])
    inp("gate_wT", [D, NE])
    inp("wgT", [NE * D, FI])
    inp("wuT", [NE * D, FI])
    inp("wdT", [NE * FI, D])
    inp("WrT", [D, D])
    inp("WkT", [D, D])
    inp("WvT", [D, D])
    inp("WoT", [D, D])
    inp("WfkT", [D, FI])
    inp("WfvT", [FI, D])
    inp("WfrT", [D, D])
    inp("WfcT", [D, FI])
    inp("WprojT", [FI, D])
    inp("A_fcT", [D, 16])
    inp("B_fcT", [16, FI])
    inp("A_projT", [FI, 16])
    inp("B_projT", [16, D])
    for n in ["ln_g", "ln_b", "conv_b", "Dp", "b_dt", "ln1_w", "ln1_b",
              "ln2_w", "ln2_b", "decay_v", "u_v", "b_proj"]:
        inp(n, [1, D])
    inp("b_fc", [1, FI])
    inp("A_log", [1, NST])
    inp("S1c", [NST, 128])
    inp("S2c", [NST, 16])
    inp("Ssum", [128, 32], BF16)
    inp("Ssel3", [NST, 2048], BF16)
    inp("eye", [128, 128], BF16)
    inp("eyef", [128, 128])

    scratch("xn_fm", [128, 8 * T], BF16)
    scratch("xn_tm", [T, D], BF16)
    scratch("siluz_fm", [128, 8 * T], BF16)
    scratch("xconv_fm", [128, 8 * T], BF16)
    scratch("dt_fm", [128, 8 * T], F16)
    scratch("spre_fm", [128, 8 * T], BF16)
    scratch("x1_fm", [128, 8 * T], F32)
    scratch("xmid_fm", [128, 8 * T], F32)
    scratch("xnn1_fm", [128, 8 * T], BF16)
    scratch("kk_fm", [128, 32 * 512 * 4], BF16)
    scratch("hf_fm", [128, 32 * 512 * 4], BF16)
    scratch("xe_tm", [NE * 128, 4 * D], BF16)
    scratch("WqfcT_d", [D, FI], BF16)
    scratch("WqprojT_d", [FI, D], BF16)
    scratch("echo_d", [128, 16], F32)

    outp("out_fm", [128, 8 * T])
    for name in taps:
        shape, dt_ = TAP_SHAPES[name]
        outp("tap_" + name, shape, dt_)

    p = P(nc)
    with contextlib.ExitStack() as stack:
        p.make_sems(stack)
        blk = stack.enter_context(nc.Block())
        emit_program(p, A, set(taps))
        for qeng, sname in (("sync", "d_sy"), ("gpsimd", "d_gp"), ("scalar", "d_ac")):
            h = p.semh[sname]
            tv = p.cnt[sname]
            if tv:
                p.em(qeng, lambda e, h=h, tv=tv: e.wait_ge(h, tv))
        p.finalize_waits()

        @blk.sync
        def _(sync):
            for fn in p.ops["sync"]:
                fn(sync)

        @blk.gpsimd
        def _(g):
            for fn in p.ops["gpsimd"]:
                fn(g)

        @blk.vector
        def _(v):
            for fn in p.ops["vector"]:
                fn(v)

        @blk.scalar
        def _(s):
            for fn in p.ops["scalar"]:
                fn(s)

        @blk.tensor
        def _(t):
            for fn in p.ops["tensor"]:
                fn(t)

    return nc


def prep_inputs(b, kw):
    d = {}
    ct = np.ascontiguousarray
    d["xT"] = ct(kw["L0_attn_x"][b].T)
    d["WinT"] = ct(kw["L0_attn_W_in"].T)
    d["conv_wT"] = ct(kw["L0_attn_conv_w"][:, 0, :].T)
    d["WdtT"] = ct(kw["L0_attn_W_dt"].T)
    d["WBT"] = ct(kw["L0_attn_W_B"].T)
    d["WCT"] = ct(kw["L0_attn_W_C"].T)
    d["WoutT"] = ct(kw["L0_attn_W_out"].T)
    d["gate_wT"] = ct(kw["L0_ffn_gate_w"].T)
    d["wgT"] = ct(kw["L0_ffn_w_gate"].transpose(0, 2, 1)).reshape(NE * D, FI)
    d["wuT"] = ct(kw["L0_ffn_w_up"].transpose(0, 2, 1)).reshape(NE * D, FI)
    d["wdT"] = ct(kw["L0_ffn_w_down"].transpose(0, 2, 1)).reshape(NE * FI, D)
    d["WrT"] = ct(kw["L1_attn_Wr"].T)
    d["WkT"] = ct(kw["L1_attn_Wk"].T)
    d["WvT"] = ct(kw["L1_attn_Wv"].T)
    d["WoT"] = ct(kw["L1_attn_Wo"].T)
    d["WfkT"] = ct(kw["L1_attn_Wfk"].T)
    d["WfvT"] = ct(kw["L1_attn_Wfv"].T)
    d["WfrT"] = ct(kw["L1_attn_Wfr"].T)
    d["WfcT"] = ct(kw["L1_ffn_W_fc"].T)
    d["WprojT"] = ct(kw["L1_ffn_W_proj"].T)
    d["A_fcT"] = ct(kw["L1_ffn_A_fc"].T)
    d["B_fcT"] = ct(kw["L1_ffn_B_fc"].T)
    d["A_projT"] = ct(kw["L1_ffn_A_proj"].T)
    d["B_projT"] = ct(kw["L1_ffn_B_proj"].T)
    d["ln_g"] = kw["L0_attn_ln_g"].reshape(1, D)
    d["ln_b"] = kw["L0_attn_ln_b"].reshape(1, D)
    d["conv_b"] = kw["L0_attn_conv_b"].reshape(1, D)
    d["Dp"] = kw["L0_attn_Dp"].reshape(1, D)
    d["b_dt"] = kw["L0_attn_b_dt"].reshape(1, D)
    d["ln1_w"] = kw["L1_attn_ln1_w"].reshape(1, D)
    d["ln1_b"] = kw["L1_attn_ln1_b"].reshape(1, D)
    d["ln2_w"] = kw["L1_attn_ln2_w"].reshape(1, D)
    d["ln2_b"] = kw["L1_attn_ln2_b"].reshape(1, D)
    d["decay_v"] = kw["L1_attn_decay"].reshape(1, D)
    d["u_v"] = kw["L1_attn_u"].reshape(1, D)
    d["b_proj"] = kw["L1_ffn_b_proj"].reshape(1, D)
    d["b_fc"] = kw["L1_ffn_b_fc"].reshape(1, FI)
    d["A_log"] = kw["L0_attn_A_log"].reshape(1, NST)
    n_ = np.arange(NST)
    pp = np.arange(128)
    d["S1c"] = (n_[:, None] % 4 == pp[None, :] // 32).astype(np.float32)
    d["S2c"] = (n_[:, None] // 4 == np.arange(16)[None, :]).astype(np.float32)
    import ml_dtypes
    d["Ssum"] = (pp[:, None] % 32 == np.arange(32)[None, :]).astype(ml_dtypes.bfloat16)
    sel3 = np.zeros((NST, 16, 128), np.float32)
    for q in range(16):
        sel3[:, q, :] = (n_[:, None] == q * 4 + pp[None, :] // 32)
    d["Ssel3"] = sel3.reshape(NST, 2048).astype(ml_dtypes.bfloat16)
    d["eye"] = np.eye(128).astype(ml_dtypes.bfloat16)
    d["eyef"] = np.eye(128).astype(np.float32)
    return d


def kernel(**kw):
    nc = build_nc(taps=DEBUG_TAPS)
    in_maps = [prep_inputs(b, kw) for b in range(8)]
    res = run_bass_kernel_spmd(nc, in_maps, core_ids=list(range(8)))
    outs = []
    for b in range(8):
        o = res.results[b]["out_fm"]
        outs.append(np.asarray(o).reshape(128, 8, T).transpose(2, 1, 0).reshape(T, D))
    return np.stack(outs, axis=0)


# version 03
# speedup vs baseline: 1.0212x; 1.0212x over previous
import contextlib
import numpy as np
import concourse.bass as bass
import concourse.mybir as mybir
from concourse.bass_utils import run_bass_kernel_spmd

F32 = mybir.dt.float32
F16 = mybir.dt.float16
BF16 = mybir.dt.bfloat16
AT = mybir.ActivationFunctionType
OP = mybir.AluOpType

T = 2048
D = 1024
NST = 64
NE = 8
FI = 4096
RNE_C = 1.5 * 2.0 ** 23
DEBUG_TAPS = ()


class P:

    ENGS = ("sync", "gpsimd", "vector", "scalar", "tensor")

    def __init__(self, nc):
        self.nc = nc
        self.ops = {k: [] for k in self.ENGS}
        self.semh = {}
        self.cnt = {}
        for name in ("s_sy", "s_gp", "s_ve", "s_ac", "s_te",
                     "d_sy", "d_gp", "d_ac", "d_gt", "d_st"):
            self.cnt[name] = 0
        self.echo_sb = {}
        self.echo_dr = None

    def make_sems(self, stack):
        for name in list(self.cnt):
            self.semh[name] = stack.enter_context(self.nc.semaphore(name))

    def em(self, eng, fn, is_dma=False):
        self.ops[eng].append(fn)
        self._last_dma = getattr(self, "_last_dma", {})
        self._last_dma[eng] = is_dma
        self._last_marked = getattr(self, "_last_marked", {})
        self._last_marked[eng] = False

    def inc(self, name, amt=1):
        self.cnt[name] += amt
        return self.cnt[name]

    def mark(self, eng):
        sname = {"sync": "s_sy", "gpsimd": "s_gp", "vector": "s_ve",
                 "scalar": "s_ac", "tensor": "s_te"}[eng]
        tv = self.inc(sname)
        h = self.semh[sname]
        already = getattr(self, "_last_marked", {}).get(eng, False)
        self._last_marked = getattr(self, "_last_marked", {})
        last_dma = getattr(self, "_last_dma", {}).get(eng, False)
        if self.ops[eng] and not already and not last_dma:
            prev = self.ops[eng].pop()
            self.em(eng, lambda e, h=h, prev=prev: prev(e).then_inc(h, 1))
        elif last_dma or not self.ops[eng]:
            self.em(eng, lambda e, h=h: e.nop().then_inc(h, 1))
        else:
            self.em(eng, lambda e, h=h: e.drain().then_inc(h, 1))
        self._last_marked[eng] = True
        return sname, tv

    SEM_ENG = {"s_sy": "sync", "s_gp": "gpsimd", "s_ve": "vector",
               "s_ac": "scalar", "s_te": "tensor",
               "d_sy": "sync", "d_gp": "gpsimd", "d_ac": "scalar"}

    def wait(self, eng, tok):
        sname, tv = tok
        h = self.semh[sname]
        box = [max(tv, self.cnt[sname])]
        self.pending = getattr(self, "pending", {})
        self.pending.setdefault(self.SEM_ENG[sname], []).append((sname, box))
        self.em(eng, lambda e, h=h, box=box: e.wait_ge(h, box[0]))
        self._flush(eng)

    def _flush(self, eng):
        self.pending = getattr(self, "pending", {})
        for sname, box in self.pending.pop(eng, []):
            box[0] = max(box[0], self.cnt[sname])

    def finalize_waits(self):
        self.pending = getattr(self, "pending", {})
        for eng in list(self.pending):
            self._flush(eng)

    def selfsync(self, eng):
        self.wait(eng, self.mark(eng))

    def barrier(self, engs=None):
        engs = engs or self.ENGS
        for _phase in range(2):
            toks = {eng: self.mark(eng) for eng in engs}
            for eng in engs:
                for other, tok in toks.items():
                    if other != eng:
                        self.wait(eng, tok)

    ECHO_THRESH = 600_000

    def dma(self, qeng, dst, src, **kw):
        sname = {"sync": "d_sy", "gpsimd": "d_gp", "scalar": "d_ac"}[qeng]
        h = self.semh[sname]

        def nbytes(ap):
            n = 1
            for s_ in ap.shape:
                n *= s_
            return n * mybir.dt.size(ap.dtype)

        if max(nbytes(dst), nbytes(src)) > self.ECHO_THRESH:
            ht = self.semh["d_gt" if qeng == "gpsimd" else "d_st"]
            self.em(qeng, lambda e, ht=ht, dst=dst, src=src, kw=kw:
                    e.dma_start(dst, src, **kw).then_inc(ht, 16), is_dma=True)
            tv = self.inc(sname, 16)
            ed, esr = self.echo_sb[qeng], self.echo_dr
            self.em(qeng, lambda e, h=h, ed=ed, esr=esr:
                    e.dma_start(ed, esr).then_inc(h, 16), is_dma=True)
            return sname, tv
        tv = self.inc(sname, 16)
        self.em(qeng, lambda e, h=h, dst=dst, src=src, kw=kw:
                e.dma_start(dst, src, **kw).then_inc(h, 16), is_dma=True)
        return sname, tv


def fmv(ap, dh=8):
    return ap.rearrange("q (dh t) -> q dh t", dh=dh)


def emit_program(p, A, taps):
    nc = p.nc
    sb = {}

    def S(name, shape, dtype):
        sb[name] = nc.alloc_sbuf_tensor("sb_" + name, shape, dtype).ap()
        return sb[name]

    VECS = ["ln_g", "ln_b", "conv_b", "Dp", "b_dt", "ln1_w", "ln1_b",
            "ln2_w", "ln2_b", "decay_v", "u_v", "b_proj"]
    for n in VECS:
        S(n, [128, 8], F32)
    S("b_fc_fm", [128, 32], F32)
    S("cw", [128, 32], F32)
    S("A_row", [1, 64], F32)
    S("Amov", [64, 16], F32)
    S("A_cm", [128, 16], F32)
    S("S1c", [64, 128], F32)
    S("S2c", [64, 16], F32)
    S("Ssum", [128, 32], BF16)
    S("Ssel3", [64, 2048], BF16)
    S("eyef", [128, 128], F32)
    S("eye", [128, 128], BF16)
    S("iota_s", [128, 512], F32)
    S("ones_c", [128, 1], F32)
    S("eps_c", [128, 1], F32)
    S("ones_r1f", [1, 512], F32)
    S("ones_r1b", [1, 16], BF16)
    S("ew", [128, 8], F32)
    S("eu", [128, 8], F32)
    S("row1", [1, 512], F32)
    S("row2", [1, 512], F32)
    S("row3", [1, 512], F32)
    S("aff", [8, 2048], F32)
    S("E8", [8, 512], F32)
    S("lo8", [8, 1], F32)
    S("hi8", [8, 1], F32)
    S("mid8", [8, 1], F32)
    S("cnt8", [8, 1], F32)
    S("tmp8a", [8, 1], F32)
    S("tmp8b", [8, 1], F32)
    S("slotcol", [128, 128], F32)
    S("affcol", [128, 128], F32)
    S("echo_g", [128, 16], F32)
    S("echo_s", [128, 16], F32)
    p.echo_sb = {"gpsimd": sb["echo_g"], "sync": sb["echo_s"],
                 "scalar": sb["echo_s"]}
    p.echo_dr = A["echo_d"]
    S("carry", [128, 512], F32)
    S("wa", [128, 16384], BF16)
    S("wb", [128, 16384], BF16)
    S("wc", [128, 16384], BF16)
    S("fa", [128, 8192], F32)
    S("fb", [128, 4096], F32)
    S("ga", [128, 8192], BF16)
    S("gb", [128, 8192], BF16)
    sb["msk"] = sb["fa"][0:8, 0:2048]
    sb["slotx"] = sb["fa"][0:8, 2048:4096]
    sb["iot8"] = sb["fa"][0:8, 4096:6144]

    ps = [nc.alloc_psum_tensor(f"ps{i}", [128, 512], F32).ap() for i in range(8)]

    def TE(fn):
        p.em("tensor", fn)

    def VE(fn, nd=False):
        if not nd:
            p.em("vector", lambda v: v.drain())
        p.em("vector", fn)

    def AC(fn, nd=False):
        if not nd:
            p.em("scalar", lambda s: s.drain())
        p.em("scalar", fn)

    def GP(fn):
        p.em("gpsimd", fn)

    def tapout(name, src_ap):
        if name in taps:
            p.barrier()
            tok = p.dma("sync", A["tap_" + name], src_ap)
            for eng in P.ENGS:
                p.wait(eng, tok)

    def tapout_d(name, dram_ap):
        tapout(name, dram_ap)


    toks = []
    for n in VECS:
        toks.append(p.dma("gpsimd", sb[n],
                          A[n].rearrange("a (dh q) -> a q dh", q=128),
                          allow_slow_non_contiguous=True))
    toks.append(p.dma("gpsimd", sb["b_fc_fm"],
                      A["b_fc"].rearrange("a (dh q) -> a q dh", q=128),
                      allow_slow_non_contiguous=True))
    toks.append(p.dma("gpsimd", sb["cw"].rearrange("q (k dh) -> q k dh", k=4),
                      A["conv_wT"].rearrange("k (dh q) -> q k dh", q=128),
                      allow_slow_non_contiguous=True))
    toks.append(p.dma("sync", sb["A_row"], A["A_log"]))
    toks.append(p.dma("sync", sb["S1c"], A["S1c"]))
    toks.append(p.dma("sync", sb["S2c"], A["S2c"]))
    toks.append(p.dma("sync", sb["Ssum"], A["Ssum"]))
    toks.append(p.dma("sync", sb["Ssel3"], A["Ssel3"]))
    toks.append(p.dma("sync", sb["eyef"], A["eyef"]))
    toks.append(p.dma("sync", sb["eye"], A["eye"]))
    for eng in ("vector", "scalar", "tensor", "gpsimd"):
        for tok in toks:
            p.wait(eng, tok)
    VE(lambda v: v.memset(sb["ones_c"], 1.0))
    VE(lambda v: v.memset(sb["eps_c"], 1e-5))
    VE(lambda v: v.memset(sb["ones_r1f"], 1.0))
    VE(lambda v: v.memset(sb["ones_r1b"], 1.0))
    VE(lambda v: v.memset(sb["carry"], 0.0))
    GP(lambda g: g.iota(sb["iota_s"], [[1, 512]], base=0, channel_multiplier=0,
                        allow_small_or_imprecise_dtypes=True))
    GP(lambda g: g.iota(sb["iot8"], [[1, 2048]], base=0, channel_multiplier=0,
                        allow_small_or_imprecise_dtypes=True))
    AC(lambda s: s.activation(sb["A_row"], sb["A_row"], AT.Exp))
    t_ac = p.mark("scalar")
    p.wait("vector", t_ac)
    VE(lambda v: v.tensor_scalar(sb["A_row"], sb["A_row"], -1.0, None, OP.mult))
    AC(lambda s: s.activation(sb["ew"], sb["decay_v"], AT.Exp))
    t_ac = p.mark("scalar")
    p.wait("vector", t_ac)
    VE(lambda v: v.tensor_scalar(sb["ew"], sb["ew"], -1.0, None, OP.mult))
    t_ve = p.mark("vector")
    p.wait("scalar", t_ve)
    AC(lambda s: s.activation(sb["ew"], sb["ew"], AT.Exp))
    AC(lambda s: s.activation(sb["eu"], sb["u_v"], AT.Exp))
    t_ve2 = p.mark("vector")
    p.wait("tensor", t_ve2)
    TE(lambda t: t.matmul(ps[0][0:64, 0:1], sb["A_row"],
                          sb["ones_r1f"][0:1, 0:1], start=True, stop=True))
    t_te = p.mark("tensor")
    p.wait("vector", t_te)
    VE(lambda v: v.tensor_copy(sb["fa"][0:64, 0:1], ps[0][0:64, 0:1]))
    p.selfsync("vector")
    VE(lambda v: v.tensor_scalar(sb["Amov"], sb["S2c"], sb["fa"][0:64, 0:1],
                                 None, OP.mult))
    t_ve = p.mark("vector")
    p.wait("tensor", t_ve)
    TE(lambda t: t.matmul(ps[1][:, 0:16], sb["S1c"], sb["Amov"],
                          start=True, stop=True))
    t_te = p.mark("tensor")
    p.wait("vector", t_te)
    VE(lambda v: v.tensor_copy(sb["A_cm"], ps[1][:, 0:16]))
    p.barrier()
    tapout("A_cm", sb["A_cm"])

    xTv = A["xT"].rearrange("(dh q) t -> q dh t", q=128)

    def ln_stats_apply(tt, src_f32_4096, dst_xn_slice, dst_xnn_slice,
                       gcol, bcol, want_xnn=True):
        xt = src_f32_4096
        xsq = sb["fa"][:, 4096:8192]
        VE(lambda v: v.tensor_tensor(xsq, xt, xt, OP.mult))
        t_ve = p.mark("vector")
        p.wait("tensor", t_ve)
        for dh in range(8):
            TE(lambda t, dh=dh: t.matmul(
                ps[0][0:1, :], sb["ones_c"], fmv(xt)[:, dh, :],
                start=(dh == 0), stop=(dh == 7)))
        for dh in range(8):
            TE(lambda t, dh=dh: t.matmul(
                ps[1][0:1, :], sb["ones_c"], fmv(xsq, 8)[:, dh, :],
                start=(dh == 0), stop=(dh == 7)))
        t_te = p.mark("tensor")
        p.wait("vector", t_te)
        r1, r2, r3 = sb["row1"], sb["row2"], sb["row3"]
        sl = slice(0, 512)
        VE(lambda v: v.tensor_scalar(r1[0:1, sl], ps[0][0:1, :], 1.0 / D, None, OP.mult))
        VE(lambda v: v.tensor_scalar(r2[0:1, sl], ps[1][0:1, :], 1.0 / D, None, OP.mult))
        VE(lambda v: v.tensor_tensor(r3[0:1, sl], r1[0:1, sl], r1[0:1, sl], OP.mult))
        VE(lambda v: v.tensor_tensor(r2[0:1, sl], r2[0:1, sl], r3[0:1, sl], OP.subtract))
        t_ve = p.mark("vector")
        p.wait("scalar", t_ve)
        AC(lambda s: s.activation(r3[0:1, sl], r2[0:1, sl], AT.Sqrt, bias=sb["eps_c"][0:1, :]))
        t_ac = p.mark("scalar")
        p.wait("vector", t_ac)
        VE(lambda v: v.reciprocal(r3[0:1, sl], r3[0:1, sl]))
        if want_xnn:
            VE(lambda v: v.tensor_tensor(r2[0:1, sl], r2[0:1, sl], r3[0:1, sl], OP.mult))
            VE(lambda v: v.tensor_tensor(r2[0:1, sl], r2[0:1, sl], r3[0:1, sl], OP.mult))
            t_ve = p.mark("vector")
            p.wait("scalar", t_ve)
            AC(lambda s: s.activation(r2[0:1, sl], r2[0:1, sl], AT.Sqrt, bias=sb["eps_c"][0:1, :]))
            t_ac = p.mark("scalar")
            p.wait("vector", t_ac)
            VE(lambda v: v.reciprocal(r2[0:1, sl], r2[0:1, sl]))
            VE(lambda v: v.tensor_tensor(r2[0:1, sl], r2[0:1, sl], r3[0:1, sl], OP.mult))
        t_ve = p.mark("vector")
        p.wait("tensor", t_ve)
        TE(lambda t: t.matmul(ps[2], sb["ones_r1f"][0:1, 0:128], r1[0:1, sl],
                              start=True, stop=True))
        TE(lambda t: t.matmul(ps[3], sb["ones_r1f"][0:1, 0:128], r3[0:1, sl],
                              start=True, stop=True))
        if want_xnn:
            TE(lambda t: t.matmul(ps[4], sb["ones_r1f"][0:1, 0:128], r2[0:1, sl],
                                  start=True, stop=True))
        t_te = p.mark("tensor")
        p.wait("vector", t_te)
        xc = sb["fa"]
        for dh in range(8):
            VE(lambda v, dh=dh: v.tensor_tensor(
                fmv(xc, 16)[:, dh, :], fmv(xt)[:, dh, :], ps[2], OP.subtract))
        if dst_xn_slice is not None:
            for dh in range(8):
                VE(lambda v, dh=dh: v.tensor_tensor(
                    dst_xn_slice(dh), fmv(xc, 16)[:, dh, :], ps[3], OP.mult))
        if want_xnn:
            for dh in range(8):
                VE(lambda v, dh=dh: v.tensor_tensor(
                    fmv(xc, 16)[:, 8 + dh, :], fmv(xc, 16)[:, dh, :], ps[4], OP.mult))
            for dh in range(8):
                VE(lambda v, dh=dh: v.tensor_scalar(
                    dst_xnn_slice(dh), fmv(xc, 16)[:, 8 + dh, :],
                    gcol[:, dh:dh + 1], bcol[:, dh:dh + 1], OP.mult, OP.add))
        t_ve = p.mark("vector")
        p.wait("tensor", t_ve)
        p.wait("scalar", t_ve)
        return t_ve

    tok_win = p.dma("gpsimd", sb["wa"],
                    A["WinT"].rearrange("(kh q) o -> q kh o", q=128))
    tok_gw = p.dma("gpsimd", sb["ga"][:, 0:64],
                   A["gate_wT"].rearrange("(kh q) o -> q kh o", q=128))

    prev_mark = None
    for tt in range(4):
        xt = sb["fb"]
        if prev_mark is not None:
            p.wait("sync", prev_mark)
        tok = p.dma("sync", fmv(xt), xTv[:, :, tt * 512:(tt + 1) * 512])
        p.wait("vector", tok)
        prev_mark = ln_stats_apply(
            tt, xt,
            lambda dh, tt=tt: fmv(sb["wb"])[:, dh, tt * 512:(tt + 1) * 512],
            lambda dh, tt=tt: fmv(sb["wc"])[:, dh, tt * 512:(tt + 1) * 512],
            sb["ln_g"], sb["ln_b"])
    p.wait("sync", prev_mark)
    tok_xn = p.dma("sync", A["xn_fm"], sb["wb"])
    tapout("xn_bf", sb["wb"])
    tapout("xnn_bf", sb["wc"])

    p.wait("tensor", tok_gw)
    gwv = sb["ga"][:, 0:64].rearrange("q (kh o) -> q kh o", kh=8)
    for tt in range(4):
        slt = slice(tt * 512, (tt + 1) * 512)
        for kh in range(8):
            TE(lambda t, kh=kh, slt=slt: t.matmul(
                ps[7][0:8, :], gwv[:, kh, :], fmv(sb["wb"])[:, kh, slt],
                start=(kh == 0), stop=(kh == 7)))
        t_te = p.mark("tensor")
        p.wait("scalar", t_te)
        AC(lambda s: s.activation(
            sb["E8"], ps[7][0:8, :], AT.Exp))
        t_ac = p.mark("scalar")
        p.wait("tensor", t_ac)
        p.wait("vector", t_ac)
        TE(lambda t: t.matmul(
            ps[0][0:1, :], sb["ones_c"][0:8, :], sb["E8"],
            start=True, stop=True))
        t_te = p.mark("tensor")
        p.wait("vector", t_te)
        VE(lambda v: v.reciprocal(sb["row1"], ps[0][0:1, :]))
        t_ve = p.mark("vector")
        p.wait("tensor", t_ve)
        TE(lambda t: t.matmul(
            ps[1][0:8, :], sb["ones_r1f"][0:1, 0:8], sb["row1"],
            start=True, stop=True))
        t_te = p.mark("tensor")
        p.wait("vector", t_te)
        VE(lambda v, slt=slt: v.tensor_tensor(
            sb["aff"][:, slt], sb["E8"], ps[1][0:8, :], OP.mult))
        prev_aff = p.mark("vector")
        p.wait("tensor", prev_aff)
    tapout("aff", sb["aff"])

    xntm_img = A["xn_tm"].rearrange("(tc q) d -> tc q d", q=128)
    prev_tm = None
    for tc in range(16):
        if prev_tm is not None:
            p.wait("scalar", prev_tm)
        for dh in range(8):
            TE(lambda t, tc=tc, dh=dh: t.transpose(
                ps[6].bitcast(BF16)[:, 0:128],
                fmv(sb["wb"])[:, dh, tc * 128:(tc + 1) * 128], sb["eye"]))
            t_te = p.mark("tensor")
            p.wait("scalar", t_te)
            AC(lambda s, dh=dh: s.copy(
                sb["ga"][:, 1024 + dh * 128:1024 + (dh + 1) * 128],
                ps[6].bitcast(BF16)[:, 0:128]))
            t_ac = p.mark("scalar")
            p.wait("tensor", t_ac)
        t_ac = p.mark("scalar")
        p.wait("sync", t_ac)
        prev_tm = p.dma("sync", xntm_img[tc], sb["ga"][:, 1024:2048])

    p.wait("tensor", tok_win)
    p.wait("tensor", tok_xn)
    wav = fmv(sb["wa"])
    for tt in range(4):
        slt = slice(tt * 512, (tt + 1) * 512)
        for oc in range(16):
            bank = oc % 2 + 5
            for kh in range(8):
                TE(lambda t, oc=oc, kh=kh, bank=bank, slt=slt: t.matmul(
                    ps[bank], wav[:, kh, oc * 128:(oc + 1) * 128],
                    fmv(sb["wc"])[:, kh, slt], start=(kh == 0), stop=(kh == 7)))
            t_te = p.mark("tensor")
            if oc < 8:
                p.wait("vector", t_te)
                VE(lambda v, oc=oc, tt=tt, bank=bank: v.tensor_copy(
                    fmv(sb["wb"])[:, oc, tt * 512:(tt + 1) * 512], ps[bank]))
                t_ve = p.mark("vector")
                p.wait("tensor", t_ve)
            else:
                p.wait("scalar", t_te)
                dst = sb["ga"] if oc < 12 else sb["gb"]
                AC(lambda s, oc=oc, tt=tt, bank=bank, dst=dst: s.activation(
                    fmv(dst, 4)[:, (oc - 8) % 4, tt * 512:(tt + 1) * 512],
                    ps[bank], AT.Silu))
                t_ac = p.mark("scalar")
                p.wait("tensor", t_ac)
    p.barrier()
    tok_sza = p.dma("sync", fmv(A["siluz_fm"])[:, 0:4, :], fmv(sb["ga"], 4))
    tok_szb = p.dma("sync", fmv(A["siluz_fm"])[:, 4:8, :], fmv(sb["gb"], 4))
    if "siluz" in taps:
        p.barrier()
        ta = p.dma("sync", fmv(A["tap_siluz"])[:, 0:4, :], fmv(sb["ga"], 4))
        tb = p.dma("sync", fmv(A["tap_siluz"])[:, 4:8, :], fmv(sb["gb"], 4))
        for eng in P.ENGS:
            p.wait(eng, tb)
    tapout("x_main", sb["wb"])

    for dh in range(8):
        accs = sb["fa"][:, 0:2048]
        VE(lambda v, dh=dh, accs=accs: v.tensor_scalar(
            accs, fmv(sb["wb"])[:, dh, :], sb["cw"][:, 24 + dh:25 + dh], None, OP.mult))
        for k in range(3):
            sh = 3 - k
            VE(lambda v, dh=dh, k=k, sh=sh, accs=accs: v.scalar_tensor_tensor(
                accs[:, sh:2048], fmv(sb["wb"])[:, dh, 0:2048 - sh],
                sb["cw"][:, k * 8 + dh:k * 8 + dh + 1], accs[:, sh:2048],
                OP.mult, OP.add))
        t_ve = p.mark("vector")
        p.wait("scalar", t_ve)
        AC(lambda s, dh=dh, accs=accs: s.activation(
            fmv(sb["wc"])[:, dh, :], accs,
            AT.Silu, bias=sb["conv_b"][:, dh:dh + 1]))
        t_ac = p.mark("scalar")
        p.wait("vector", t_ac)
    p.barrier()
    tok_xc = p.dma("sync", A["xconv_fm"], sb["wc"])
    tapout("x_conv", sb["wc"])

    tok_wdt = p.dma("gpsimd", sb["wa"][:, 0:8192],
                    A["WdtT"].rearrange("(kh q) o -> q kh o", q=128))
    p.wait("tensor", tok_wdt)
    wdtv = sb["wa"][:, 0:8192].rearrange("q (kh o) -> q kh o", kh=8)
    for tt in range(4):
        slt = slice(tt * 512, (tt + 1) * 512)
        for oc in range(8):
            bank = oc % 2 + 5
            for kh in range(8):
                TE(lambda t, oc=oc, kh=kh, bank=bank, slt=slt: t.matmul(
                    ps[bank], wdtv[:, kh, oc * 128:(oc + 1) * 128],
                    fmv(sb["wc"])[:, kh, slt], start=(kh == 0), stop=(kh == 7)))
            t_te = p.mark("tensor")
            p.wait("scalar", t_te)
            AC(lambda s, oc=oc, bank=bank: s.activation(
                sb["fa"][:, 0:512], ps[bank], AT.Exp,
                bias=sb["b_dt"][:, oc:oc + 1]))
            AC(lambda s, oc=oc, tt=tt: s.activation(
                fmv(sb["wb"].bitcast(F16))[:, oc, tt * 512:(tt + 1) * 512],
                sb["fa"][:, 0:512], AT.Ln, bias=sb["ones_c"]))
            t_ac = p.mark("scalar")
            p.wait("tensor", t_ac)
    p.barrier()
    tok_dt = p.dma("sync", A["dt_fm"], sb["wb"].bitcast(F16))
    tapout("dt", sb["wb"].bitcast(F16))

    p.wait("gpsimd", tok_sza)
    tok_wB = p.dma("gpsimd", sb["ga"][:, 0:512],
                   A["WBT"].rearrange("(kh q) o -> q kh o", q=128))
    tok_wC = p.dma("gpsimd", sb["ga"][:, 512:1024],
                   A["WCT"].rearrange("(kh q) o -> q kh o", q=128))
    p.wait("tensor", tok_wB)
    p.wait("tensor", tok_wC)
    for tt in range(4):
        slt = slice(tt * 512, (tt + 1) * 512)
        for which, base in (("B", 0), ("C", 512)):
            bank = 5 if which == "B" else 6
            stat = sb["ga"][:, base:base + 512].rearrange("q (kh o) -> q kh o", kh=8)
            for kh in range(8):
                TE(lambda t, kh=kh, bank=bank, slt=slt, stat=stat: t.matmul(
                    ps[bank][0:64, :], stat[:, kh, 0:64],
                    fmv(sb["wc"])[:, kh, slt], start=(kh == 0), stop=(kh == 7)))
            t_te = p.mark("tensor")
            p.wait("vector", t_te)
            fbb = sb["fb"].bitcast(BF16)
            dst = fbb[0:64, 0:2048] if which == "B" else fbb[0:64, 2048:4096]
            VE(lambda v, dst=dst, tt=tt, bank=bank: v.tensor_copy(
                dst[:, tt * 512:(tt + 1) * 512], ps[bank][0:64, :]))
            t_ve = p.mark("vector")
            p.wait("tensor", t_ve)
    p.barrier()
    tapout("Bm", sb["fb"].bitcast(BF16)[0:64, 0:2048])
    tapout("Cm", sb["fb"].bitcast(BF16)[0:64, 2048:4096])


    p.barrier()
    msk, slx, io8 = sb["msk"], sb["slotx"], sb["iot8"]
    VE(lambda v: v.memset(sb["lo8"], 0.0))
    VE(lambda v: v.memset(sb["hi8"], 1.0))
    for it in range(30):
        VE(lambda v: v.tensor_tensor(sb["mid8"], sb["lo8"], sb["hi8"], OP.add))
        VE(lambda v: v.tensor_scalar(sb["mid8"], sb["mid8"], 0.5, None, OP.mult))
        VE(lambda v: v.tensor_scalar(msk, sb["aff"], sb["mid8"], None, OP.is_ge))
        VE(lambda v: v.tensor_reduce(sb["cnt8"], msk, mybir.AxisListType.X, OP.add))
        VE(lambda v: v.tensor_scalar(sb["cnt8"], sb["cnt8"], 512.0, None, OP.is_ge))
        VE(lambda v: v.tensor_tensor(sb["tmp8a"], sb["mid8"], sb["lo8"], OP.subtract))
        VE(lambda v: v.tensor_tensor(sb["tmp8a"], sb["tmp8a"], sb["cnt8"], OP.mult))
        VE(lambda v: v.tensor_tensor(sb["lo8"], sb["lo8"], sb["tmp8a"], OP.add))
        VE(lambda v: v.tensor_tensor(sb["tmp8b"], sb["hi8"], sb["mid8"], OP.subtract))
        VE(lambda v: v.tensor_tensor(sb["tmp8b"], sb["tmp8b"], sb["cnt8"], OP.mult))
        VE(lambda v: v.tensor_tensor(sb["hi8"], sb["mid8"], sb["tmp8b"], OP.add))
    VE(lambda v: v.tensor_scalar(msk, sb["aff"], sb["lo8"], None, OP.is_ge))
    GP(lambda g: g.iota(io8, [[1, 2048]], base=0, channel_multiplier=0,
                        allow_small_or_imprecise_dtypes=True))
    t_gp = p.mark("gpsimd")
    p.wait("vector", t_gp)
    VE(lambda v: v.memset(sb["fa"][0:8, 6144:8192], 1.0))
    VE(lambda v: v.tensor_tensor_scan(slx, sb["fa"][0:8, 6144:8192], msk,
                                      0.0, OP.mult, OP.add))
    VE(lambda v: v.tensor_tensor(slx, slx, msk, OP.mult))
    VE(lambda v: v.tensor_scalar(slx, slx, 1.0, None, OP.subtract))
    t_ve = p.mark("vector")
    p.wait("tensor", t_ve)
    for c in range(16):
        TE(lambda t, c=c: t.transpose(ps[6][:, 0:8], slx[:, c * 128:(c + 1) * 128],
                                      sb["eyef"][0:8, 0:8]))
        TE(lambda t, c=c: t.transpose(ps[7][:, 0:8], sb["aff"][:, c * 128:(c + 1) * 128],
                                      sb["eyef"][0:8, 0:8]))
        t_te = p.mark("tensor")
        p.wait("vector", t_te)
        VE(lambda v, c=c: v.tensor_copy(sb["slotcol"][:, c * 8:(c + 1) * 8], ps[6][:, 0:8]))
        VE(lambda v, c=c: v.tensor_copy(sb["affcol"][:, c * 8:(c + 1) * 8], ps[7][:, 0:8]))
        t_ve = p.mark("vector")
        p.wait("tensor", t_ve)
    p.barrier()

    tapout("slotx", sb["slotx"])
    tapout("thr", sb["lo8"])
    tapout("slotcol", sb["slotcol"])
    tapout_d("xntm", A["xn_tm"])

    xe_img = A["xe_tm"].rearrange("(e q) f -> e q f", q=128)
    wgv = A["wgT"].rearrange("(e kh q) o -> e q kh o", q=128, kh=8)
    wuv = A["wuT"].rearrange("(e kh q) o -> e q kh o", q=128, kh=8)
    wdv = A["wdT"].rearrange("(e2 ic q) o -> e2 ic q o", q=128, ic=32)
    xnv = A["xn_tm"].rearrange("(tc q) (dc w) -> q tc dc w", q=128, w=128)
    for e in range(NE):
        for c in range(16):
            VE(lambda v, c=c, e=e: v.tensor_scalar(
                sb["ga"].rearrange("q (c s) -> q c s", c=16)[:, c, :],
                sb["iota_s"], sb["slotcol"][:, c * 8 + e:c * 8 + e + 1], None,
                OP.is_equal))
        t_ve = p.mark("vector")
        p.wait("tensor", t_ve)
        p.wait("gpsimd", t_ve)
        st_slot = [0, 4096]
        toks = [None] * 8
        marks = [None] * 8
        toks[0] = p.dma("gpsimd", sb["gb"][:, 0:2048], xnv[:, :, 0, :])
        for dc in range(8):
            if dc + 1 < 8:
                if dc >= 1:
                    p.wait("gpsimd", marks[dc - 1])
                sl = st_slot[(dc + 1) % 2]
                toks[dc + 1] = p.dma("gpsimd", sb["gb"][:, sl:sl + 2048],
                                     xnv[:, :, dc + 1, :])
            bank = dc % 2
            sl = st_slot[dc % 2]
            p.wait("tensor", toks[dc])
            for tc in range(16):
                TE(lambda t, tc=tc, bank=bank, sl=sl: t.matmul(
                    ps[bank], sb["gb"][:, sl:sl + 2048].rearrange("q (tc w) -> q tc w", tc=16)[:, tc, :],
                    sb["ga"].rearrange("q (c s) -> q c s", c=16)[:, tc, :],
                    start=(tc == 0), stop=(tc == 15)))
            marks[dc] = t_te = p.mark("tensor")
            p.wait("scalar", t_te)
            AC(lambda s, dc=dc, bank=bank: s.copy(
                sb["wc"].rearrange("q (dc s) -> q dc s", dc=32)[:, dc, :], ps[bank]))
            t_ac = p.mark("scalar")
            p.wait("tensor", t_ac)
        p.wait("gpsimd", t_ac)
        if e == 0:
            tapout("xin0", sb["wc"][:, 0:4096])
        gu_slots = [(5120, 6144), (8192, 9216)]
        gtoks = [None] * 32
        gmarks = [None] * 32

        def gu_load(ic):
            sg, su = gu_slots[ic % 2]
            p.dma("gpsimd", sb["wc"][:, sg:sg + 1024],
                  wgv[e, :, :, ic * 128:(ic + 1) * 128])
            gtoks[ic] = p.dma("gpsimd", sb["wc"][:, su:su + 1024],
                              wuv[e, :, :, ic * 128:(ic + 1) * 128])
        gu_load(0)
        for ic in range(32):
            if ic + 1 < 32:
                if ic >= 1:
                    p.wait("gpsimd", gmarks[ic - 1])
                gu_load(ic + 1)
            sg, su = gu_slots[ic % 2]
            p.wait("tensor", gtoks[ic])
            for kh in range(8):
                TE(lambda t, kh=kh, sg=sg: t.matmul(
                    ps[2], sb["wc"][:, sg:sg + 1024].rearrange("q (kh w) -> q kh w", kh=8)[:, kh, :],
                    sb["wc"].rearrange("q (dc s) -> q dc s", dc=32)[:, kh, :],
                    start=(kh == 0), stop=(kh == 7)))
            for kh in range(8):
                TE(lambda t, kh=kh, su=su: t.matmul(
                    ps[3], sb["wc"][:, su:su + 1024].rearrange("q (kh w) -> q kh w", kh=8)[:, kh, :],
                    sb["wc"].rearrange("q (dc s) -> q dc s", dc=32)[:, kh, :],
                    start=(kh == 0), stop=(kh == 7)))
            gmarks[ic] = t_te = p.mark("tensor")
            p.wait("scalar", t_te)
            AC(lambda s: s.activation(sb["gb"][:, 2048:2560], ps[2], AT.Silu))
            t_ac = p.mark("scalar")
            p.wait("vector", t_ac)
            VE(lambda v, ic=ic: v.tensor_tensor(
                sb["wa"].rearrange("q (ic s) -> q ic s", ic=32)[:, ic, :],
                sb["gb"][:, 2048:2560], ps[3], OP.mult))
            t_ve = p.mark("vector")
            p.wait("tensor", t_ve)
        if e == 0:
            tapout("hi0", sb["wa"][:, 0:16384])
        dn_slots = [10240, 11264]
        dtoks = [None] * 32
        dmarks = [None] * 32
        dtoks[0] = p.dma("gpsimd", sb["wc"][:, 10240:11264], wdv[e, 0, :, :])
        for ic in range(32):
            if ic + 1 < 32:
                if ic >= 1:
                    p.wait("gpsimd", dmarks[ic - 1])
                sl = dn_slots[(ic + 1) % 2]
                dtoks[ic + 1] = p.dma("gpsimd", sb["wc"][:, sl:sl + 1024],
                                      wdv[e, ic + 1, :, :])
            sl = dn_slots[ic % 2]
            p.wait("tensor", dtoks[ic])
            for sc in range(4):
                for dh2 in range(2):
                    TE(lambda t, ic=ic, sc=sc, dh2=dh2, sl=sl: t.matmul(
                        ps[sc * 2 + dh2],
                        sb["wa"].rearrange("q (ic s) -> q ic s", ic=32)[:, ic, sc * 128:(sc + 1) * 128],
                        sb["wc"][:, sl + dh2 * 512:sl + (dh2 + 1) * 512],
                        start=(ic == 0), stop=(ic == 31)))
            dmarks[ic] = p.mark("tensor")
        t_te = p.mark("tensor")
        p.wait("scalar", t_te)
        for sc in range(4):
            for dh2 in range(2):
                AC(lambda s, sc=sc, dh2=dh2: s.copy(
                    sb["gb"][:, sc * 1024 + dh2 * 512: sc * 1024 + (dh2 + 1) * 512],
                    ps[sc * 2 + dh2]))
        t_ac = p.mark("scalar")
        p.wait("tensor", t_ac)
        p.wait("gpsimd", t_ac)
        p.wait("sync", t_ac)
        tok = p.dma("sync", xe_img[e], sb["gb"][:, 0:4096])
        p.wait("vector", tok)
        p.wait("scalar", tok)
        p.wait("gpsimd", tok)
        if e == 0:
            tapout("xe0", sb["gb"][:, 0:4096])
    p.barrier()

    wcf16 = sb["wc"].bitcast(F16)
    fbb = sb["fb"].bitcast(BF16)
    dtimg = A["dt_fm"].rearrange("q (dh t) -> q dh t", dh=8)
    xcimg = A["xconv_fm"].rearrange("q (dh t) -> q dh t", dh=8)
    szimg = A["siluz_fm"].rearrange("q (dh t) -> q dh t", dh=8)
    spimg = A["spre_fm"].rearrange("q (dh t) -> q dh t", dh=8)
    DTS = [0, 1024]
    UDS = [2048, 3072]
    BBS = [4096, 5120]
    SZS = 6144
    SPS = 7168
    for th in range(2):
        prev_ac = None
        for q in range(16):
            for hh in range(2):
                lo = th * 1024 + hh * 512
                TE(lambda t, q=q, lo=lo: t.matmul(
                    ps[6], sb["Ssel3"][:, q * 128:(q + 1) * 128],
                    fbb[0:64, lo:lo + 512], start=True, stop=True))
                TE(lambda t, q=q, lo=lo: t.matmul(
                    ps[7], sb["Ssel3"][:, q * 128:(q + 1) * 128],
                    fbb[0:64, 2048 + lo:2048 + lo + 512], start=True, stop=True))
                t_te = p.mark("tensor")
                p.wait("scalar", t_te)
                AC(lambda s, q=q, hh=hh: s.copy(
                    fmv(sb["wa"], 16)[:, q, hh * 512:(hh + 1) * 512], ps[6]))
                AC(lambda s, q=q, hh=hh: s.copy(
                    fmv(sb["wb"], 16)[:, q, hh * 512:(hh + 1) * 512], ps[7]))
                prev_ac = p.mark("scalar")
                p.wait("tensor", prev_ac)
        p.wait("vector", prev_ac)
        p.wait("sync", prev_ac)
        prev_ep = None
        prev_sp_tok = None
        for g in range(32):
            dh, dl0 = g // 4, (g % 4) * 32
            gs = g % 2
            tsl = slice(th * 1024, (th + 1) * 1024)
            if prev_ep is not None:
                p.wait("sync", prev_ep)
            for pg in range(4):
                p.dma("sync", wcf16[pg * 32:(pg + 1) * 32, DTS[gs]:DTS[gs] + 1024],
                      dtimg[dl0:dl0 + 32, dh, tsl])
                p.dma("sync", sb["wc"][pg * 32:(pg + 1) * 32, UDS[gs]:UDS[gs] + 1024],
                      xcimg[dl0:dl0 + 32, dh, tsl])
            tok_dup = p.dma("sync", sb["wc"][0:32, SZS:SZS + 1024],
                            szimg[dl0:dl0 + 32, dh, tsl])
            p.wait("scalar", tok_dup)
            p.wait("vector", tok_dup)
            for qp in range(8):
                q0, q1 = 2 * qp, 2 * qp + 1
                AC(lambda s, q=q0, gs=gs: s.activation(
                    sb["fa"][:, 0:1024], wcf16[:, DTS[gs]:DTS[gs] + 1024],
                    AT.Exp, scale=sb["A_cm"][:, q:q + 1]), nd=(qp > 0))
                AC(lambda s, q=q1, gs=gs: s.activation(
                    sb["fa"][:, 1024:2048], wcf16[:, DTS[gs]:DTS[gs] + 1024],
                    AT.Exp, scale=sb["A_cm"][:, q:q + 1]), nd=True)
                t_ac = p.mark("scalar")
                VE(lambda v, q=q0, gs=gs: v.tensor_tensor(
                    sb["wc"][:, BBS[0]:BBS[0] + 1024],
                    sb["wc"][:, UDS[gs]:UDS[gs] + 1024],
                    fmv(sb["wa"], 16)[:, q, :], OP.mult))
                VE(lambda v, q=q1, gs=gs: v.tensor_tensor(
                    sb["wc"][:, BBS[1]:BBS[1] + 1024],
                    sb["wc"][:, UDS[gs]:UDS[gs] + 1024],
                    fmv(sb["wa"], 16)[:, q, :], OP.mult), nd=True)
                p.wait("vector", t_ac)
                for qq, asl in ((q0, 0), (q1, 1024)):
                    hbuf = sb["ga"] if qq < 8 else sb["gb"]
                    hsl = (qq % 8) * 1024
                    VE(lambda v, q=qq, g=g, asl=asl, hbuf=hbuf, hsl=hsl:
                       v.tensor_tensor_scan(
                           hbuf[:, hsl:hsl + 1024], sb["fa"][:, asl:asl + 1024],
                           sb["wc"][:, BBS[asl // 1024]:BBS[asl // 1024] + 1024],
                           sb["carry"][:, g * 16 + q:g * 16 + q + 1],
                           OP.mult, OP.add), nd=(asl == 1024))
                t_ve = p.mark("vector")
                p.wait("scalar", t_ve)
            VE(lambda v, g=g: v.tensor_copy(
                sb["carry"][:, g * 16:g * 16 + 8],
                sb["ga"].rearrange("q (a w) -> q a w", a=8)[:, :, 1023]))
            VE(lambda v, g=g: v.tensor_copy(
                sb["carry"][:, g * 16 + 8:g * 16 + 16],
                sb["gb"].rearrange("q (a w) -> q a w", a=8)[:, :, 1023]), nd=True)
            for q in range(16):
                hbuf = sb["ga"] if q < 8 else sb["gb"]
                hsl = (q % 8) * 1024
                VE(lambda v, q=q, hbuf=hbuf, hsl=hsl: v.tensor_tensor(
                    hbuf[:, hsl:hsl + 1024], hbuf[:, hsl:hsl + 1024],
                    fmv(sb["wb"], 16)[:, q, :], OP.mult), nd=(q > 0))
            t_ve = p.mark("vector")
            p.wait("tensor", t_ve)
            for ti in range(2):
                for q in range(16):
                    hbuf = sb["ga"] if q < 8 else sb["gb"]
                    hsl = (q % 8) * 1024
                    TE(lambda t, q=q, ti=ti, hbuf=hbuf, hsl=hsl: t.matmul(
                        ps[ti][0:32, :], sb["Ssum"],
                        hbuf[:, hsl + ti * 512: hsl + (ti + 1) * 512],
                        start=(q == 0), stop=(q == 15)))
            t_te = p.mark("tensor")
            p.wait("vector", t_te)
            if prev_sp_tok is not None:
                p.wait("vector", prev_sp_tok)
            for ti in range(2):
                VE(lambda v, ti=ti, gs=gs, dl0=dl0, dh=dh: v.tensor_scalar(
                    sb["fa"][0:32, 2048 + ti * 512: 2048 + (ti + 1) * 512],
                    sb["wc"][0:32, UDS[gs] + ti * 512: UDS[gs] + (ti + 1) * 512],
                    sb["Dp"][dl0:dl0 + 32, dh:dh + 1], None, OP.mult), nd=(ti == 1))
            for ti in range(2):
                VE(lambda v, ti=ti: v.tensor_tensor(
                    sb["fa"][0:32, 2048 + ti * 512: 2048 + (ti + 1) * 512],
                    sb["fa"][0:32, 2048 + ti * 512: 2048 + (ti + 1) * 512],
                    ps[ti][0:32, :], OP.add), nd=(ti == 1))
            for ti in range(2):
                VE(lambda v, ti=ti: v.tensor_tensor(
                    sb["wc"][0:32, SPS + ti * 512: SPS + (ti + 1) * 512],
                    sb["fa"][0:32, 2048 + ti * 512: 2048 + (ti + 1) * 512],
                    sb["wc"][0:32, SZS + ti * 512: SZS + (ti + 1) * 512], OP.mult), nd=(ti == 1))
            prev_ep = p.mark("vector")
            p.wait("tensor", prev_ep)
            p.wait("sync", prev_ep)
            prev_sp_tok = p.dma("sync", spimg[dl0:dl0 + 32, dh, tsl],
                                sb["wc"][0:32, SPS:SPS + 1024])
    p.barrier()
    tapout_d("spre", A["spre_fm"])

    tok_wo = p.dma("gpsimd", sb["wb"][:, 0:8192],
                   A["WoutT"].rearrange("(kh q) o -> q kh o", q=128))
    wov = sb["wb"][:, 0:8192].rearrange("q (kh o) -> q kh o", kh=8)
    xeimg2 = A["xe_tm"].rearrange("(e q) (sh w) -> e q sh w", q=128, sh=4)
    x1img = A["x1_fm"].rearrange("q (dh t) -> q dh t", dh=8)
    xnnimg = A["xnn1_fm"].rearrange("q (dh t) -> q dh t", dh=8)
    p.wait("tensor", tok_wo)
    for tt in range(4):
        ts512 = slice(tt * 512, (tt + 1) * 512)
        for e in range(8):
            for tc in range(4):
                ci = tt * 4 + tc
                VE(lambda v, ci=ci, e=e: v.tensor_scalar(
                    sb["fa"].bitcast(BF16)[:, 0:512], sb["iota_s"],
                    sb["slotcol"][:, ci * 8 + e:ci * 8 + e + 1],
                    sb["affcol"][:, ci * 8 + e:ci * 8 + e + 1],
                    OP.is_equal, OP.mult))
                t_ve = p.mark("vector")
                p.wait("tensor", t_ve)
                for sh in range(4):
                    TE(lambda t, sh=sh: t.transpose(
                        ps[6].bitcast(BF16)[:, 0:128],
                        sb["fa"].bitcast(BF16)[:, sh * 128:(sh + 1) * 128],
                        sb["eye"]))
                    t_te = p.mark("tensor")
                    p.wait("scalar", t_te)
                    dstb = sb["ga"] if e < 4 else sb["wb"][:, 8192:16384]
                    zi = (e * 4 + sh) % 16
                    AC(lambda s, dstb=dstb, zi=zi, tc=tc: s.copy(
                        dstb.rearrange("q (z w) -> q z w", z=16)
                        [:, zi, tc * 128:(tc + 1) * 128],
                        ps[6].bitcast(BF16)[:, 0:128]))
                    t_ac = p.mark("scalar")
                    p.wait("tensor", t_ac)
                    p.wait("vector", t_ac)
        if tt == 0 and "p2g" in taps:
            tapout("p2g", sb["ga"])
            tapout("affcol", sb["affcol"])
        tok_sp = p.dma("sync", sb["wc"][:, 0:4096],
                       A["spre_fm"].rearrange("q (dh t) -> q dh t", dh=8)[:, :, ts512])
        tok_xn2 = p.dma("sync", sb["wc"][:, 4096:8192],
                        A["xn_fm"].rearrange("q (dh t) -> q dh t", dh=8)[:, :, ts512])
        tok_x = p.dma("sync", fmv(sb["fb"]), xTv[:, :, ts512])
        p.wait("tensor", tok_sp)
        p.wait("vector", tok_x)
        dbuf = "gmoe" not in taps

        def st_load(dc2):
            base = 4096 if (dbuf and dc2 % 2) else 0
            tk = None
            for e2 in range(8):
                tk = p.dma("sync", sb["gb"][:, base + e2 * 512:base + (e2 + 1) * 512],
                           xeimg2[e2, :, :, dc2 * 128:(dc2 + 1) * 128])
            return tk
        if tt > 0:
            p.wait("sync", smarks[7])
        stoks = [None] * 8
        smarks = [None] * 8
        if dbuf:
            stoks[0] = st_load(0)
        for dc in range(8):
            bank_g = dc % 2
            bank_s = 2 + dc % 2
            if dbuf:
                if dc + 1 < 8:
                    if dc >= 1:
                        p.wait("sync", smarks[dc - 1])
                    stoks[dc + 1] = st_load(dc + 1)
            else:
                if dc >= 1:
                    p.wait("sync", smarks[dc - 1])
                stoks[dc] = st_load(dc)
            base = 4096 if (dbuf and dc % 2) else 0
            p.wait("tensor", stoks[dc])
            for e in range(8):
                for sh in range(4):
                    srcb = sb["ga"] if e < 4 else sb["wb"][:, 8192:16384]
                    zi = (e * 4 + sh) % 16
                    TE(lambda t, srcb=srcb, zi=zi, bank_g=bank_g, e=e, sh=sh, base=base: t.matmul(
                        ps[bank_g],
                        sb["gb"][:, base + e * 512 + sh * 128:base + e * 512 + (sh + 1) * 128],
                        srcb.rearrange("q (z w) -> q z w", z=16)[:, zi, :],
                        start=(e == 0 and sh == 0), stop=(e == 7 and sh == 3)))
            smarks[dc] = p.mark("tensor")
            for kh in range(8):
                TE(lambda t, kh=kh, dc=dc, bank_s=bank_s: t.matmul(
                    ps[bank_s], wov[:, kh, dc * 128:(dc + 1) * 128],
                    sb["wc"][:, 0:4096].rearrange("q (kh w) -> q kh w", kh=8)[:, kh, :],
                    start=(kh == 0), stop=(kh == 7)))
            t_te = p.mark("tensor")
            p.wait("scalar", t_te)
            p.wait("vector", t_te)
            if "gmoe" in taps:
                AC(lambda s, bank_g=bank_g, dc=dc: s.copy(
                    sb["wc"].rearrange("q (z w) -> q z w", z=32)[:, 16 + dc, :],
                    ps[bank_g]))
            AC(lambda s, bank_g=bank_g: s.activation(
                sb["fa"][:, 0:512], ps[bank_g], AT.Silu))
            t_ac = p.mark("scalar")
            p.wait("vector", t_ac)
            VE(lambda v, dc=dc, bank_s=bank_s: v.tensor_tensor(
                sb["fa"][:, 512:1024], ps[bank_s],
                sb["wc"][:, 4096:8192].rearrange("q (kh w) -> q kh w", kh=8)[:, dc, :],
                OP.add))
            VE(lambda v: v.tensor_tensor(
                sb["fa"][:, 512:1024], sb["fa"][:, 512:1024], sb["fa"][:, 0:512],
                OP.mult))
            VE(lambda v, dc=dc: v.tensor_tensor(
                sb["fa"].rearrange("q (z w) -> q z w", z=16)[:, 8 + dc, :],
                fmv(sb["fb"])[:, dc, :], sb["fa"][:, 512:1024], OP.add))
            VE(lambda v, dc=dc: v.tensor_tensor(
                sb["fa"][:, 0:512],
                sb["fa"].rearrange("q (z w) -> q z w", z=16)[:, 8 + dc, :],
                sb["fa"].rearrange("q (z w) -> q z w", z=16)[:, 8 + dc, :], OP.mult))
            t_ve = p.mark("vector")
            p.wait("tensor", t_ve)
            TE(lambda t, dc=dc: t.matmul(
                ps[4][0:1, :], sb["ones_c"],
                sb["fa"].rearrange("q (z w) -> q z w", z=16)[:, 8 + dc, :],
                start=(dc == 0), stop=(dc == 7)))
            TE(lambda t, dc=dc: t.matmul(
                ps[5][0:1, :], sb["ones_c"], sb["fa"][:, 0:512],
                start=(dc == 0), stop=(dc == 7)))
            t_te = p.mark("tensor")
            p.wait("vector", t_te)
            p.wait("scalar", t_te)
        if "gmoe" in taps:
            t_g = p.mark("scalar")
            p.wait("sync", t_g)
            p.dma("sync",
                  A["tap_gmoe"].rearrange("q (dh t) -> q dh t", dh=8)[:, :, ts512],
                  sb["wc"].rearrange("q (z w) -> q z w", z=32)[:, 16:24, :])
        p.wait("sync", t_ve)
        tok_x1 = p.dma("sync", x1img[:, :, ts512], sb["fa"][:, 4096:8192])
        r1, r2, r3 = sb["row1"], sb["row2"], sb["row3"]
        VE(lambda v: v.tensor_scalar(r1, ps[4][0:1, :], 1.0 / D, None, OP.mult))
        VE(lambda v: v.tensor_scalar(r2, ps[5][0:1, :], 1.0 / D, None, OP.mult))
        VE(lambda v: v.tensor_tensor(r3, r1, r1, OP.mult))
        VE(lambda v: v.tensor_tensor(r2, r2, r3, OP.subtract))
        t_ve = p.mark("vector")
        p.wait("scalar", t_ve)
        AC(lambda s: s.activation(r3, r2, AT.Sqrt, bias=sb["eps_c"][0:1, :]))
        t_ac = p.mark("scalar")
        p.wait("vector", t_ac)
        VE(lambda v: v.reciprocal(r3, r3))
        VE(lambda v: v.tensor_tensor(r2, r2, r3, OP.mult))
        VE(lambda v: v.tensor_tensor(r2, r2, r3, OP.mult))
        t_ve = p.mark("vector")
        p.wait("scalar", t_ve)
        AC(lambda s: s.activation(r2, r2, AT.Sqrt, bias=sb["eps_c"][0:1, :]))
        t_ac = p.mark("scalar")
        p.wait("vector", t_ac)
        VE(lambda v: v.reciprocal(r2, r2))
        VE(lambda v: v.tensor_tensor(r2, r2, r3, OP.mult))
        t_ve = p.mark("vector")
        p.wait("tensor", t_ve)
        TE(lambda t: t.matmul(ps[6], sb["ones_r1f"][0:1, 0:128], r1, start=True, stop=True))
        TE(lambda t: t.matmul(ps[7], sb["ones_r1f"][0:1, 0:128], r3, start=True, stop=True))
        TE(lambda t: t.matmul(ps[4], sb["ones_r1f"][0:1, 0:128], r2, start=True, stop=True))
        t_te = p.mark("tensor")
        p.wait("vector", t_te)
        for dc in range(8):
            x1s = sb["fa"].rearrange("q (z w) -> q z w", z=16)[:, 8 + dc, :]
            VE(lambda v, x1s=x1s: v.tensor_tensor(sb["fa"][:, 0:512], x1s, ps[6], OP.subtract))
            VE(lambda v, dc=dc, ts512=ts512: v.tensor_tensor(
                fmv(sb["wa"])[:, dc, ts512], sb["fa"][:, 0:512], ps[7], OP.mult))
            VE(lambda v: v.tensor_tensor(sb["fa"][:, 512:1024], sb["fa"][:, 0:512], ps[4], OP.mult))
            VE(lambda v, dc=dc: v.tensor_scalar(
                sb["wc"].rearrange("q (z w) -> q z w", z=32)[:, dc, :],
                sb["fa"][:, 512:1024],
                sb["ln1_w"][:, dc:dc + 1], sb["ln1_b"][:, dc:dc + 1], OP.mult, OP.add))
        t_ve = p.mark("vector")
        p.wait("sync", t_ve)
        p.wait("tensor", t_ve)
        tok_xnn = p.dma("sync", xnnimg[:, :, ts512], sb["wc"][:, 0:4096])
        p.wait("vector", tok_xnn)
        p.wait("scalar", tok_xnn)
    p.barrier()
    tapout_d("x1", A["x1_fm"])
    tapout("xn1", sb["wa"])
    tapout_d("xnn1", A["xnn1_fm"])

    fab = sb["fa"].bitcast(BF16)
    fbb16 = sb["fb"].bitcast(BF16)
    for wi, wname in enumerate(["WrT", "WkT", "WvT"]):
        tok_w = p.dma("gpsimd", fab[:, 0:8192],
                      A[wname].rearrange("(kh q) o -> q kh o", q=128))
        p.wait("tensor", tok_w)
        wv = fab[:, 0:8192].rearrange("q (kh o) -> q kh o", kh=8)
        for tt in range(4):
            ts512 = slice(tt * 512, (tt + 1) * 512)
            tok_xnn = p.dma("sync", fbb16[:, 0:4096], xnnimg[:, :, ts512])
            p.wait("tensor", tok_xnn)
            for oc in range(8):
                bank = oc % 2
                for kh in range(8):
                    TE(lambda t, oc=oc, kh=kh, bank=bank: t.matmul(
                        ps[bank], wv[:, kh, oc * 128:(oc + 1) * 128],
                        fbb16[:, 0:4096].rearrange("q (kh w) -> q kh w", kh=8)[:, kh, :],
                        start=(kh == 0), stop=(kh == 7)))
                t_te = p.mark("tensor")
                p.wait("scalar", t_te)
                if wi == 0:
                    AC(lambda s, oc=oc, ts512=ts512, bank=bank: s.activation(
                        fmv(sb["wb"])[:, oc, ts512], ps[bank], AT.Sigmoid))
                elif wi == 1:
                    AC(lambda s, oc=oc, ts512=ts512, bank=bank: s.copy(
                        fmv(sb["wc"].bitcast(F16))[:, oc, ts512], ps[bank]))
                else:
                    AC(lambda s, oc=oc, ts512=ts512, bank=bank: s.copy(
                        (fmv(sb["ga"], 4)[:, oc, ts512] if oc < 4 else
                         fmv(sb["gb"], 4)[:, oc - 4, ts512]), ps[bank]))
                t_ac = p.mark("scalar")
                p.wait("tensor", t_ac)
            t_ac = p.mark("scalar")
            p.wait("sync", t_ac)
        t_mm = p.mark("tensor")
        p.wait("gpsimd", t_mm)
    p.barrier()
    tapout("rr", sb["wb"])
    tapout("kk_", sb["wc"].bitcast(F16))
    tapout("vva", sb["ga"])
    tapout("vvb", sb["gb"])

    VE(lambda v: v.memset(sb["fb"][:, 2048:2049], 0.0))
    VE(lambda v: v.memset(sb["fa"][:, 6144:6145], 0.0))
    for ch in range(8):
        vsl = (fmv(sb["ga"], 4)[:, ch, :] if ch < 4 else
               fmv(sb["gb"], 4)[:, ch - 4, :])
        ksl = fmv(sb["wc"].bitcast(F16))[:, ch, :]
        VE(lambda v: v.memset(sb["fa"][:, 6144:8192], 1.0))
        VE(lambda v, ch=ch: v.tensor_scalar(
            sb["fa"][:, 6144:8192], sb["fa"][:, 6144:8192],
            sb["ew"][:, ch:ch + 1], None, OP.mult))
        for dr in range(2):
            t_vp = p.mark("vector")
            p.wait("scalar", t_vp)
            AC(lambda s, ksl=ksl: s.activation(sb["fa"][:, 0:2048], ksl, AT.Exp))
            t_ac = p.mark("scalar")
            p.wait("vector", t_ac)
            VE(lambda v, vsl=vsl: v.tensor_tensor(
                sb["fa"][:, 2048:4096], sb["fa"][:, 0:2048], vsl, OP.mult))
            ewb = sb["fa"][:, 6144:8192]
            if dr == 0:
                nout = sb["fb"][:, 0:2048]
                dout = sb["fa"][:, 4096:6144]
                bnum = sb["fa"][:, 2048:4096]
                bden = sb["fa"][:, 0:2048]
            else:
                nout = bass.AP(sb["fb"].tensor, sb["fb"].offset + 2047,
                               [list(sb["fb"].ap[0]), [-1, 2048]])
                dout = bass.AP(sb["fa"].tensor, sb["fa"].offset + 4096 + 2047,
                               [list(sb["fa"].ap[0]), [-1, 2048]])
                bnum = bass.AP(sb["fa"].tensor, sb["fa"].offset + 2048 + 2047,
                               [list(sb["fa"].ap[0]), [-1, 2048]])
                bden = bass.AP(sb["fa"].tensor, sb["fa"].offset + 2047,
                               [list(sb["fa"].ap[0]), [-1, 2048]])
            if ch == 0 and dr == 0:
                tapout("p6a", sb["fa"])
            VE(lambda v, nout=nout, bnum=bnum, ewb=ewb: v.tensor_tensor_scan(
                nout, ewb, bnum, 0.0, OP.mult, OP.add))
            VE(lambda v, dout=dout, bden=bden, ewb=ewb: v.tensor_tensor_scan(
                dout, ewb, bden, 0.0, OP.mult, OP.add))
            if ch == 0 and dr == 0:
                tapout("p6b", sb["fb"])
                tapout("p6c", sb["fa"])
            eucol = sb["eu"][:, ch:ch + 1]
            VE(lambda v, eucol=eucol: v.tensor_scalar(
                sb["fa"][:, 2048:4096], sb["fa"][:, 2048:4096], eucol, None,
                OP.mult))
            VE(lambda v, eucol=eucol: v.tensor_scalar(
                sb["fa"][:, 0:2048], sb["fa"][:, 0:2048], eucol, None,
                OP.mult))
            if dr == 0:
                VE(lambda v: v.tensor_tensor(
                    sb["fa"][:, 2049:4096], sb["fa"][:, 2049:4096],
                    sb["fb"][:, 0:2047], OP.add))
                VE(lambda v: v.tensor_tensor(
                    sb["fa"][:, 1:2048], sb["fa"][:, 1:2048],
                    sb["fa"][:, 4096:6143], OP.add))
            else:
                VE(lambda v: v.tensor_tensor(
                    sb["fa"][:, 2048:4095], sb["fa"][:, 2048:4095],
                    sb["fb"][:, 1:2048], OP.add))
                VE(lambda v: v.tensor_tensor(
                    sb["fa"][:, 0:2047], sb["fa"][:, 0:2047],
                    sb["fa"][:, 4097:6144], OP.add))
            VE(lambda v: v.reciprocal(sb["fa"][:, 0:2048], sb["fa"][:, 0:2048]))
            if dr == 0:
                VE(lambda v: v.tensor_tensor(
                    sb["fb"][:, 2048:4096], sb["fa"][:, 2048:4096],
                    sb["fa"][:, 0:2048], OP.mult))
            else:
                VE(lambda v: v.tensor_tensor(
                    sb["fa"][:, 4096:6144], sb["fa"][:, 2048:4096],
                    sb["fa"][:, 0:2048], OP.mult))
        if ch == 0:
            tapout("p6d", sb["fb"])
            tapout("p6e", sb["fa"])
        VE(lambda v: v.tensor_tensor(
            sb["fa"][:, 4096:6144], sb["fa"][:, 4096:6144],
            sb["fb"][:, 2048:4096], OP.add))
        VE(lambda v: v.tensor_scalar(
            sb["fa"][:, 4096:6144], sb["fa"][:, 4096:6144], 0.5, None, OP.mult))
        VE(lambda v, ch=ch: v.tensor_tensor(
            fmv(sb["wb"])[:, ch, :], fmv(sb["wb"])[:, ch, :],
            sb["fa"][:, 4096:6144], OP.mult))
    t_ve = p.mark("vector")
    p.barrier()
    tapout("rw", sb["wb"])

    tok_wo2 = p.dma("gpsimd", fab[:, 0:8192],
                    A["WoT"].rearrange("(kh q) o -> q kh o", q=128))
    p.wait("tensor", tok_wo2)
    wov2 = fab[:, 0:8192].rearrange("q (kh o) -> q kh o", kh=8)
    xmimg = A["xmid_fm"].rearrange("q (dh t) -> q dh t", dh=8)
    for tt in range(4):
        ts512 = slice(tt * 512, (tt + 1) * 512)
        for dc in range(8):
            bank = dc % 2
            for kh in range(8):
                TE(lambda t, dc=dc, kh=kh, bank=bank, ts512=ts512: t.matmul(
                    ps[bank], wov2[:, kh, dc * 128:(dc + 1) * 128],
                    fmv(sb["wb"])[:, kh, ts512], start=(kh == 0), stop=(kh == 7)))
            t_te = p.mark("tensor")
            p.wait("vector", t_te)
            VE(lambda v, dc=dc, ts512=ts512, bank=bank: v.tensor_tensor(
                fmv(sb["fb"])[:, dc, :], ps[bank], fmv(sb["wa"])[:, dc, ts512],
                OP.add))
            VE(lambda v, dc=dc: v.tensor_tensor(
                sb["fa"][:, 4096:4608], fmv(sb["fb"])[:, dc, :],
                fmv(sb["fb"])[:, dc, :], OP.mult))
            t_ve = p.mark("vector")
            p.wait("tensor", t_ve)
            TE(lambda t, dc=dc: t.matmul(
                ps[4][0:1, :], sb["ones_c"], fmv(sb["fb"])[:, dc, :],
                start=(dc == 0), stop=(dc == 7)))
            TE(lambda t, dc=dc: t.matmul(
                ps[5][0:1, :], sb["ones_c"], sb["fa"][:, 4096:4608],
                start=(dc == 0), stop=(dc == 7)))
            t_te = p.mark("tensor")
            p.wait("vector", t_te)
            p.wait("scalar", t_te)
        p.wait("sync", t_ve)
        tok_xm = p.dma("sync", xmimg[:, :, ts512], sb["fb"])
        p.wait("vector", tok_xm)
        r1, r2, r3 = sb["row1"], sb["row2"], sb["row3"]
        VE(lambda v: v.tensor_scalar(r1, ps[4][0:1, :], 1.0 / D, None, OP.mult))
        VE(lambda v: v.tensor_scalar(r2, ps[5][0:1, :], 1.0 / D, None, OP.mult))
        VE(lambda v: v.tensor_tensor(r3, r1, r1, OP.mult))
        VE(lambda v: v.tensor_tensor(r2, r2, r3, OP.subtract))
        t_ve = p.mark("vector")
        p.wait("scalar", t_ve)
        AC(lambda s: s.activation(r3, r2, AT.Sqrt, bias=sb["eps_c"][0:1, :]))
        t_ac = p.mark("scalar")
        p.wait("vector", t_ac)
        VE(lambda v: v.reciprocal(r3, r3))
        t_ve = p.mark("vector")
        p.wait("tensor", t_ve)
        TE(lambda t: t.matmul(ps[6], sb["ones_r1f"][0:1, 0:128], r1, start=True, stop=True))
        TE(lambda t: t.matmul(ps[7], sb["ones_r1f"][0:1, 0:128], r3, start=True, stop=True))
        t_te = p.mark("tensor")
        p.wait("vector", t_te)
        for dc in range(8):
            VE(lambda v, dc=dc: v.tensor_tensor(
                sb["fa"][:, 4096:4608], fmv(sb["fb"])[:, dc, :], ps[6], OP.subtract))
            VE(lambda v: v.tensor_tensor(
                sb["fa"][:, 4608:5120], sb["fa"][:, 4096:4608], ps[7], OP.mult))
            VE(lambda v, dc=dc, ts512=ts512: v.tensor_scalar(
                fmv(sb["wb"])[:, dc, ts512], sb["fa"][:, 4608:5120],
                sb["ln2_w"][:, dc:dc + 1], sb["ln2_b"][:, dc:dc + 1],
                OP.mult, OP.add))
        t_ve = p.mark("vector")
        p.wait("tensor", t_ve)
        p.wait("sync", t_ve)
    p.barrier()
    tapout_d("xmid", A["xmid_fm"])
    tapout("xn2", sb["wb"])

    for wname, dstname, nch, cw in (("WfcT", "WqfcT_d", 8, 4096),
                                    ("WprojT", "WqprojT_d", 32, 1024)):
        VE(lambda v: v.memset(sb["fa"][:, 7168:7169], 0.0))
        t_ve = p.mark("vector")
        p.wait("sync", t_ve)
        wimg = A[wname].rearrange("(n q) o -> n q o", q=128)
        prev = None
        for n in range(nch):
            tok = p.dma("sync", sb["fa"][:, 0:cw], wimg[n])
            p.wait("vector", tok)
            VE(lambda v, cw=cw: v.tensor_reduce(
                sb["fa"][:, 7169:7170], sb["fa"][:, 0:cw], mybir.AxisListType.X,
                OP.max, apply_absolute_value=True))
            VE(lambda v: v.tensor_tensor(
                sb["fa"][:, 7168:7169], sb["fa"][:, 7168:7169],
                sb["fa"][:, 7169:7170], OP.max))
            prev = p.mark("vector")
            p.wait("sync", prev)
        p.wait("tensor", prev)
        TE(lambda t: t.transpose(ps[6][0:1, 0:128], sb["fa"][:, 7168:7169],
                                 sb["eyef"]))
        t_te = p.mark("tensor")
        p.wait("vector", t_te)
        VE(lambda v: v.tensor_reduce(
            sb["fa"][0:1, 7170:7171], ps[6][0:1, 0:128], mybir.AxisListType.X,
            OP.max))
        VE(lambda v: v.tensor_scalar(
            sb["fa"][0:1, 7171:7172], sb["fa"][0:1, 7170:7171], 1.0 / 127.0,
            None, OP.mult))
        VE(lambda v: v.reciprocal(sb["fa"][0:1, 7172:7173], sb["fa"][0:1, 7171:7172]))
        t_ve = p.mark("vector")
        p.wait("tensor", t_ve)
        TE(lambda t: t.matmul(ps[6][:, 0:2], sb["ones_r1f"][0:1, 0:128],
                              sb["fa"][0:1, 7171:7173], start=True, stop=True))
        t_te = p.mark("tensor")
        p.wait("vector", t_te)
        VE(lambda v: v.tensor_copy(sb["fa"][:, 7173:7175], ps[6][:, 0:2]))
        p.selfsync("vector")
        for n in range(nch):
            tok = p.dma("sync", sb["fa"][:, 0:cw], wimg[n])
            p.wait("vector", tok)
            VE(lambda v, cw=cw: v.tensor_scalar(
                sb["fa"][:, 0:cw], sb["fa"][:, 0:cw],
                sb["fa"][:, 7174:7175], RNE_C, OP.mult, OP.add))
            VE(lambda v, cw=cw: v.tensor_scalar(
                sb["fa"][:, 0:cw], sb["fa"][:, 0:cw], -RNE_C, None, OP.add))
            VE(lambda v, cw=cw: v.tensor_scalar(
                sb["fa"][:, 0:cw], sb["fa"][:, 0:cw], 127.0, -128.0,
                OP.min, OP.max))
            VE(lambda v, cw=cw: v.tensor_scalar(
                sb["fb"].bitcast(BF16)[:, 0:cw], sb["fa"][:, 0:cw],
                sb["fa"][:, 7173:7174], None, OP.mult))
            t_ve = p.mark("vector")
            p.wait("sync", t_ve)
            tok2 = p.dma("sync",
                         A[dstname].rearrange("(n q) o -> n q o", q=128)[n],
                         sb["fb"].bitcast(BF16)[:, 0:cw])
            p.wait("vector", tok2)
    p.barrier()

    kkimg = A["kk_fm"].rearrange("q (ic t) -> q ic t", ic=32)
    hfimg = A["hf_fm"].rearrange("q (ic t) -> q ic t", ic=32)
    fktoks = [None] * 32
    fkmarks = [None] * 32

    def fk_load(ic2):
        return p.dma("gpsimd", fab[:, (ic2 % 2) * 1024:(ic2 % 2) * 1024 + 1024],
                     A["WfkT"].rearrange("(kh q) o -> q kh o", q=128)[:, :, ic2 * 128:(ic2 + 1) * 128])
    fktoks[0] = fk_load(0)
    for ic in range(32):
        slot = (ic % 2) * 1024
        if ic + 1 < 32:
            if ic >= 1:
                p.wait("gpsimd", fkmarks[ic - 1])
            fktoks[ic + 1] = fk_load(ic + 1)
        p.wait("tensor", fktoks[ic])
        for tt in range(4):
            ts512 = slice(tt * 512, (tt + 1) * 512)
            bank = tt % 2
            for kh in range(8):
                TE(lambda t, kh=kh, bank=bank, slot=slot, ts512=ts512: t.matmul(
                    ps[bank],
                    fab[:, slot:slot + 1024].rearrange("q (kh w) -> q kh w", kh=8)[:, kh, :],
                    fmv(sb["wb"])[:, kh, ts512], start=(kh == 0), stop=(kh == 7)))
            t_te = p.mark("tensor")
            p.wait("vector", t_te)
            VE(lambda v, bank=bank: v.tensor_scalar(
                sb["fa"][:, 2048:2560], ps[bank], 0.0, None, OP.max))
            VE(lambda v, bank=bank: v.tensor_tensor(
                fbb16[:, 0:512], sb["fa"][:, 2048:2560], ps[bank], OP.mult))
            t_ve = p.mark("vector")
            p.wait("tensor", t_ve)
            p.wait("sync", t_ve)
            tok_kk = p.dma("sync", kkimg[:, ic, ts512], fbb16[:, 0:512])
            p.wait("vector", tok_kk)
        fkmarks[ic] = p.mark("tensor")
    p.barrier()

    tok_afc = p.dma("gpsimd", fab[:, 0:128],
                    A["A_fcT"].rearrange("(kh q) o -> q kh o", q=128))
    tok_bfc = p.dma("gpsimd", sb["gb"][32:48, 0:4096], A["B_fcT"])
    p.wait("tensor", tok_bfc)
    for tt in range(4):
        ts512 = slice(tt * 512, (tt + 1) * 512)
        for kh in range(8):
            TE(lambda t, kh=kh, ts512=ts512: t.matmul(
                ps[2][32:48, :],
                fab[:, 0:128].rearrange("q (kh w) -> q kh w", kh=8)[:, kh, :],
                fmv(sb["wa"])[:, kh, ts512], start=(kh == 0), stop=(kh == 7)))
        t_te = p.mark("tensor")
        p.wait("vector", t_te)
        VE(lambda v, ts512=ts512: v.tensor_scalar(
            sb["gb"][32:48, 4096:6144][:, ts512], ps[2][32:48, :], 2.0, None, OP.mult))
        t_ve = p.mark("vector")
        p.wait("tensor", t_ve)
    fctoks = [None] * 32
    fcmarks = [None] * 32

    def fc_load(ic2):
        return p.dma("gpsimd", fab[:, (ic2 % 2) * 1024:(ic2 % 2) * 1024 + 1024],
                     A["WqfcT_d"].rearrange("(kh q) o -> q kh o", q=128)[:, :, ic2 * 128:(ic2 + 1) * 128])
    fctoks[0] = fc_load(0)
    for ic in range(32):
        slot = (ic % 2) * 1024
        if ic + 1 < 32:
            if ic >= 1:
                p.wait("gpsimd", fcmarks[ic - 1])
            fctoks[ic + 1] = fc_load(ic + 1)
        p.wait("tensor", fctoks[ic])
        for tt in range(4):
            ts512 = slice(tt * 512, (tt + 1) * 512)
            bank = tt % 2
            for kh in range(8):
                TE(lambda t, kh=kh, bank=bank, slot=slot, ts512=ts512: t.matmul(
                    ps[bank],
                    fab[:, slot:slot + 1024].rearrange("q (kh w) -> q kh w", kh=8)[:, kh, :],
                    fmv(sb["wa"])[:, kh, ts512], start=(kh == 0), stop=False))
            TE(lambda t, ic=ic, ts512=ts512, bank=bank: t.matmul(
                ps[bank], sb["gb"][32:48, ic * 128:(ic + 1) * 128],
                sb["gb"][32:48, 4096:6144][:, ts512], start=False, stop=True))
            t_te = p.mark("tensor")
            p.wait("scalar", t_te)
            AC(lambda s, ic=ic, bank=bank: s.activation(
                fbb16[:, 0:512], ps[bank], AT.Gelu,
                bias=sb["b_fc_fm"][:, ic:ic + 1]))
            t_ac = p.mark("scalar")
            p.wait("tensor", t_ac)
            p.wait("sync", t_ac)
            tok_h = p.dma("sync", hfimg[:, ic, ts512], fbb16[:, 0:512])
            p.wait("scalar", tok_h)
        fcmarks[ic] = p.mark("tensor")
    p.barrier()

    tok_fr = p.dma("gpsimd", sb["ga"][:, 0:8192],
                   A["WfrT"].rearrange("(kh q) o -> q kh o", q=128))
    tok_bpj = p.dma("gpsimd", sb["gb"][32:48, 4096:5120], A["B_projT"])
    wcf32 = sb["wc"].bitcast(F32)
    outimg = A["out_fm"].rearrange("q (dh t) -> q dh t", dh=8)
    frv = sb["ga"][:, 0:8192].rearrange("q (kh o) -> q kh o", kh=8)
    p.wait("tensor", tok_bpj)
    for tt in range(4):
        ts512 = slice(tt * 512, (tt + 1) * 512)
        if tt > 0:
            p.wait("gpsimd", pjm[31])
            p.wait("sync", pjm[31])
        kvt = [None] * 32
        kvm = [None] * 32

        def kv_load(ih2):
            tg = p.dma("gpsimd", fbb16[:, (ih2 % 2) * 1024:(ih2 % 2) * 1024 + 1024],
                  A["WfvT"].rearrange("(ih q) o -> q ih o", q=128, ih=32)[:, ih2, :])
            return tg, p.dma("sync", fbb16[:, 2048 + (ih2 % 2) * 512:2048 + (ih2 % 2) * 512 + 512],
                         kkimg[:, ih2, ts512])
        kvt[0] = kv_load(0)
        for ih in range(32):
            wslot = (ih % 2) * 1024
            kslot = 2048 + (ih % 2) * 512
            if ih + 1 < 32:
                if ih >= 1:
                    p.wait("gpsimd", kvm[ih - 1])
                    p.wait("sync", kvm[ih - 1])
                kvt[ih + 1] = kv_load(ih + 1)
            p.wait("tensor", kvt[ih][0])
            p.wait("tensor", kvt[ih][1])
            for dc in range(8):
                TE(lambda t, ih=ih, dc=dc, wslot=wslot, kslot=kslot: t.matmul(
                    ps[dc],
                    fbb16[:, wslot:wslot + 1024].rearrange("q (dc w) -> q dc w", dc=8)[:, dc, :],
                    fbb16[:, kslot:kslot + 512],
                    start=(ih == 0), stop=(ih == 31)))
            kvm[ih] = p.mark("tensor")
        p.wait("gpsimd", kvm[31])
        p.wait("sync", kvm[31])
        t_te = p.mark("tensor")
        p.wait("vector", t_te)
        for dc in range(8):
            VE(lambda v, dc=dc: v.tensor_copy(
                sb["fa"].rearrange("q (z w) -> q z w", z=16)[:, dc, :], ps[dc]))
        t_ve = p.mark("vector")
        p.wait("tensor", t_ve)
        for dc in range(8):
            bank = dc % 2
            for kh in range(8):
                TE(lambda t, dc=dc, kh=kh, bank=bank, ts512=ts512: t.matmul(
                    ps[bank], frv[:, kh, dc * 128:(dc + 1) * 128],
                    fmv(sb["wb"])[:, kh, ts512], start=(kh == 0), stop=(kh == 7)))
            t_te = p.mark("tensor")
            p.wait("scalar", t_te)
            AC(lambda s, dc=dc, bank=bank: s.activation(
                sb["wc"].rearrange("q (z w) -> q z w", z=32)[:, 16 + dc, :],
                ps[bank], AT.Sigmoid))
            t_ac = p.mark("scalar")
            p.wait("tensor", t_ac)
        l2t = [None] * 32
        l2m = [None] * 32

        def l2_load(ih2):
            tg = p.dma("gpsimd", fbb16[:, (ih2 % 2) * 1024:(ih2 % 2) * 1024 + 16],
                  A["A_projT"].rearrange("(ih q) o -> q ih o", q=128, ih=32)[:, ih2, :])
            return tg, p.dma("sync", fbb16[:, 2048 + (ih2 % 2) * 512:2048 + (ih2 % 2) * 512 + 512],
                         hfimg[:, ih2, ts512])
        l2t[0] = l2_load(0)
        for ih in range(32):
            wslot = (ih % 2) * 1024
            kslot = 2048 + (ih % 2) * 512
            if ih + 1 < 32:
                if ih >= 1:
                    p.wait("gpsimd", l2m[ih - 1])
                    p.wait("sync", l2m[ih - 1])
                l2t[ih + 1] = l2_load(ih + 1)
            p.wait("tensor", l2t[ih][0])
            p.wait("tensor", l2t[ih][1])
            TE(lambda t, ih=ih, wslot=wslot, kslot=kslot: t.matmul(
                ps[2][32:48, :], fbb16[:, wslot:wslot + 16],
                fbb16[:, kslot:kslot + 512], start=(ih == 0), stop=(ih == 31)))
            l2m[ih] = p.mark("tensor")
        p.wait("gpsimd", l2m[31])
        p.wait("sync", l2m[31])
        t_te = p.mark("tensor")
        p.wait("vector", t_te)
        VE(lambda v: v.tensor_scalar(
            sb["gb"][32:48, 6144:6656], ps[2][32:48, :], 2.0, None, OP.mult))
        t_ve = p.mark("vector")
        p.wait("tensor", t_ve)
        p.wait("gpsimd", t_ve)
        pjt = [None] * 32
        pjm = [None] * 32

        def pj_load(ih2):
            tg = p.dma("gpsimd", fbb16[:, (ih2 % 2) * 1024:(ih2 % 2) * 1024 + 1024],
                  A["WqprojT_d"].rearrange("(ih q) o -> q ih o", q=128, ih=32)[:, ih2, :])
            return tg, p.dma("sync", fbb16[:, 2048 + (ih2 % 2) * 512:2048 + (ih2 % 2) * 512 + 512],
                         hfimg[:, ih2, ts512])
        pjt[0] = pj_load(0)
        for ih in range(32):
            wslot = (ih % 2) * 1024
            kslot = 2048 + (ih % 2) * 512
            if ih + 1 < 32:
                if ih >= 1:
                    p.wait("gpsimd", pjm[ih - 1])
                    p.wait("sync", pjm[ih - 1])
                pjt[ih + 1] = pj_load(ih + 1)
            p.wait("tensor", pjt[ih][0])
            p.wait("tensor", pjt[ih][1])
            for dc in range(8):
                TE(lambda t, ih=ih, dc=dc, wslot=wslot, kslot=kslot: t.matmul(
                    ps[dc],
                    fbb16[:, wslot:wslot + 1024].rearrange("q (dc w) -> q dc w", dc=8)[:, dc, :],
                    fbb16[:, kslot:kslot + 512],
                    start=(ih == 0), stop=False))
            pjm[ih] = p.mark("tensor")
        for dc in range(8):
            TE(lambda t, dc=dc: t.matmul(
                ps[dc], sb["gb"][32:48, 4096 + dc * 128:4096 + (dc + 1) * 128],
                sb["gb"][32:48, 6144:6656], start=False, stop=True))
        t_te = p.mark("tensor")
        p.wait("scalar", t_te)
        tok_xm2 = p.dma("sync", sb["fa"][:, 4096:8192], xmimg[:, :, ts512])
        tok_x12 = p.dma("sync", wcf32[:, 0:4096], x1img[:, :, ts512])
        p.wait("scalar", tok_xm2)
        p.wait("vector", tok_x12)
        for dc in range(8):
            AC(lambda s, dc=dc: s.activation(
                fbb16[:, 3072 + dc * 512:3072 + (dc + 1) * 512], ps[dc], AT.Silu,
                bias=sb["b_proj"][:, dc:dc + 1]))
        t_ac = p.mark("scalar")
        p.wait("vector", t_ac)
        for dc in range(8):
            VE(lambda v, dc=dc: v.tensor_tensor(
                sb["fa"][:, 0:4096].rearrange("q (z w) -> q z w", z=8)[:, dc, :],
                sb["fa"][:, 0:4096].rearrange("q (z w) -> q z w", z=8)[:, dc, :],
                sb["wc"].rearrange("q (z w) -> q z w", z=32)[:, 16 + dc, :],
                OP.mult))
            VE(lambda v, dc=dc: v.tensor_tensor(
                sb["fa"][:, 0:4096].rearrange("q (z w) -> q z w", z=8)[:, dc, :],
                sb["fa"][:, 0:4096].rearrange("q (z w) -> q z w", z=8)[:, dc, :],
                sb["fa"][:, 4096:8192].rearrange("q (z w) -> q z w", z=8)[:, dc, :],
                OP.add))
            VE(lambda v, dc=dc: v.tensor_tensor(
                sb["fa"][:, 0:4096].rearrange("q (z w) -> q z w", z=8)[:, dc, :],
                sb["fa"][:, 0:4096].rearrange("q (z w) -> q z w", z=8)[:, dc, :],
                fbb16[:, 3072 + dc * 512:3072 + (dc + 1) * 512], OP.mult))
            VE(lambda v, dc=dc: v.tensor_tensor(
                wcf32[:, 0:4096].rearrange("q (z w) -> q z w", z=8)[:, dc, :],
                wcf32[:, 0:4096].rearrange("q (z w) -> q z w", z=8)[:, dc, :],
                sb["fa"][:, 0:4096].rearrange("q (z w) -> q z w", z=8)[:, dc, :],
                OP.add))
        t_ve = p.mark("vector")
        p.wait("sync", t_ve)
        p.wait("tensor", t_ve)
        tok_out = p.dma("sync", outimg[:, :, ts512], wcf32[:, 0:4096])
        p.wait("vector", tok_out)
        p.wait("scalar", tok_out)
    p.barrier()
    return sb, ps


TAP_SHAPES = {
    "A_cm": ([128, 16], F32),
    "xn_bf": ([128, 8 * T], BF16),
    "xnn_bf": ([128, 8 * T], BF16),
    "x_main": ([128, 8 * T], BF16),
    "siluz": ([128, 8 * T], BF16),
    "x_conv": ([128, 8 * T], BF16),
    "dt": ([128, 8 * T], F16),
    "Bm": ([64, T], BF16),
    "Cm": ([64, T], BF16),
    "aff": ([8, T], F32),
    "spre": ([128, 8 * T], BF16),
    "x1": ([128, 8 * T], F32),
    "xn1": ([128, 8 * T], BF16),
    "xnn1": ([128, 8 * T], BF16),
    "rr": ([128, 8 * T], BF16),
    "kk_": ([128, 8 * T], F16),
    "rw": ([128, 8 * T], BF16),
    "xmid": ([128, 8 * T], F32),
    "xn2": ([128, 8 * T], BF16),
    "slotx": ([8, T], F32),
    "thr": ([8, 1], F32),
    "xin0": ([128, 8 * 512], BF16),
    "xe0": ([128, 4 * 1024], BF16),
    "gmoe": ([128, 8 * T], BF16),
    "slotcol": ([128, 128], F32),
    "xntm": ([T, D], BF16),
    "hi0": ([128, 16384], BF16),
    "p2g": ([128, 8192], BF16),
    "affcol": ([128, 128], F32),
    "vva": ([128, 8192], BF16),
    "vvb": ([128, 8192], BF16),
    "p6a": ([128, 8192], F32),
    "p6b": ([128, 4096], F32),
    "p6c": ([128, 8192], F32),
    "p6d": ([128, 4096], F32),
    "p6e": ([128, 8192], F32),
}


def build_nc(taps=()):
    nc = bass.Bass("TRN2", target_bir_lowering=False, debug=False)
    A = {}

    def inp(name, shape, dtype=F32):
        A[name] = nc.dram_tensor(name, shape, dtype, kind="ExternalInput").ap()

    def scratch(name, shape, dtype):
        A[name] = nc.dram_tensor(name, shape, dtype, kind="Internal").ap()

    def outp(name, shape, dtype=F32):
        A[name] = nc.dram_tensor(name, shape, dtype, kind="ExternalOutput").ap()

    inp("xT", [D, T])
    inp("WinT", [D, 2 * D])
    inp("conv_wT", [4, D])
    inp("WdtT", [D, D])
    inp("WBT", [D, NST])
    inp("WCT", [D, NST])
    inp("WoutT", [D, D])
    inp("gate_wT", [D, NE])
    inp("wgT", [NE * D, FI])
    inp("wuT", [NE * D, FI])
    inp("wdT", [NE * FI, D])
    inp("WrT", [D, D])
    inp("WkT", [D, D])
    inp("WvT", [D, D])
    inp("WoT", [D, D])
    inp("WfkT", [D, FI])
    inp("WfvT", [FI, D])
    inp("WfrT", [D, D])
    inp("WfcT", [D, FI])
    inp("WprojT", [FI, D])
    inp("A_fcT", [D, 16])
    inp("B_fcT", [16, FI])
    inp("A_projT", [FI, 16])
    inp("B_projT", [16, D])
    for n in ["ln_g", "ln_b", "conv_b", "Dp", "b_dt", "ln1_w", "ln1_b",
              "ln2_w", "ln2_b", "decay_v", "u_v", "b_proj"]:
        inp(n, [1, D])
    inp("b_fc", [1, FI])
    inp("A_log", [1, NST])
    inp("S1c", [NST, 128])
    inp("S2c", [NST, 16])
    inp("Ssum", [128, 32], BF16)
    inp("Ssel3", [NST, 2048], BF16)
    inp("eye", [128, 128], BF16)
    inp("eyef", [128, 128])

    scratch("xn_fm", [128, 8 * T], BF16)
    scratch("xn_tm", [T, D], BF16)
    scratch("siluz_fm", [128, 8 * T], BF16)
    scratch("xconv_fm", [128, 8 * T], BF16)
    scratch("dt_fm", [128, 8 * T], F16)
    scratch("spre_fm", [128, 8 * T], BF16)
    scratch("x1_fm", [128, 8 * T], F32)
    scratch("xmid_fm", [128, 8 * T], F32)
    scratch("xnn1_fm", [128, 8 * T], BF16)
    scratch("kk_fm", [128, 32 * 512 * 4], BF16)
    scratch("hf_fm", [128, 32 * 512 * 4], BF16)
    scratch("xe_tm", [NE * 128, 4 * D], BF16)
    scratch("WqfcT_d", [D, FI], BF16)
    scratch("WqprojT_d", [FI, D], BF16)
    scratch("echo_d", [128, 16], F32)

    outp("out_fm", [128, 8 * T])
    for name in taps:
        shape, dt_ = TAP_SHAPES[name]
        outp("tap_" + name, shape, dt_)

    p = P(nc)
    with contextlib.ExitStack() as stack:
        p.make_sems(stack)
        blk = stack.enter_context(nc.Block())
        emit_program(p, A, set(taps))
        for qeng, sname in (("sync", "d_sy"), ("gpsimd", "d_gp"), ("scalar", "d_ac")):
            h = p.semh[sname]
            tv = p.cnt[sname]
            if tv:
                p.em(qeng, lambda e, h=h, tv=tv: e.wait_ge(h, tv))
        p.finalize_waits()

        @blk.sync
        def _(sync):
            for fn in p.ops["sync"]:
                fn(sync)

        @blk.gpsimd
        def _(g):
            for fn in p.ops["gpsimd"]:
                fn(g)

        @blk.vector
        def _(v):
            for fn in p.ops["vector"]:
                fn(v)

        @blk.scalar
        def _(s):
            for fn in p.ops["scalar"]:
                fn(s)

        @blk.tensor
        def _(t):
            for fn in p.ops["tensor"]:
                fn(t)

    return nc


def prep_inputs(b, kw):
    d = {}
    ct = np.ascontiguousarray
    d["xT"] = ct(kw["L0_attn_x"][b].T)
    d["WinT"] = ct(kw["L0_attn_W_in"].T)
    d["conv_wT"] = ct(kw["L0_attn_conv_w"][:, 0, :].T)
    d["WdtT"] = ct(kw["L0_attn_W_dt"].T)
    d["WBT"] = ct(kw["L0_attn_W_B"].T)
    d["WCT"] = ct(kw["L0_attn_W_C"].T)
    d["WoutT"] = ct(kw["L0_attn_W_out"].T)
    d["gate_wT"] = ct(kw["L0_ffn_gate_w"].T)
    d["wgT"] = ct(kw["L0_ffn_w_gate"].transpose(0, 2, 1)).reshape(NE * D, FI)
    d["wuT"] = ct(kw["L0_ffn_w_up"].transpose(0, 2, 1)).reshape(NE * D, FI)
    d["wdT"] = ct(kw["L0_ffn_w_down"].transpose(0, 2, 1)).reshape(NE * FI, D)
    d["WrT"] = ct(kw["L1_attn_Wr"].T)
    d["WkT"] = ct(kw["L1_attn_Wk"].T)
    d["WvT"] = ct(kw["L1_attn_Wv"].T)
    d["WoT"] = ct(kw["L1_attn_Wo"].T)
    d["WfkT"] = ct(kw["L1_attn_Wfk"].T)
    d["WfvT"] = ct(kw["L1_attn_Wfv"].T)
    d["WfrT"] = ct(kw["L1_attn_Wfr"].T)
    d["WfcT"] = ct(kw["L1_ffn_W_fc"].T)
    d["WprojT"] = ct(kw["L1_ffn_W_proj"].T)
    d["A_fcT"] = ct(kw["L1_ffn_A_fc"].T)
    d["B_fcT"] = ct(kw["L1_ffn_B_fc"].T)
    d["A_projT"] = ct(kw["L1_ffn_A_proj"].T)
    d["B_projT"] = ct(kw["L1_ffn_B_proj"].T)
    d["ln_g"] = kw["L0_attn_ln_g"].reshape(1, D)
    d["ln_b"] = kw["L0_attn_ln_b"].reshape(1, D)
    d["conv_b"] = kw["L0_attn_conv_b"].reshape(1, D)
    d["Dp"] = kw["L0_attn_Dp"].reshape(1, D)
    d["b_dt"] = kw["L0_attn_b_dt"].reshape(1, D)
    d["ln1_w"] = kw["L1_attn_ln1_w"].reshape(1, D)
    d["ln1_b"] = kw["L1_attn_ln1_b"].reshape(1, D)
    d["ln2_w"] = kw["L1_attn_ln2_w"].reshape(1, D)
    d["ln2_b"] = kw["L1_attn_ln2_b"].reshape(1, D)
    d["decay_v"] = kw["L1_attn_decay"].reshape(1, D)
    d["u_v"] = kw["L1_attn_u"].reshape(1, D)
    d["b_proj"] = kw["L1_ffn_b_proj"].reshape(1, D)
    d["b_fc"] = kw["L1_ffn_b_fc"].reshape(1, FI)
    d["A_log"] = kw["L0_attn_A_log"].reshape(1, NST)
    n_ = np.arange(NST)
    pp = np.arange(128)
    d["S1c"] = (n_[:, None] % 4 == pp[None, :] // 32).astype(np.float32)
    d["S2c"] = (n_[:, None] // 4 == np.arange(16)[None, :]).astype(np.float32)
    import ml_dtypes
    d["Ssum"] = (pp[:, None] % 32 == np.arange(32)[None, :]).astype(ml_dtypes.bfloat16)
    sel3 = np.zeros((NST, 16, 128), np.float32)
    for q in range(16):
        sel3[:, q, :] = (n_[:, None] == q * 4 + pp[None, :] // 32)
    d["Ssel3"] = sel3.reshape(NST, 2048).astype(ml_dtypes.bfloat16)
    d["eye"] = np.eye(128).astype(ml_dtypes.bfloat16)
    d["eyef"] = np.eye(128).astype(np.float32)
    return d


def kernel(**kw):
    nc = build_nc(taps=DEBUG_TAPS)
    in_maps = [prep_inputs(b, kw) for b in range(8)]
    res = run_bass_kernel_spmd(nc, in_maps, core_ids=list(range(8)))
    outs = []
    for b in range(8):
        o = res.results[b]["out_fm"]
        outs.append(np.asarray(o).reshape(128, 8, T).transpose(2, 1, 0).reshape(T, D))
    return np.stack(outs, axis=0)


# version 04
# speedup vs baseline: 1.0312x; 1.0098x over previous
import contextlib
import numpy as np
import concourse.bass as bass
import concourse.mybir as mybir
from concourse.bass_utils import run_bass_kernel_spmd

F32 = mybir.dt.float32
F16 = mybir.dt.float16
BF16 = mybir.dt.bfloat16
AT = mybir.ActivationFunctionType
OP = mybir.AluOpType

T = 2048
D = 1024
NST = 64
NE = 8
FI = 4096
RNE_C = 1.5 * 2.0 ** 23
DEBUG_TAPS = ()


class P:

    ENGS = ("sync", "gpsimd", "vector", "scalar", "tensor")

    def __init__(self, nc):
        self.nc = nc
        self.ops = {k: [] for k in self.ENGS}
        self.semh = {}
        self.cnt = {}
        for name in ("s_sy", "s_gp", "s_ve", "s_ac", "s_te",
                     "d_sy", "d_gp", "d_ac", "d_gt", "d_st"):
            self.cnt[name] = 0
        self.echo_sb = {}
        self.echo_dr = None

    def make_sems(self, stack):
        for name in list(self.cnt):
            self.semh[name] = stack.enter_context(self.nc.semaphore(name))

    def em(self, eng, fn, is_dma=False):
        self.ops[eng].append(fn)
        self._last_dma = getattr(self, "_last_dma", {})
        self._last_dma[eng] = is_dma
        self._last_marked = getattr(self, "_last_marked", {})
        self._last_marked[eng] = False

    def inc(self, name, amt=1):
        self.cnt[name] += amt
        return self.cnt[name]

    def mark(self, eng):
        sname = {"sync": "s_sy", "gpsimd": "s_gp", "vector": "s_ve",
                 "scalar": "s_ac", "tensor": "s_te"}[eng]
        tv = self.inc(sname)
        h = self.semh[sname]
        already = getattr(self, "_last_marked", {}).get(eng, False)
        self._last_marked = getattr(self, "_last_marked", {})
        last_dma = getattr(self, "_last_dma", {}).get(eng, False)
        if self.ops[eng] and not already and not last_dma:
            prev = self.ops[eng].pop()
            self.em(eng, lambda e, h=h, prev=prev: prev(e).then_inc(h, 1))
        elif last_dma or not self.ops[eng]:
            self.em(eng, lambda e, h=h: e.nop().then_inc(h, 1))
        else:
            self.em(eng, lambda e, h=h: e.drain().then_inc(h, 1))
        self._last_marked[eng] = True
        return sname, tv

    SEM_ENG = {"s_sy": "sync", "s_gp": "gpsimd", "s_ve": "vector",
               "s_ac": "scalar", "s_te": "tensor",
               "d_sy": "sync", "d_gp": "gpsimd", "d_ac": "scalar"}

    def wait(self, eng, tok):
        sname, tv = tok
        h = self.semh[sname]
        box = [max(tv, self.cnt[sname])]
        self.pending = getattr(self, "pending", {})
        self.pending.setdefault(self.SEM_ENG[sname], []).append((sname, box))
        self.em(eng, lambda e, h=h, box=box: e.wait_ge(h, box[0]))
        self._flush(eng)

    def _flush(self, eng):
        self.pending = getattr(self, "pending", {})
        for sname, box in self.pending.pop(eng, []):
            box[0] = max(box[0], self.cnt[sname])

    def finalize_waits(self):
        self.pending = getattr(self, "pending", {})
        for eng in list(self.pending):
            self._flush(eng)

    def selfsync(self, eng):
        self.wait(eng, self.mark(eng))

    def barrier(self, engs=None):
        engs = engs or self.ENGS
        for _phase in range(2):
            toks = {eng: self.mark(eng) for eng in engs}
            for eng in engs:
                for other, tok in toks.items():
                    if other != eng:
                        self.wait(eng, tok)

    ECHO_THRESH = 600_000

    def dma(self, qeng, dst, src, **kw):
        sname = {"sync": "d_sy", "gpsimd": "d_gp", "scalar": "d_ac"}[qeng]
        h = self.semh[sname]

        def nbytes(ap):
            n = 1
            for s_ in ap.shape:
                n *= s_
            return n * mybir.dt.size(ap.dtype)

        if max(nbytes(dst), nbytes(src)) > self.ECHO_THRESH:
            ht = self.semh["d_gt" if qeng == "gpsimd" else "d_st"]
            self.em(qeng, lambda e, ht=ht, dst=dst, src=src, kw=kw:
                    e.dma_start(dst, src, **kw).then_inc(ht, 16), is_dma=True)
            tv = self.inc(sname, 16)
            ed, esr = self.echo_sb[qeng], self.echo_dr
            self.em(qeng, lambda e, h=h, ed=ed, esr=esr:
                    e.dma_start(ed, esr).then_inc(h, 16), is_dma=True)
            return sname, tv
        tv = self.inc(sname, 16)
        self.em(qeng, lambda e, h=h, dst=dst, src=src, kw=kw:
                e.dma_start(dst, src, **kw).then_inc(h, 16), is_dma=True)
        return sname, tv


def fmv(ap, dh=8):
    return ap.rearrange("q (dh t) -> q dh t", dh=dh)


def emit_program(p, A, taps):
    nc = p.nc
    sb = {}

    def S(name, shape, dtype):
        sb[name] = nc.alloc_sbuf_tensor("sb_" + name, shape, dtype).ap()
        return sb[name]

    VECS = ["ln_g", "ln_b", "conv_b", "Dp", "b_dt", "ln1_w", "ln1_b",
            "ln2_w", "ln2_b", "decay_v", "u_v", "b_proj"]
    for n in VECS:
        S(n, [128, 8], F32)
    S("b_fc_fm", [128, 32], F32)
    S("cw", [128, 32], F32)
    S("A_row", [1, 64], F32)
    S("Amov", [64, 16], F32)
    S("A_cm", [128, 16], F32)
    S("S1c", [64, 128], F32)
    S("S2c", [64, 16], F32)
    S("Ssum", [128, 32], BF16)
    S("Ssel3", [64, 2048], BF16)
    S("eyef", [128, 128], F32)
    S("eye", [128, 128], BF16)
    S("iota_s", [128, 512], F32)
    S("ones_c", [128, 1], F32)
    S("eps_c", [128, 1], F32)
    S("ones_r1f", [1, 512], F32)
    S("ones_r1b", [1, 16], BF16)
    S("ew", [128, 8], F32)
    S("eu", [128, 8], F32)
    S("row1", [1, 512], F32)
    S("row2", [1, 512], F32)
    S("row3", [1, 512], F32)
    S("aff", [8, 2048], F32)
    S("E8", [8, 512], F32)
    S("lo8", [8, 1], F32)
    S("hi8", [8, 1], F32)
    S("mid8", [8, 1], F32)
    S("cnt8", [8, 1], F32)
    S("tmp8a", [8, 1], F32)
    S("tmp8b", [8, 1], F32)
    S("slotcol", [128, 128], F32)
    S("affcol", [128, 128], F32)
    S("echo_g", [128, 16], F32)
    S("echo_s", [128, 16], F32)
    p.echo_sb = {"gpsimd": sb["echo_g"], "sync": sb["echo_s"],
                 "scalar": sb["echo_s"]}
    p.echo_dr = A["xT"][0:128, 0:16]
    S("carry", [128, 512], F32)
    S("wa", [128, 16384], BF16)
    S("wb", [128, 16384], BF16)
    S("wc", [128, 16384], BF16)
    S("fa", [128, 8192], F32)
    S("fb", [128, 4096], F32)
    S("ga", [128, 8192], BF16)
    S("gb", [128, 8192], BF16)
    sb["msk"] = sb["fa"][0:8, 0:2048]
    sb["slotx"] = sb["fa"][0:8, 2048:4096]
    sb["iot8"] = sb["fa"][0:8, 4096:6144]

    ps = [nc.alloc_psum_tensor(f"ps{i}", [128, 512], F32).ap() for i in range(8)]

    def TE(fn):
        p.em("tensor", fn)

    def VE(fn, nd=False):
        if not nd:
            p.em("vector", lambda v: v.drain())
        p.em("vector", fn)

    def AC(fn, nd=False):
        if not nd:
            p.em("scalar", lambda s: s.drain())
        p.em("scalar", fn)

    def GP(fn):
        p.em("gpsimd", fn)

    def tapout(name, src_ap):
        if name in taps:
            p.barrier()
            tok = p.dma("sync", A["tap_" + name], src_ap)
            for eng in P.ENGS:
                p.wait(eng, tok)

    def tapout_d(name, dram_ap):
        tapout(name, dram_ap)


    toks = []
    for n in VECS:
        toks.append(p.dma("gpsimd", sb[n],
                          A[n].rearrange("a (dh q) -> a q dh", q=128),
                          allow_slow_non_contiguous=True))
    toks.append(p.dma("gpsimd", sb["b_fc_fm"],
                      A["b_fc"].rearrange("a (dh q) -> a q dh", q=128),
                      allow_slow_non_contiguous=True))
    toks.append(p.dma("gpsimd", sb["cw"].rearrange("q (k dh) -> q k dh", k=4),
                      A["conv_wT"].rearrange("k (dh q) -> q k dh", q=128),
                      allow_slow_non_contiguous=True))
    toks.append(p.dma("sync", sb["A_row"], A["A_log"]))
    toks.append(p.dma("sync", sb["S1c"], A["S1c"]))
    toks.append(p.dma("sync", sb["S2c"], A["S2c"]))
    toks.append(p.dma("sync", sb["Ssum"], A["Ssum"]))
    toks.append(p.dma("sync", sb["Ssel3"], A["Ssel3"]))
    toks.append(p.dma("sync", sb["eyef"], A["eyef"]))
    toks.append(p.dma("sync", sb["eye"], A["eye"]))
    for eng in ("vector", "scalar", "tensor", "gpsimd"):
        for tok in toks:
            p.wait(eng, tok)
    VE(lambda v: v.memset(sb["ones_c"], 1.0))
    VE(lambda v: v.memset(sb["eps_c"], 1e-5))
    VE(lambda v: v.memset(sb["ones_r1f"], 1.0))
    VE(lambda v: v.memset(sb["ones_r1b"], 1.0))
    VE(lambda v: v.memset(sb["carry"], 0.0))
    GP(lambda g: g.iota(sb["iota_s"], [[1, 512]], base=0, channel_multiplier=0,
                        allow_small_or_imprecise_dtypes=True))
    GP(lambda g: g.iota(sb["iot8"], [[1, 2048]], base=0, channel_multiplier=0,
                        allow_small_or_imprecise_dtypes=True))
    AC(lambda s: s.activation(sb["A_row"], sb["A_row"], AT.Exp))
    t_ac = p.mark("scalar")
    p.wait("vector", t_ac)
    VE(lambda v: v.tensor_scalar(sb["A_row"], sb["A_row"], -1.0, None, OP.mult))
    AC(lambda s: s.activation(sb["ew"], sb["decay_v"], AT.Exp))
    t_ac = p.mark("scalar")
    p.wait("vector", t_ac)
    VE(lambda v: v.tensor_scalar(sb["ew"], sb["ew"], -1.0, None, OP.mult))
    t_ve = p.mark("vector")
    p.wait("scalar", t_ve)
    AC(lambda s: s.activation(sb["ew"], sb["ew"], AT.Exp))
    AC(lambda s: s.activation(sb["eu"], sb["u_v"], AT.Exp))
    t_ve2 = p.mark("vector")
    p.wait("tensor", t_ve2)
    TE(lambda t: t.matmul(ps[0][0:64, 0:1], sb["A_row"],
                          sb["ones_r1f"][0:1, 0:1], start=True, stop=True))
    t_te = p.mark("tensor")
    p.wait("vector", t_te)
    VE(lambda v: v.tensor_copy(sb["fa"][0:64, 0:1], ps[0][0:64, 0:1]))
    p.selfsync("vector")
    VE(lambda v: v.tensor_scalar(sb["Amov"], sb["S2c"], sb["fa"][0:64, 0:1],
                                 None, OP.mult))
    t_ve = p.mark("vector")
    p.wait("tensor", t_ve)
    TE(lambda t: t.matmul(ps[1][:, 0:16], sb["S1c"], sb["Amov"],
                          start=True, stop=True))
    t_te = p.mark("tensor")
    p.wait("vector", t_te)
    VE(lambda v: v.tensor_copy(sb["A_cm"], ps[1][:, 0:16]))
    p.barrier()
    tapout("A_cm", sb["A_cm"])

    xTv = A["xT"].rearrange("(dh q) t -> q dh t", q=128)

    def ln_stats_apply(tt, src_f32_4096, dst_xn_slice, dst_xnn_slice,
                       gcol, bcol, want_xnn=True):
        xt = src_f32_4096
        xsq = sb["fa"][:, 4096:8192]
        VE(lambda v: v.tensor_tensor(xsq, xt, xt, OP.mult))
        t_ve = p.mark("vector")
        p.wait("tensor", t_ve)
        for dh in range(8):
            TE(lambda t, dh=dh: t.matmul(
                ps[0][0:1, :], sb["ones_c"], fmv(xt)[:, dh, :],
                start=(dh == 0), stop=(dh == 7)))
        for dh in range(8):
            TE(lambda t, dh=dh: t.matmul(
                ps[1][0:1, :], sb["ones_c"], fmv(xsq, 8)[:, dh, :],
                start=(dh == 0), stop=(dh == 7)))
        t_te = p.mark("tensor")
        p.wait("vector", t_te)
        r1, r2, r3 = sb["row1"], sb["row2"], sb["row3"]
        sl = slice(0, 512)
        VE(lambda v: v.tensor_scalar(r1[0:1, sl], ps[0][0:1, :], 1.0 / D, None, OP.mult))
        VE(lambda v: v.tensor_scalar(r2[0:1, sl], ps[1][0:1, :], 1.0 / D, None, OP.mult))
        VE(lambda v: v.tensor_tensor(r3[0:1, sl], r1[0:1, sl], r1[0:1, sl], OP.mult))
        VE(lambda v: v.tensor_tensor(r2[0:1, sl], r2[0:1, sl], r3[0:1, sl], OP.subtract))
        t_ve = p.mark("vector")
        p.wait("scalar", t_ve)
        AC(lambda s: s.activation(r3[0:1, sl], r2[0:1, sl], AT.Sqrt, bias=sb["eps_c"][0:1, :]))
        t_ac = p.mark("scalar")
        p.wait("vector", t_ac)
        VE(lambda v: v.reciprocal(r3[0:1, sl], r3[0:1, sl]))
        if want_xnn:
            VE(lambda v: v.tensor_tensor(r2[0:1, sl], r2[0:1, sl], r3[0:1, sl], OP.mult))
            VE(lambda v: v.tensor_tensor(r2[0:1, sl], r2[0:1, sl], r3[0:1, sl], OP.mult))
            t_ve = p.mark("vector")
            p.wait("scalar", t_ve)
            AC(lambda s: s.activation(r2[0:1, sl], r2[0:1, sl], AT.Sqrt, bias=sb["eps_c"][0:1, :]))
            t_ac = p.mark("scalar")
            p.wait("vector", t_ac)
            VE(lambda v: v.reciprocal(r2[0:1, sl], r2[0:1, sl]))
            VE(lambda v: v.tensor_tensor(r2[0:1, sl], r2[0:1, sl], r3[0:1, sl], OP.mult))
        t_ve = p.mark("vector")
        p.wait("tensor", t_ve)
        TE(lambda t: t.matmul(ps[2], sb["ones_r1f"][0:1, 0:128], r1[0:1, sl],
                              start=True, stop=True))
        TE(lambda t: t.matmul(ps[3], sb["ones_r1f"][0:1, 0:128], r3[0:1, sl],
                              start=True, stop=True))
        if want_xnn:
            TE(lambda t: t.matmul(ps[4], sb["ones_r1f"][0:1, 0:128], r2[0:1, sl],
                                  start=True, stop=True))
        t_te = p.mark("tensor")
        p.wait("vector", t_te)
        xc = sb["fa"]
        for dh in range(8):
            VE(lambda v, dh=dh: v.tensor_tensor(
                fmv(xc, 16)[:, dh, :], fmv(xt)[:, dh, :], ps[2], OP.subtract))
        if dst_xn_slice is not None:
            for dh in range(8):
                VE(lambda v, dh=dh: v.tensor_tensor(
                    dst_xn_slice(dh), fmv(xc, 16)[:, dh, :], ps[3], OP.mult))
        if want_xnn:
            for dh in range(8):
                VE(lambda v, dh=dh: v.tensor_tensor(
                    fmv(xc, 16)[:, 8 + dh, :], fmv(xc, 16)[:, dh, :], ps[4], OP.mult))
            for dh in range(8):
                VE(lambda v, dh=dh: v.tensor_scalar(
                    dst_xnn_slice(dh), fmv(xc, 16)[:, 8 + dh, :],
                    gcol[:, dh:dh + 1], bcol[:, dh:dh + 1], OP.mult, OP.add))
        t_ve = p.mark("vector")
        p.wait("tensor", t_ve)
        p.wait("scalar", t_ve)
        return t_ve

    tok_win = p.dma("gpsimd", sb["wa"],
                    A["WinT"].rearrange("(kh q) o -> q kh o", q=128))
    tok_gw = p.dma("gpsimd", sb["ga"][:, 0:64],
                   A["gate_wT"].rearrange("(kh q) o -> q kh o", q=128))

    prev_mark = None
    for tt in range(4):
        xt = sb["fb"]
        if prev_mark is not None:
            p.wait("sync", prev_mark)
        tok = p.dma("sync", fmv(xt), xTv[:, :, tt * 512:(tt + 1) * 512])
        p.wait("vector", tok)
        prev_mark = ln_stats_apply(
            tt, xt,
            lambda dh, tt=tt: fmv(sb["wb"])[:, dh, tt * 512:(tt + 1) * 512],
            lambda dh, tt=tt: fmv(sb["wc"])[:, dh, tt * 512:(tt + 1) * 512],
            sb["ln_g"], sb["ln_b"])
    p.wait("sync", prev_mark)
    tok_xn = p.dma("sync", A["xn_fm"], sb["wb"])
    tapout("xn_bf", sb["wb"])
    tapout("xnn_bf", sb["wc"])

    p.wait("tensor", tok_gw)
    gwv = sb["ga"][:, 0:64].rearrange("q (kh o) -> q kh o", kh=8)
    for tt in range(4):
        slt = slice(tt * 512, (tt + 1) * 512)
        for kh in range(8):
            TE(lambda t, kh=kh, slt=slt: t.matmul(
                ps[7][0:8, :], gwv[:, kh, :], fmv(sb["wb"])[:, kh, slt],
                start=(kh == 0), stop=(kh == 7)))
        t_te = p.mark("tensor")
        p.wait("scalar", t_te)
        AC(lambda s: s.activation(
            sb["E8"], ps[7][0:8, :], AT.Exp))
        t_ac = p.mark("scalar")
        p.wait("tensor", t_ac)
        p.wait("vector", t_ac)
        TE(lambda t: t.matmul(
            ps[0][0:1, :], sb["ones_c"][0:8, :], sb["E8"],
            start=True, stop=True))
        t_te = p.mark("tensor")
        p.wait("vector", t_te)
        VE(lambda v: v.reciprocal(sb["row1"], ps[0][0:1, :]))
        t_ve = p.mark("vector")
        p.wait("tensor", t_ve)
        TE(lambda t: t.matmul(
            ps[1][0:8, :], sb["ones_r1f"][0:1, 0:8], sb["row1"],
            start=True, stop=True))
        t_te = p.mark("tensor")
        p.wait("vector", t_te)
        VE(lambda v, slt=slt: v.tensor_tensor(
            sb["aff"][:, slt], sb["E8"], ps[1][0:8, :], OP.mult))
        prev_aff = p.mark("vector")
        p.wait("tensor", prev_aff)
    tapout("aff", sb["aff"])

    xntm_img = A["xn_tm"].rearrange("(tc q) d -> tc q d", q=128)
    prev_tm = None
    for tc in range(16):
        if prev_tm is not None:
            p.wait("scalar", prev_tm)
        for dh in range(8):
            TE(lambda t, tc=tc, dh=dh: t.transpose(
                ps[6].bitcast(BF16)[:, 0:128],
                fmv(sb["wb"])[:, dh, tc * 128:(tc + 1) * 128], sb["eye"]))
            t_te = p.mark("tensor")
            p.wait("scalar", t_te)
            AC(lambda s, dh=dh: s.copy(
                sb["ga"][:, 1024 + dh * 128:1024 + (dh + 1) * 128],
                ps[6].bitcast(BF16)[:, 0:128]))
            t_ac = p.mark("scalar")
            p.wait("tensor", t_ac)
        t_ac = p.mark("scalar")
        p.wait("sync", t_ac)
        prev_tm = p.dma("sync", xntm_img[tc], sb["ga"][:, 1024:2048])

    p.wait("tensor", tok_win)
    p.wait("tensor", tok_xn)
    wav = fmv(sb["wa"])
    for tt in range(4):
        slt = slice(tt * 512, (tt + 1) * 512)
        for oc in range(16):
            bank = oc % 2 + 5
            for kh in range(8):
                TE(lambda t, oc=oc, kh=kh, bank=bank, slt=slt: t.matmul(
                    ps[bank], wav[:, kh, oc * 128:(oc + 1) * 128],
                    fmv(sb["wc"])[:, kh, slt], start=(kh == 0), stop=(kh == 7)))
            t_te = p.mark("tensor")
            if oc < 8:
                p.wait("vector", t_te)
                VE(lambda v, oc=oc, tt=tt, bank=bank: v.tensor_copy(
                    fmv(sb["wb"])[:, oc, tt * 512:(tt + 1) * 512], ps[bank]))
                t_ve = p.mark("vector")
                p.wait("tensor", t_ve)
            else:
                p.wait("scalar", t_te)
                dst = sb["ga"] if oc < 12 else sb["gb"]
                AC(lambda s, oc=oc, tt=tt, bank=bank, dst=dst: s.activation(
                    fmv(dst, 4)[:, (oc - 8) % 4, tt * 512:(tt + 1) * 512],
                    ps[bank], AT.Silu))
                t_ac = p.mark("scalar")
                p.wait("tensor", t_ac)
    p.barrier()
    tok_sza = p.dma("sync", fmv(A["siluz_fm"])[:, 0:4, :], fmv(sb["ga"], 4))
    tok_szb = p.dma("sync", fmv(A["siluz_fm"])[:, 4:8, :], fmv(sb["gb"], 4))
    if "siluz" in taps:
        p.barrier()
        ta = p.dma("sync", fmv(A["tap_siluz"])[:, 0:4, :], fmv(sb["ga"], 4))
        tb = p.dma("sync", fmv(A["tap_siluz"])[:, 4:8, :], fmv(sb["gb"], 4))
        for eng in P.ENGS:
            p.wait(eng, tb)
    tapout("x_main", sb["wb"])

    for dh in range(8):
        accs = sb["fa"][:, 0:2048]
        VE(lambda v, dh=dh, accs=accs: v.tensor_scalar(
            accs, fmv(sb["wb"])[:, dh, :], sb["cw"][:, 24 + dh:25 + dh], None, OP.mult))
        for k in range(3):
            sh = 3 - k
            VE(lambda v, dh=dh, k=k, sh=sh, accs=accs: v.scalar_tensor_tensor(
                accs[:, sh:2048], fmv(sb["wb"])[:, dh, 0:2048 - sh],
                sb["cw"][:, k * 8 + dh:k * 8 + dh + 1], accs[:, sh:2048],
                OP.mult, OP.add))
        t_ve = p.mark("vector")
        p.wait("scalar", t_ve)
        AC(lambda s, dh=dh, accs=accs: s.activation(
            fmv(sb["wc"])[:, dh, :], accs,
            AT.Silu, bias=sb["conv_b"][:, dh:dh + 1]))
        t_ac = p.mark("scalar")
        p.wait("vector", t_ac)
    p.barrier()
    tok_xc = p.dma("sync", A["xconv_fm"], sb["wc"])
    tapout("x_conv", sb["wc"])

    tok_wdt = p.dma("gpsimd", sb["wa"][:, 0:8192],
                    A["WdtT"].rearrange("(kh q) o -> q kh o", q=128))
    p.wait("tensor", tok_wdt)
    wdtv = sb["wa"][:, 0:8192].rearrange("q (kh o) -> q kh o", kh=8)
    for tt in range(4):
        slt = slice(tt * 512, (tt + 1) * 512)
        for oc in range(8):
            bank = oc % 2 + 5
            for kh in range(8):
                TE(lambda t, oc=oc, kh=kh, bank=bank, slt=slt: t.matmul(
                    ps[bank], wdtv[:, kh, oc * 128:(oc + 1) * 128],
                    fmv(sb["wc"])[:, kh, slt], start=(kh == 0), stop=(kh == 7)))
            t_te = p.mark("tensor")
            p.wait("scalar", t_te)
            AC(lambda s, oc=oc, bank=bank: s.activation(
                sb["fa"][:, 0:512], ps[bank], AT.Exp,
                bias=sb["b_dt"][:, oc:oc + 1]))
            AC(lambda s, oc=oc, tt=tt: s.activation(
                fmv(sb["wb"].bitcast(F16))[:, oc, tt * 512:(tt + 1) * 512],
                sb["fa"][:, 0:512], AT.Ln, bias=sb["ones_c"]))
            t_ac = p.mark("scalar")
            p.wait("tensor", t_ac)
    p.barrier()
    tok_dt = p.dma("sync", A["dt_fm"], sb["wb"].bitcast(F16))
    tapout("dt", sb["wb"].bitcast(F16))

    p.wait("gpsimd", tok_sza)
    tok_wB = p.dma("gpsimd", sb["ga"][:, 0:512],
                   A["WBT"].rearrange("(kh q) o -> q kh o", q=128))
    tok_wC = p.dma("gpsimd", sb["ga"][:, 512:1024],
                   A["WCT"].rearrange("(kh q) o -> q kh o", q=128))
    p.wait("tensor", tok_wB)
    p.wait("tensor", tok_wC)
    for tt in range(4):
        slt = slice(tt * 512, (tt + 1) * 512)
        for which, base in (("B", 0), ("C", 512)):
            bank = 5 if which == "B" else 6
            stat = sb["ga"][:, base:base + 512].rearrange("q (kh o) -> q kh o", kh=8)
            for kh in range(8):
                TE(lambda t, kh=kh, bank=bank, slt=slt, stat=stat: t.matmul(
                    ps[bank][0:64, :], stat[:, kh, 0:64],
                    fmv(sb["wc"])[:, kh, slt], start=(kh == 0), stop=(kh == 7)))
            t_te = p.mark("tensor")
            p.wait("vector", t_te)
            fbb = sb["fb"].bitcast(BF16)
            dst = fbb[0:64, 0:2048] if which == "B" else fbb[0:64, 2048:4096]
            VE(lambda v, dst=dst, tt=tt, bank=bank: v.tensor_copy(
                dst[:, tt * 512:(tt + 1) * 512], ps[bank][0:64, :]))
            t_ve = p.mark("vector")
            p.wait("tensor", t_ve)
    p.barrier()
    tapout("Bm", sb["fb"].bitcast(BF16)[0:64, 0:2048])
    tapout("Cm", sb["fb"].bitcast(BF16)[0:64, 2048:4096])


    p.barrier()
    msk, slx, io8 = sb["msk"], sb["slotx"], sb["iot8"]
    VE(lambda v: v.memset(sb["lo8"], 0.0))
    VE(lambda v: v.memset(sb["hi8"], 1.0))
    for it in range(30):
        VE(lambda v: v.tensor_tensor(sb["mid8"], sb["lo8"], sb["hi8"], OP.add))
        VE(lambda v: v.tensor_scalar(sb["mid8"], sb["mid8"], 0.5, None, OP.mult))
        VE(lambda v: v.tensor_scalar(msk, sb["aff"], sb["mid8"], None, OP.is_ge))
        VE(lambda v: v.tensor_reduce(sb["cnt8"], msk, mybir.AxisListType.X, OP.add))
        VE(lambda v: v.tensor_scalar(sb["cnt8"], sb["cnt8"], 512.0, None, OP.is_ge))
        VE(lambda v: v.tensor_tensor(sb["tmp8a"], sb["mid8"], sb["lo8"], OP.subtract))
        VE(lambda v: v.tensor_tensor(sb["tmp8a"], sb["tmp8a"], sb["cnt8"], OP.mult))
        VE(lambda v: v.tensor_tensor(sb["lo8"], sb["lo8"], sb["tmp8a"], OP.add))
        VE(lambda v: v.tensor_tensor(sb["tmp8b"], sb["hi8"], sb["mid8"], OP.subtract))
        VE(lambda v: v.tensor_tensor(sb["tmp8b"], sb["tmp8b"], sb["cnt8"], OP.mult))
        VE(lambda v: v.tensor_tensor(sb["hi8"], sb["mid8"], sb["tmp8b"], OP.add))
    VE(lambda v: v.tensor_scalar(msk, sb["aff"], sb["lo8"], None, OP.is_ge))
    GP(lambda g: g.iota(io8, [[1, 2048]], base=0, channel_multiplier=0,
                        allow_small_or_imprecise_dtypes=True))
    t_gp = p.mark("gpsimd")
    p.wait("vector", t_gp)
    VE(lambda v: v.memset(sb["fa"][0:8, 6144:8192], 1.0))
    VE(lambda v: v.tensor_tensor_scan(slx, sb["fa"][0:8, 6144:8192], msk,
                                      0.0, OP.mult, OP.add))
    VE(lambda v: v.tensor_tensor(slx, slx, msk, OP.mult))
    VE(lambda v: v.tensor_scalar(slx, slx, 1.0, None, OP.subtract))
    t_ve = p.mark("vector")
    p.wait("tensor", t_ve)
    for c in range(16):
        TE(lambda t, c=c: t.transpose(ps[6][:, 0:8], slx[:, c * 128:(c + 1) * 128],
                                      sb["eyef"][0:8, 0:8]))
        TE(lambda t, c=c: t.transpose(ps[7][:, 0:8], sb["aff"][:, c * 128:(c + 1) * 128],
                                      sb["eyef"][0:8, 0:8]))
        t_te = p.mark("tensor")
        p.wait("vector", t_te)
        VE(lambda v, c=c: v.tensor_copy(sb["slotcol"][:, c * 8:(c + 1) * 8], ps[6][:, 0:8]))
        VE(lambda v, c=c: v.tensor_copy(sb["affcol"][:, c * 8:(c + 1) * 8], ps[7][:, 0:8]))
        t_ve = p.mark("vector")
        p.wait("tensor", t_ve)
    p.barrier()

    tapout("slotx", sb["slotx"])
    tapout("thr", sb["lo8"])
    tapout("slotcol", sb["slotcol"])
    tapout_d("xntm", A["xn_tm"])

    xe_img = A["xe_tm"].rearrange("(e q) f -> e q f", q=128)
    wgv = A["wgT"].rearrange("(e kh q) o -> e q kh o", q=128, kh=8)
    wuv = A["wuT"].rearrange("(e kh q) o -> e q kh o", q=128, kh=8)
    wdv = A["wdT"].rearrange("(e2 ic q) o -> e2 ic q o", q=128, ic=32)
    xnv = A["xn_tm"].rearrange("(tc q) (dc w) -> q tc dc w", q=128, w=128)
    for e in range(NE):
        for c in range(16):
            VE(lambda v, c=c, e=e: v.tensor_scalar(
                sb["ga"].rearrange("q (c s) -> q c s", c=16)[:, c, :],
                sb["iota_s"], sb["slotcol"][:, c * 8 + e:c * 8 + e + 1], None,
                OP.is_equal))
        t_ve = p.mark("vector")
        p.wait("tensor", t_ve)
        p.wait("gpsimd", t_ve)
        st_slot = [0, 4096]
        toks = [None] * 8
        marks = [None] * 8
        toks[0] = p.dma("gpsimd", sb["gb"][:, 0:2048], xnv[:, :, 0, :])
        for dc in range(8):
            if dc + 1 < 8:
                if dc >= 1:
                    p.wait("gpsimd", marks[dc - 1])
                sl = st_slot[(dc + 1) % 2]
                toks[dc + 1] = p.dma("gpsimd", sb["gb"][:, sl:sl + 2048],
                                     xnv[:, :, dc + 1, :])
            bank = dc % 2
            sl = st_slot[dc % 2]
            p.wait("tensor", toks[dc])
            for tc in range(16):
                TE(lambda t, tc=tc, bank=bank, sl=sl: t.matmul(
                    ps[bank], sb["gb"][:, sl:sl + 2048].rearrange("q (tc w) -> q tc w", tc=16)[:, tc, :],
                    sb["ga"].rearrange("q (c s) -> q c s", c=16)[:, tc, :],
                    start=(tc == 0), stop=(tc == 15)))
            marks[dc] = t_te = p.mark("tensor")
            p.wait("scalar", t_te)
            AC(lambda s, dc=dc, bank=bank: s.copy(
                sb["wc"].rearrange("q (dc s) -> q dc s", dc=32)[:, dc, :], ps[bank]))
            t_ac = p.mark("scalar")
            p.wait("tensor", t_ac)
        p.wait("gpsimd", t_ac)
        if e == 0:
            tapout("xin0", sb["wc"][:, 0:4096])
        gu_slots = [(5120, 6144), (8192, 9216)]
        gtoks = [None] * 32
        gmarks = [None] * 32

        def gu_load(ic):
            sg, su = gu_slots[ic % 2]
            p.dma("gpsimd", sb["wc"][:, sg:sg + 1024],
                  wgv[e, :, :, ic * 128:(ic + 1) * 128])
            gtoks[ic] = p.dma("gpsimd", sb["wc"][:, su:su + 1024],
                              wuv[e, :, :, ic * 128:(ic + 1) * 128])
        gu_load(0)
        for ic in range(32):
            if ic + 1 < 32:
                if ic >= 1:
                    p.wait("gpsimd", gmarks[ic - 1])
                gu_load(ic + 1)
            sg, su = gu_slots[ic % 2]
            p.wait("tensor", gtoks[ic])
            for kh in range(8):
                TE(lambda t, kh=kh, sg=sg: t.matmul(
                    ps[2], sb["wc"][:, sg:sg + 1024].rearrange("q (kh w) -> q kh w", kh=8)[:, kh, :],
                    sb["wc"].rearrange("q (dc s) -> q dc s", dc=32)[:, kh, :],
                    start=(kh == 0), stop=(kh == 7)))
            for kh in range(8):
                TE(lambda t, kh=kh, su=su: t.matmul(
                    ps[3], sb["wc"][:, su:su + 1024].rearrange("q (kh w) -> q kh w", kh=8)[:, kh, :],
                    sb["wc"].rearrange("q (dc s) -> q dc s", dc=32)[:, kh, :],
                    start=(kh == 0), stop=(kh == 7)))
            gmarks[ic] = t_te = p.mark("tensor")
            p.wait("scalar", t_te)
            AC(lambda s: s.activation(sb["gb"][:, 2048:2560], ps[2], AT.Silu))
            t_ac = p.mark("scalar")
            p.wait("vector", t_ac)
            VE(lambda v, ic=ic: v.tensor_tensor(
                sb["wa"].rearrange("q (ic s) -> q ic s", ic=32)[:, ic, :],
                sb["gb"][:, 2048:2560], ps[3], OP.mult))
            t_ve = p.mark("vector")
            p.wait("tensor", t_ve)
        if e == 0:
            tapout("hi0", sb["wa"][:, 0:16384])
        dn_slots = [10240, 11264]
        dtoks = [None] * 32
        dmarks = [None] * 32
        dtoks[0] = p.dma("gpsimd", sb["wc"][:, 10240:11264], wdv[e, 0, :, :])
        for ic in range(32):
            if ic + 1 < 32:
                if ic >= 1:
                    p.wait("gpsimd", dmarks[ic - 1])
                sl = dn_slots[(ic + 1) % 2]
                dtoks[ic + 1] = p.dma("gpsimd", sb["wc"][:, sl:sl + 1024],
                                      wdv[e, ic + 1, :, :])
            sl = dn_slots[ic % 2]
            p.wait("tensor", dtoks[ic])
            for sc in range(4):
                for dh2 in range(2):
                    TE(lambda t, ic=ic, sc=sc, dh2=dh2, sl=sl: t.matmul(
                        ps[sc * 2 + dh2],
                        sb["wa"].rearrange("q (ic s) -> q ic s", ic=32)[:, ic, sc * 128:(sc + 1) * 128],
                        sb["wc"][:, sl + dh2 * 512:sl + (dh2 + 1) * 512],
                        start=(ic == 0), stop=(ic == 31)))
            dmarks[ic] = p.mark("tensor")
        t_te = p.mark("tensor")
        p.wait("scalar", t_te)
        for sc in range(4):
            for dh2 in range(2):
                AC(lambda s, sc=sc, dh2=dh2: s.copy(
                    sb["gb"][:, sc * 1024 + dh2 * 512: sc * 1024 + (dh2 + 1) * 512],
                    ps[sc * 2 + dh2]))
        t_ac = p.mark("scalar")
        p.wait("tensor", t_ac)
        p.wait("gpsimd", t_ac)
        p.wait("sync", t_ac)
        tok = p.dma("sync", xe_img[e], sb["gb"][:, 0:4096])
        p.wait("vector", tok)
        p.wait("scalar", tok)
        p.wait("gpsimd", tok)
        if e == 0:
            tapout("xe0", sb["gb"][:, 0:4096])
    p.barrier()

    wcf16 = sb["wc"].bitcast(F16)
    fbb = sb["fb"].bitcast(BF16)
    dtimg = A["dt_fm"].rearrange("q (dh t) -> q dh t", dh=8)
    xcimg = A["xconv_fm"].rearrange("q (dh t) -> q dh t", dh=8)
    szimg = A["siluz_fm"].rearrange("q (dh t) -> q dh t", dh=8)
    spimg = A["spre_fm"].rearrange("q (dh t) -> q dh t", dh=8)
    DTS = [0, 1024]
    UDS = [2048, 3072]
    BBS = [4096, 5120]
    SZS = 6144
    SPS = 7168
    for th in range(2):
        prev_ac = None
        for q in range(16):
            for hh in range(2):
                lo = th * 1024 + hh * 512
                TE(lambda t, q=q, lo=lo: t.matmul(
                    ps[6], sb["Ssel3"][:, q * 128:(q + 1) * 128],
                    fbb[0:64, lo:lo + 512], start=True, stop=True))
                TE(lambda t, q=q, lo=lo: t.matmul(
                    ps[7], sb["Ssel3"][:, q * 128:(q + 1) * 128],
                    fbb[0:64, 2048 + lo:2048 + lo + 512], start=True, stop=True))
                t_te = p.mark("tensor")
                p.wait("scalar", t_te)
                AC(lambda s, q=q, hh=hh: s.copy(
                    fmv(sb["wa"], 16)[:, q, hh * 512:(hh + 1) * 512], ps[6]))
                AC(lambda s, q=q, hh=hh: s.copy(
                    fmv(sb["wb"], 16)[:, q, hh * 512:(hh + 1) * 512], ps[7]))
                prev_ac = p.mark("scalar")
                p.wait("tensor", prev_ac)
        p.wait("vector", prev_ac)
        p.wait("sync", prev_ac)
        prev_ep = None
        prev_sp_tok = None
        for g in range(32):
            dh, dl0 = g // 4, (g % 4) * 32
            gs = g % 2
            tsl = slice(th * 1024, (th + 1) * 1024)
            if prev_ep is not None:
                p.wait("sync", prev_ep)
            for pg in range(4):
                p.dma("sync", wcf16[pg * 32:(pg + 1) * 32, DTS[gs]:DTS[gs] + 1024],
                      dtimg[dl0:dl0 + 32, dh, tsl])
                p.dma("sync", sb["wc"][pg * 32:(pg + 1) * 32, UDS[gs]:UDS[gs] + 1024],
                      xcimg[dl0:dl0 + 32, dh, tsl])
            tok_dup = p.dma("sync", sb["wc"][0:32, SZS:SZS + 1024],
                            szimg[dl0:dl0 + 32, dh, tsl])
            p.wait("scalar", tok_dup)
            p.wait("vector", tok_dup)
            for qp in range(8):
                q0, q1 = 2 * qp, 2 * qp + 1
                AC(lambda s, q=q0, gs=gs: s.activation(
                    sb["fa"][:, 0:1024], wcf16[:, DTS[gs]:DTS[gs] + 1024],
                    AT.Exp, scale=sb["A_cm"][:, q:q + 1]), nd=(qp > 0))
                AC(lambda s, q=q1, gs=gs: s.activation(
                    sb["fa"][:, 1024:2048], wcf16[:, DTS[gs]:DTS[gs] + 1024],
                    AT.Exp, scale=sb["A_cm"][:, q:q + 1]), nd=True)
                t_ac = p.mark("scalar")
                VE(lambda v, q=q0, gs=gs: v.tensor_tensor(
                    sb["wc"][:, BBS[0]:BBS[0] + 1024],
                    sb["wc"][:, UDS[gs]:UDS[gs] + 1024],
                    fmv(sb["wa"], 16)[:, q, :], OP.mult))
                VE(lambda v, q=q1, gs=gs: v.tensor_tensor(
                    sb["wc"][:, BBS[1]:BBS[1] + 1024],
                    sb["wc"][:, UDS[gs]:UDS[gs] + 1024],
                    fmv(sb["wa"], 16)[:, q, :], OP.mult), nd=True)
                p.wait("vector", t_ac)
                for qq, asl in ((q0, 0), (q1, 1024)):
                    hbuf = sb["ga"] if qq < 8 else sb["gb"]
                    hsl = (qq % 8) * 1024
                    VE(lambda v, q=qq, g=g, asl=asl, hbuf=hbuf, hsl=hsl:
                       v.tensor_tensor_scan(
                           hbuf[:, hsl:hsl + 1024], sb["fa"][:, asl:asl + 1024],
                           sb["wc"][:, BBS[asl // 1024]:BBS[asl // 1024] + 1024],
                           sb["carry"][:, g * 16 + q:g * 16 + q + 1],
                           OP.mult, OP.add), nd=(asl == 1024))
                t_ve = p.mark("vector")
                p.wait("scalar", t_ve)
            VE(lambda v, g=g: v.tensor_copy(
                sb["carry"][:, g * 16:g * 16 + 8],
                sb["ga"].rearrange("q (a w) -> q a w", a=8)[:, :, 1023]))
            VE(lambda v, g=g: v.tensor_copy(
                sb["carry"][:, g * 16 + 8:g * 16 + 16],
                sb["gb"].rearrange("q (a w) -> q a w", a=8)[:, :, 1023]), nd=True)
            for q in range(16):
                hbuf = sb["ga"] if q < 8 else sb["gb"]
                hsl = (q % 8) * 1024
                VE(lambda v, q=q, hbuf=hbuf, hsl=hsl: v.tensor_tensor(
                    hbuf[:, hsl:hsl + 1024], hbuf[:, hsl:hsl + 1024],
                    fmv(sb["wb"], 16)[:, q, :], OP.mult), nd=(q > 0))
            t_ve = p.mark("vector")
            p.wait("tensor", t_ve)
            for ti in range(2):
                for q in range(16):
                    hbuf = sb["ga"] if q < 8 else sb["gb"]
                    hsl = (q % 8) * 1024
                    TE(lambda t, q=q, ti=ti, hbuf=hbuf, hsl=hsl: t.matmul(
                        ps[ti][0:32, :], sb["Ssum"],
                        hbuf[:, hsl + ti * 512: hsl + (ti + 1) * 512],
                        start=(q == 0), stop=(q == 15)))
            t_te = p.mark("tensor")
            p.wait("vector", t_te)
            if prev_sp_tok is not None:
                p.wait("vector", prev_sp_tok)
            for ti in range(2):
                VE(lambda v, ti=ti, gs=gs, dl0=dl0, dh=dh: v.tensor_scalar(
                    sb["fa"][0:32, 2048 + ti * 512: 2048 + (ti + 1) * 512],
                    sb["wc"][0:32, UDS[gs] + ti * 512: UDS[gs] + (ti + 1) * 512],
                    sb["Dp"][dl0:dl0 + 32, dh:dh + 1], None, OP.mult), nd=(ti == 1))
            for ti in range(2):
                VE(lambda v, ti=ti: v.tensor_tensor(
                    sb["fa"][0:32, 2048 + ti * 512: 2048 + (ti + 1) * 512],
                    sb["fa"][0:32, 2048 + ti * 512: 2048 + (ti + 1) * 512],
                    ps[ti][0:32, :], OP.add), nd=(ti == 1))
            for ti in range(2):
                VE(lambda v, ti=ti: v.tensor_tensor(
                    sb["wc"][0:32, SPS + ti * 512: SPS + (ti + 1) * 512],
                    sb["fa"][0:32, 2048 + ti * 512: 2048 + (ti + 1) * 512],
                    sb["wc"][0:32, SZS + ti * 512: SZS + (ti + 1) * 512], OP.mult), nd=(ti == 1))
            prev_ep = p.mark("vector")
            p.wait("tensor", prev_ep)
            p.wait("sync", prev_ep)
            prev_sp_tok = p.dma("sync", spimg[dl0:dl0 + 32, dh, tsl],
                                sb["wc"][0:32, SPS:SPS + 1024])
    p.barrier()
    tapout_d("spre", A["spre_fm"])

    tok_wo = p.dma("gpsimd", sb["wb"][:, 0:8192],
                   A["WoutT"].rearrange("(kh q) o -> q kh o", q=128))
    wov = sb["wb"][:, 0:8192].rearrange("q (kh o) -> q kh o", kh=8)
    xeimg2 = A["xe_tm"].rearrange("(e q) (sh w) -> e q sh w", q=128, sh=4)
    x1img = A["x1_fm"].rearrange("q (dh t) -> q dh t", dh=8)
    xnnimg = A["xnn1_fm"].rearrange("q (dh t) -> q dh t", dh=8)
    p.wait("tensor", tok_wo)
    for tt in range(4):
        ts512 = slice(tt * 512, (tt + 1) * 512)
        for e in range(8):
            for tc in range(4):
                ci = tt * 4 + tc
                VE(lambda v, ci=ci, e=e: v.tensor_scalar(
                    sb["fa"].bitcast(BF16)[:, 0:512], sb["iota_s"],
                    sb["slotcol"][:, ci * 8 + e:ci * 8 + e + 1],
                    sb["affcol"][:, ci * 8 + e:ci * 8 + e + 1],
                    OP.is_equal, OP.mult))
                t_ve = p.mark("vector")
                p.wait("tensor", t_ve)
                for sh in range(4):
                    TE(lambda t, sh=sh: t.transpose(
                        ps[6].bitcast(BF16)[:, 0:128],
                        sb["fa"].bitcast(BF16)[:, sh * 128:(sh + 1) * 128],
                        sb["eye"]))
                    t_te = p.mark("tensor")
                    p.wait("scalar", t_te)
                    dstb = sb["ga"] if e < 4 else sb["wb"][:, 8192:16384]
                    zi = (e * 4 + sh) % 16
                    AC(lambda s, dstb=dstb, zi=zi, tc=tc: s.copy(
                        dstb.rearrange("q (z w) -> q z w", z=16)
                        [:, zi, tc * 128:(tc + 1) * 128],
                        ps[6].bitcast(BF16)[:, 0:128]))
                    t_ac = p.mark("scalar")
                    p.wait("tensor", t_ac)
                    p.wait("vector", t_ac)
        if tt == 0 and "p2g" in taps:
            tapout("p2g", sb["ga"])
            tapout("affcol", sb["affcol"])
        tok_sp = p.dma("sync", sb["wc"][:, 0:4096],
                       A["spre_fm"].rearrange("q (dh t) -> q dh t", dh=8)[:, :, ts512])
        tok_xn2 = p.dma("sync", sb["wc"][:, 4096:8192],
                        A["xn_fm"].rearrange("q (dh t) -> q dh t", dh=8)[:, :, ts512])
        tok_x = p.dma("sync", fmv(sb["fb"]), xTv[:, :, ts512])
        p.wait("tensor", tok_sp)
        p.wait("vector", tok_x)
        dbuf = "gmoe" not in taps

        def st_load(dc2):
            base = 4096 if (dbuf and dc2 % 2) else 0
            tk = None
            for e2 in range(8):
                tk = p.dma("sync", sb["gb"][:, base + e2 * 512:base + (e2 + 1) * 512],
                           xeimg2[e2, :, :, dc2 * 128:(dc2 + 1) * 128])
            return tk
        if tt > 0:
            p.wait("sync", smarks[7])
        stoks = [None] * 8
        smarks = [None] * 8
        if dbuf:
            stoks[0] = st_load(0)
        for dc in range(8):
            bank_g = dc % 2
            bank_s = 2 + dc % 2
            if dbuf:
                if dc + 1 < 8:
                    if dc >= 1:
                        p.wait("sync", smarks[dc - 1])
                    stoks[dc + 1] = st_load(dc + 1)
            else:
                if dc >= 1:
                    p.wait("sync", smarks[dc - 1])
                stoks[dc] = st_load(dc)
            base = 4096 if (dbuf and dc % 2) else 0
            p.wait("tensor", stoks[dc])
            for e in range(8):
                for sh in range(4):
                    srcb = sb["ga"] if e < 4 else sb["wb"][:, 8192:16384]
                    zi = (e * 4 + sh) % 16
                    TE(lambda t, srcb=srcb, zi=zi, bank_g=bank_g, e=e, sh=sh, base=base: t.matmul(
                        ps[bank_g],
                        sb["gb"][:, base + e * 512 + sh * 128:base + e * 512 + (sh + 1) * 128],
                        srcb.rearrange("q (z w) -> q z w", z=16)[:, zi, :],
                        start=(e == 0 and sh == 0), stop=(e == 7 and sh == 3)))
            smarks[dc] = p.mark("tensor")
            for kh in range(8):
                TE(lambda t, kh=kh, dc=dc, bank_s=bank_s: t.matmul(
                    ps[bank_s], wov[:, kh, dc * 128:(dc + 1) * 128],
                    sb["wc"][:, 0:4096].rearrange("q (kh w) -> q kh w", kh=8)[:, kh, :],
                    start=(kh == 0), stop=(kh == 7)))
            t_te = p.mark("tensor")
            p.wait("scalar", t_te)
            p.wait("vector", t_te)
            if "gmoe" in taps:
                AC(lambda s, bank_g=bank_g, dc=dc: s.copy(
                    sb["wc"].rearrange("q (z w) -> q z w", z=32)[:, 16 + dc, :],
                    ps[bank_g]))
            AC(lambda s, bank_g=bank_g: s.activation(
                sb["fa"][:, 0:512], ps[bank_g], AT.Silu))
            t_ac = p.mark("scalar")
            p.wait("vector", t_ac)
            VE(lambda v, dc=dc, bank_s=bank_s: v.tensor_tensor(
                sb["fa"][:, 512:1024], ps[bank_s],
                sb["wc"][:, 4096:8192].rearrange("q (kh w) -> q kh w", kh=8)[:, dc, :],
                OP.add))
            VE(lambda v: v.tensor_tensor(
                sb["fa"][:, 512:1024], sb["fa"][:, 512:1024], sb["fa"][:, 0:512],
                OP.mult))
            VE(lambda v, dc=dc: v.tensor_tensor(
                sb["fa"].rearrange("q (z w) -> q z w", z=16)[:, 8 + dc, :],
                fmv(sb["fb"])[:, dc, :], sb["fa"][:, 512:1024], OP.add))
            VE(lambda v, dc=dc: v.tensor_tensor(
                sb["fa"][:, 0:512],
                sb["fa"].rearrange("q (z w) -> q z w", z=16)[:, 8 + dc, :],
                sb["fa"].rearrange("q (z w) -> q z w", z=16)[:, 8 + dc, :], OP.mult))
            t_ve = p.mark("vector")
            p.wait("tensor", t_ve)
            TE(lambda t, dc=dc: t.matmul(
                ps[4][0:1, :], sb["ones_c"],
                sb["fa"].rearrange("q (z w) -> q z w", z=16)[:, 8 + dc, :],
                start=(dc == 0), stop=(dc == 7)))
            TE(lambda t, dc=dc: t.matmul(
                ps[5][0:1, :], sb["ones_c"], sb["fa"][:, 0:512],
                start=(dc == 0), stop=(dc == 7)))
            t_te = p.mark("tensor")
            p.wait("vector", t_te)
            p.wait("scalar", t_te)
        if "gmoe" in taps:
            t_g = p.mark("scalar")
            p.wait("sync", t_g)
            p.dma("sync",
                  A["tap_gmoe"].rearrange("q (dh t) -> q dh t", dh=8)[:, :, ts512],
                  sb["wc"].rearrange("q (z w) -> q z w", z=32)[:, 16:24, :])
        p.wait("sync", t_ve)
        tok_x1 = p.dma("sync", x1img[:, :, ts512], sb["fa"][:, 4096:8192])
        r1, r2, r3 = sb["row1"], sb["row2"], sb["row3"]
        VE(lambda v: v.tensor_scalar(r1, ps[4][0:1, :], 1.0 / D, None, OP.mult))
        VE(lambda v: v.tensor_scalar(r2, ps[5][0:1, :], 1.0 / D, None, OP.mult))
        VE(lambda v: v.tensor_tensor(r3, r1, r1, OP.mult))
        VE(lambda v: v.tensor_tensor(r2, r2, r3, OP.subtract))
        t_ve = p.mark("vector")
        p.wait("scalar", t_ve)
        AC(lambda s: s.activation(r3, r2, AT.Sqrt, bias=sb["eps_c"][0:1, :]))
        t_ac = p.mark("scalar")
        p.wait("vector", t_ac)
        VE(lambda v: v.reciprocal(r3, r3))
        VE(lambda v: v.tensor_tensor(r2, r2, r3, OP.mult))
        VE(lambda v: v.tensor_tensor(r2, r2, r3, OP.mult))
        t_ve = p.mark("vector")
        p.wait("scalar", t_ve)
        AC(lambda s: s.activation(r2, r2, AT.Sqrt, bias=sb["eps_c"][0:1, :]))
        t_ac = p.mark("scalar")
        p.wait("vector", t_ac)
        VE(lambda v: v.reciprocal(r2, r2))
        VE(lambda v: v.tensor_tensor(r2, r2, r3, OP.mult))
        t_ve = p.mark("vector")
        p.wait("tensor", t_ve)
        TE(lambda t: t.matmul(ps[6], sb["ones_r1f"][0:1, 0:128], r1, start=True, stop=True))
        TE(lambda t: t.matmul(ps[7], sb["ones_r1f"][0:1, 0:128], r3, start=True, stop=True))
        TE(lambda t: t.matmul(ps[4], sb["ones_r1f"][0:1, 0:128], r2, start=True, stop=True))
        t_te = p.mark("tensor")
        p.wait("vector", t_te)
        for dc in range(8):
            x1s = sb["fa"].rearrange("q (z w) -> q z w", z=16)[:, 8 + dc, :]
            VE(lambda v, x1s=x1s: v.tensor_tensor(sb["fa"][:, 0:512], x1s, ps[6], OP.subtract))
            VE(lambda v, dc=dc, ts512=ts512: v.tensor_tensor(
                fmv(sb["wa"])[:, dc, ts512], sb["fa"][:, 0:512], ps[7], OP.mult))
            VE(lambda v: v.tensor_tensor(sb["fa"][:, 512:1024], sb["fa"][:, 0:512], ps[4], OP.mult))
            VE(lambda v, dc=dc: v.tensor_scalar(
                sb["wc"].rearrange("q (z w) -> q z w", z=32)[:, dc, :],
                sb["fa"][:, 512:1024],
                sb["ln1_w"][:, dc:dc + 1], sb["ln1_b"][:, dc:dc + 1], OP.mult, OP.add))
        t_ve = p.mark("vector")
        p.wait("sync", t_ve)
        p.wait("tensor", t_ve)
        tok_xnn = p.dma("sync", xnnimg[:, :, ts512], sb["wc"][:, 0:4096])
        p.wait("vector", tok_xnn)
        p.wait("scalar", tok_xnn)
    p.barrier()
    tapout_d("x1", A["x1_fm"])
    tapout("xn1", sb["wa"])
    tapout_d("xnn1", A["xnn1_fm"])

    fab = sb["fa"].bitcast(BF16)
    fbb16 = sb["fb"].bitcast(BF16)
    for wi, wname in enumerate(["WrT", "WkT", "WvT"]):
        tok_w = p.dma("gpsimd", fab[:, 0:8192],
                      A[wname].rearrange("(kh q) o -> q kh o", q=128))
        p.wait("tensor", tok_w)
        wv = fab[:, 0:8192].rearrange("q (kh o) -> q kh o", kh=8)
        for tt in range(4):
            ts512 = slice(tt * 512, (tt + 1) * 512)
            tok_xnn = p.dma("sync", fbb16[:, 0:4096], xnnimg[:, :, ts512])
            p.wait("tensor", tok_xnn)
            for oc in range(8):
                bank = oc % 2
                for kh in range(8):
                    TE(lambda t, oc=oc, kh=kh, bank=bank: t.matmul(
                        ps[bank], wv[:, kh, oc * 128:(oc + 1) * 128],
                        fbb16[:, 0:4096].rearrange("q (kh w) -> q kh w", kh=8)[:, kh, :],
                        start=(kh == 0), stop=(kh == 7)))
                t_te = p.mark("tensor")
                p.wait("scalar", t_te)
                if wi == 0:
                    AC(lambda s, oc=oc, ts512=ts512, bank=bank: s.activation(
                        fmv(sb["wb"])[:, oc, ts512], ps[bank], AT.Sigmoid))
                elif wi == 1:
                    AC(lambda s, oc=oc, ts512=ts512, bank=bank: s.copy(
                        fmv(sb["wc"].bitcast(F16))[:, oc, ts512], ps[bank]))
                else:
                    AC(lambda s, oc=oc, ts512=ts512, bank=bank: s.copy(
                        (fmv(sb["ga"], 4)[:, oc, ts512] if oc < 4 else
                         fmv(sb["gb"], 4)[:, oc - 4, ts512]), ps[bank]))
                t_ac = p.mark("scalar")
                p.wait("tensor", t_ac)
            t_ac = p.mark("scalar")
            p.wait("sync", t_ac)
        t_mm = p.mark("tensor")
        p.wait("gpsimd", t_mm)
    p.barrier()
    tapout("rr", sb["wb"])
    tapout("kk_", sb["wc"].bitcast(F16))
    tapout("vva", sb["ga"])
    tapout("vvb", sb["gb"])

    VE(lambda v: v.memset(sb["fb"][:, 2048:2049], 0.0))
    VE(lambda v: v.memset(sb["fa"][:, 6144:6145], 0.0))
    for ch in range(8):
        vsl = (fmv(sb["ga"], 4)[:, ch, :] if ch < 4 else
               fmv(sb["gb"], 4)[:, ch - 4, :])
        ksl = fmv(sb["wc"].bitcast(F16))[:, ch, :]
        VE(lambda v: v.memset(sb["fa"][:, 6144:8192], 1.0))
        VE(lambda v, ch=ch: v.tensor_scalar(
            sb["fa"][:, 6144:8192], sb["fa"][:, 6144:8192],
            sb["ew"][:, ch:ch + 1], None, OP.mult))
        for dr in range(2):
            t_vp = p.mark("vector")
            p.wait("scalar", t_vp)
            AC(lambda s, ksl=ksl: s.activation(sb["fa"][:, 0:2048], ksl, AT.Exp))
            t_ac = p.mark("scalar")
            p.wait("vector", t_ac)
            VE(lambda v, vsl=vsl: v.tensor_tensor(
                sb["fa"][:, 2048:4096], sb["fa"][:, 0:2048], vsl, OP.mult))
            ewb = sb["fa"][:, 6144:8192]
            if dr == 0:
                nout = sb["fb"][:, 0:2048]
                dout = sb["fa"][:, 4096:6144]
                bnum = sb["fa"][:, 2048:4096]
                bden = sb["fa"][:, 0:2048]
            else:
                nout = bass.AP(sb["fb"].tensor, sb["fb"].offset + 2047,
                               [list(sb["fb"].ap[0]), [-1, 2048]])
                dout = bass.AP(sb["fa"].tensor, sb["fa"].offset + 4096 + 2047,
                               [list(sb["fa"].ap[0]), [-1, 2048]])
                bnum = bass.AP(sb["fa"].tensor, sb["fa"].offset + 2048 + 2047,
                               [list(sb["fa"].ap[0]), [-1, 2048]])
                bden = bass.AP(sb["fa"].tensor, sb["fa"].offset + 2047,
                               [list(sb["fa"].ap[0]), [-1, 2048]])
            if ch == 0 and dr == 0:
                tapout("p6a", sb["fa"])
            VE(lambda v, nout=nout, bnum=bnum, ewb=ewb: v.tensor_tensor_scan(
                nout, ewb, bnum, 0.0, OP.mult, OP.add))
            VE(lambda v, dout=dout, bden=bden, ewb=ewb: v.tensor_tensor_scan(
                dout, ewb, bden, 0.0, OP.mult, OP.add))
            if ch == 0 and dr == 0:
                tapout("p6b", sb["fb"])
                tapout("p6c", sb["fa"])
            eucol = sb["eu"][:, ch:ch + 1]
            VE(lambda v, eucol=eucol: v.tensor_scalar(
                sb["fa"][:, 2048:4096], sb["fa"][:, 2048:4096], eucol, None,
                OP.mult))
            VE(lambda v, eucol=eucol: v.tensor_scalar(
                sb["fa"][:, 0:2048], sb["fa"][:, 0:2048], eucol, None,
                OP.mult))
            if dr == 0:
                VE(lambda v: v.tensor_tensor(
                    sb["fa"][:, 2049:4096], sb["fa"][:, 2049:4096],
                    sb["fb"][:, 0:2047], OP.add))
                VE(lambda v: v.tensor_tensor(
                    sb["fa"][:, 1:2048], sb["fa"][:, 1:2048],
                    sb["fa"][:, 4096:6143], OP.add))
            else:
                VE(lambda v: v.tensor_tensor(
                    sb["fa"][:, 2048:4095], sb["fa"][:, 2048:4095],
                    sb["fb"][:, 1:2048], OP.add))
                VE(lambda v: v.tensor_tensor(
                    sb["fa"][:, 0:2047], sb["fa"][:, 0:2047],
                    sb["fa"][:, 4097:6144], OP.add))
            VE(lambda v: v.reciprocal(sb["fa"][:, 0:2048], sb["fa"][:, 0:2048]))
            if dr == 0:
                VE(lambda v: v.tensor_tensor(
                    sb["fb"][:, 2048:4096], sb["fa"][:, 2048:4096],
                    sb["fa"][:, 0:2048], OP.mult))
            else:
                VE(lambda v: v.tensor_tensor(
                    sb["fa"][:, 4096:6144], sb["fa"][:, 2048:4096],
                    sb["fa"][:, 0:2048], OP.mult))
        if ch == 0:
            tapout("p6d", sb["fb"])
            tapout("p6e", sb["fa"])
        VE(lambda v: v.tensor_tensor(
            sb["fa"][:, 4096:6144], sb["fa"][:, 4096:6144],
            sb["fb"][:, 2048:4096], OP.add))
        VE(lambda v: v.tensor_scalar(
            sb["fa"][:, 4096:6144], sb["fa"][:, 4096:6144], 0.5, None, OP.mult))
        VE(lambda v, ch=ch: v.tensor_tensor(
            fmv(sb["wb"])[:, ch, :], fmv(sb["wb"])[:, ch, :],
            sb["fa"][:, 4096:6144], OP.mult))
    t_ve = p.mark("vector")
    p.barrier()
    tapout("rw", sb["wb"])

    tok_wo2 = p.dma("gpsimd", fab[:, 0:8192],
                    A["WoT"].rearrange("(kh q) o -> q kh o", q=128))
    p.wait("tensor", tok_wo2)
    wov2 = fab[:, 0:8192].rearrange("q (kh o) -> q kh o", kh=8)
    xmimg = A["xmid_fm"].rearrange("q (dh t) -> q dh t", dh=8)
    for tt in range(4):
        ts512 = slice(tt * 512, (tt + 1) * 512)
        for dc in range(8):
            bank = dc % 2
            for kh in range(8):
                TE(lambda t, dc=dc, kh=kh, bank=bank, ts512=ts512: t.matmul(
                    ps[bank], wov2[:, kh, dc * 128:(dc + 1) * 128],
                    fmv(sb["wb"])[:, kh, ts512], start=(kh == 0), stop=(kh == 7)))
            t_te = p.mark("tensor")
            p.wait("vector", t_te)
            VE(lambda v, dc=dc, ts512=ts512, bank=bank: v.tensor_tensor(
                fmv(sb["fb"])[:, dc, :], ps[bank], fmv(sb["wa"])[:, dc, ts512],
                OP.add))
            VE(lambda v, dc=dc: v.tensor_tensor(
                sb["fa"][:, 4096:4608], fmv(sb["fb"])[:, dc, :],
                fmv(sb["fb"])[:, dc, :], OP.mult))
            t_ve = p.mark("vector")
            p.wait("tensor", t_ve)
            TE(lambda t, dc=dc: t.matmul(
                ps[4][0:1, :], sb["ones_c"], fmv(sb["fb"])[:, dc, :],
                start=(dc == 0), stop=(dc == 7)))
            TE(lambda t, dc=dc: t.matmul(
                ps[5][0:1, :], sb["ones_c"], sb["fa"][:, 4096:4608],
                start=(dc == 0), stop=(dc == 7)))
            t_te = p.mark("tensor")
            p.wait("vector", t_te)
            p.wait("scalar", t_te)
        p.wait("sync", t_ve)
        tok_xm = p.dma("sync", xmimg[:, :, ts512], sb["fb"])
        p.wait("vector", tok_xm)
        r1, r2, r3 = sb["row1"], sb["row2"], sb["row3"]
        VE(lambda v: v.tensor_scalar(r1, ps[4][0:1, :], 1.0 / D, None, OP.mult))
        VE(lambda v: v.tensor_scalar(r2, ps[5][0:1, :], 1.0 / D, None, OP.mult))
        VE(lambda v: v.tensor_tensor(r3, r1, r1, OP.mult))
        VE(lambda v: v.tensor_tensor(r2, r2, r3, OP.subtract))
        t_ve = p.mark("vector")
        p.wait("scalar", t_ve)
        AC(lambda s: s.activation(r3, r2, AT.Sqrt, bias=sb["eps_c"][0:1, :]))
        t_ac = p.mark("scalar")
        p.wait("vector", t_ac)
        VE(lambda v: v.reciprocal(r3, r3))
        t_ve = p.mark("vector")
        p.wait("tensor", t_ve)
        TE(lambda t: t.matmul(ps[6], sb["ones_r1f"][0:1, 0:128], r1, start=True, stop=True))
        TE(lambda t: t.matmul(ps[7], sb["ones_r1f"][0:1, 0:128], r3, start=True, stop=True))
        t_te = p.mark("tensor")
        p.wait("vector", t_te)
        for dc in range(8):
            VE(lambda v, dc=dc: v.tensor_tensor(
                sb["fa"][:, 4096:4608], fmv(sb["fb"])[:, dc, :], ps[6], OP.subtract))
            VE(lambda v: v.tensor_tensor(
                sb["fa"][:, 4608:5120], sb["fa"][:, 4096:4608], ps[7], OP.mult))
            VE(lambda v, dc=dc, ts512=ts512: v.tensor_scalar(
                fmv(sb["wb"])[:, dc, ts512], sb["fa"][:, 4608:5120],
                sb["ln2_w"][:, dc:dc + 1], sb["ln2_b"][:, dc:dc + 1],
                OP.mult, OP.add))
        t_ve = p.mark("vector")
        p.wait("tensor", t_ve)
        p.wait("sync", t_ve)
    p.barrier()
    tapout_d("xmid", A["xmid_fm"])
    tapout("xn2", sb["wb"])

    for wname, dstname, nch, cw in (("WfcT", "WqfcT_d", 8, 4096),
                                    ("WprojT", "WqprojT_d", 32, 1024)):
        VE(lambda v: v.memset(sb["fa"][:, 7168:7169], 0.0))
        t_ve = p.mark("vector")
        p.wait("sync", t_ve)
        wimg = A[wname].rearrange("(n q) o -> n q o", q=128)
        prev = None
        for n in range(nch):
            tok = p.dma("sync", sb["fa"][:, 0:cw], wimg[n])
            p.wait("vector", tok)
            VE(lambda v, cw=cw: v.tensor_reduce(
                sb["fa"][:, 7169:7170], sb["fa"][:, 0:cw], mybir.AxisListType.X,
                OP.max, apply_absolute_value=True))
            VE(lambda v: v.tensor_tensor(
                sb["fa"][:, 7168:7169], sb["fa"][:, 7168:7169],
                sb["fa"][:, 7169:7170], OP.max))
            prev = p.mark("vector")
            p.wait("sync", prev)
        p.wait("tensor", prev)
        TE(lambda t: t.transpose(ps[6][0:1, 0:128], sb["fa"][:, 7168:7169],
                                 sb["eyef"]))
        t_te = p.mark("tensor")
        p.wait("vector", t_te)
        VE(lambda v: v.tensor_reduce(
            sb["fa"][0:1, 7170:7171], ps[6][0:1, 0:128], mybir.AxisListType.X,
            OP.max))
        VE(lambda v: v.tensor_scalar(
            sb["fa"][0:1, 7171:7172], sb["fa"][0:1, 7170:7171], 1.0 / 127.0,
            None, OP.mult))
        VE(lambda v: v.reciprocal(sb["fa"][0:1, 7172:7173], sb["fa"][0:1, 7171:7172]))
        t_ve = p.mark("vector")
        p.wait("tensor", t_ve)
        TE(lambda t: t.matmul(ps[6][:, 0:2], sb["ones_r1f"][0:1, 0:128],
                              sb["fa"][0:1, 7171:7173], start=True, stop=True))
        t_te = p.mark("tensor")
        p.wait("vector", t_te)
        VE(lambda v: v.tensor_copy(sb["fa"][:, 7173:7175], ps[6][:, 0:2]))
        p.selfsync("vector")
        for n in range(nch):
            tok = p.dma("sync", sb["fa"][:, 0:cw], wimg[n])
            p.wait("vector", tok)
            VE(lambda v, cw=cw: v.tensor_scalar(
                sb["fa"][:, 0:cw], sb["fa"][:, 0:cw],
                sb["fa"][:, 7174:7175], RNE_C, OP.mult, OP.add))
            VE(lambda v, cw=cw: v.tensor_scalar(
                sb["fa"][:, 0:cw], sb["fa"][:, 0:cw], -RNE_C, None, OP.add))
            VE(lambda v, cw=cw: v.tensor_scalar(
                sb["fa"][:, 0:cw], sb["fa"][:, 0:cw], 127.0, -128.0,
                OP.min, OP.max))
            VE(lambda v, cw=cw: v.tensor_scalar(
                sb["fb"].bitcast(BF16)[:, 0:cw], sb["fa"][:, 0:cw],
                sb["fa"][:, 7173:7174], None, OP.mult))
            t_ve = p.mark("vector")
            p.wait("sync", t_ve)
            tok2 = p.dma("sync",
                         A[dstname].rearrange("(n q) o -> n q o", q=128)[n],
                         sb["fb"].bitcast(BF16)[:, 0:cw])
            p.wait("vector", tok2)
    p.barrier()

    kkimg = A["kk_fm"].rearrange("q (ic t) -> q ic t", ic=32)
    hfimg = A["hf_fm"].rearrange("q (ic t) -> q ic t", ic=32)
    fktoks = [None] * 32
    fkmarks = [None] * 32

    def fk_load(ic2):
        return p.dma("gpsimd", fab[:, (ic2 % 2) * 1024:(ic2 % 2) * 1024 + 1024],
                     A["WfkT"].rearrange("(kh q) o -> q kh o", q=128)[:, :, ic2 * 128:(ic2 + 1) * 128])
    fktoks[0] = fk_load(0)
    for ic in range(32):
        slot = (ic % 2) * 1024
        if ic + 1 < 32:
            if ic >= 1:
                p.wait("gpsimd", fkmarks[ic - 1])
            fktoks[ic + 1] = fk_load(ic + 1)
        p.wait("tensor", fktoks[ic])
        for tt in range(4):
            ts512 = slice(tt * 512, (tt + 1) * 512)
            bank = tt % 2
            for kh in range(8):
                TE(lambda t, kh=kh, bank=bank, slot=slot, ts512=ts512: t.matmul(
                    ps[bank],
                    fab[:, slot:slot + 1024].rearrange("q (kh w) -> q kh w", kh=8)[:, kh, :],
                    fmv(sb["wb"])[:, kh, ts512], start=(kh == 0), stop=(kh == 7)))
            t_te = p.mark("tensor")
            p.wait("vector", t_te)
            VE(lambda v, bank=bank: v.tensor_scalar(
                sb["fa"][:, 2048:2560], ps[bank], 0.0, None, OP.max))
            VE(lambda v, bank=bank: v.tensor_tensor(
                fbb16[:, 0:512], sb["fa"][:, 2048:2560], ps[bank], OP.mult))
            t_ve = p.mark("vector")
            p.wait("tensor", t_ve)
            p.wait("sync", t_ve)
            tok_kk = p.dma("sync", kkimg[:, ic, ts512], fbb16[:, 0:512])
            p.wait("vector", tok_kk)
        fkmarks[ic] = p.mark("tensor")
    p.barrier()

    tok_afc = p.dma("gpsimd", fab[:, 0:128],
                    A["A_fcT"].rearrange("(kh q) o -> q kh o", q=128))
    tok_bfc = p.dma("gpsimd", sb["gb"][32:48, 0:4096], A["B_fcT"])
    p.wait("tensor", tok_bfc)
    for tt in range(4):
        ts512 = slice(tt * 512, (tt + 1) * 512)
        for kh in range(8):
            TE(lambda t, kh=kh, ts512=ts512: t.matmul(
                ps[2][32:48, :],
                fab[:, 0:128].rearrange("q (kh w) -> q kh w", kh=8)[:, kh, :],
                fmv(sb["wa"])[:, kh, ts512], start=(kh == 0), stop=(kh == 7)))
        t_te = p.mark("tensor")
        p.wait("vector", t_te)
        VE(lambda v, ts512=ts512: v.tensor_scalar(
            sb["gb"][32:48, 4096:6144][:, ts512], ps[2][32:48, :], 2.0, None, OP.mult))
        t_ve = p.mark("vector")
        p.wait("tensor", t_ve)
    fctoks = [None] * 32
    fcmarks = [None] * 32

    def fc_load(ic2):
        return p.dma("gpsimd", fab[:, (ic2 % 2) * 1024:(ic2 % 2) * 1024 + 1024],
                     A["WqfcT_d"].rearrange("(kh q) o -> q kh o", q=128)[:, :, ic2 * 128:(ic2 + 1) * 128])
    fctoks[0] = fc_load(0)
    for ic in range(32):
        slot = (ic % 2) * 1024
        if ic + 1 < 32:
            if ic >= 1:
                p.wait("gpsimd", fcmarks[ic - 1])
            fctoks[ic + 1] = fc_load(ic + 1)
        p.wait("tensor", fctoks[ic])
        for tt in range(4):
            ts512 = slice(tt * 512, (tt + 1) * 512)
            bank = tt % 2
            for kh in range(8):
                TE(lambda t, kh=kh, bank=bank, slot=slot, ts512=ts512: t.matmul(
                    ps[bank],
                    fab[:, slot:slot + 1024].rearrange("q (kh w) -> q kh w", kh=8)[:, kh, :],
                    fmv(sb["wa"])[:, kh, ts512], start=(kh == 0), stop=False))
            TE(lambda t, ic=ic, ts512=ts512, bank=bank: t.matmul(
                ps[bank], sb["gb"][32:48, ic * 128:(ic + 1) * 128],
                sb["gb"][32:48, 4096:6144][:, ts512], start=False, stop=True))
            t_te = p.mark("tensor")
            p.wait("scalar", t_te)
            AC(lambda s, ic=ic, bank=bank: s.activation(
                fbb16[:, 0:512], ps[bank], AT.Gelu,
                bias=sb["b_fc_fm"][:, ic:ic + 1]))
            t_ac = p.mark("scalar")
            p.wait("tensor", t_ac)
            p.wait("sync", t_ac)
            tok_h = p.dma("sync", hfimg[:, ic, ts512], fbb16[:, 0:512])
            p.wait("scalar", tok_h)
        fcmarks[ic] = p.mark("tensor")
    p.barrier()

    tok_fr = p.dma("gpsimd", sb["ga"][:, 0:8192],
                   A["WfrT"].rearrange("(kh q) o -> q kh o", q=128))
    tok_bpj = p.dma("gpsimd", sb["gb"][32:48, 4096:5120], A["B_projT"])
    wcf32 = sb["wc"].bitcast(F32)
    outimg = A["out_fm"].rearrange("q (dh t) -> q dh t", dh=8)
    frv = sb["ga"][:, 0:8192].rearrange("q (kh o) -> q kh o", kh=8)
    p.wait("tensor", tok_bpj)
    for tt in range(4):
        ts512 = slice(tt * 512, (tt + 1) * 512)
        if tt > 0:
            p.wait("gpsimd", pjm[31])
            p.wait("sync", pjm[31])
        kvt = [None] * 32
        kvm = [None] * 32

        def kv_load(ih2):
            tg = p.dma("gpsimd", fbb16[:, (ih2 % 2) * 1024:(ih2 % 2) * 1024 + 1024],
                  A["WfvT"].rearrange("(ih q) o -> q ih o", q=128, ih=32)[:, ih2, :])
            return tg, p.dma("sync", fbb16[:, 2048 + (ih2 % 2) * 512:2048 + (ih2 % 2) * 512 + 512],
                         kkimg[:, ih2, ts512])
        kvt[0] = kv_load(0)
        for ih in range(32):
            wslot = (ih % 2) * 1024
            kslot = 2048 + (ih % 2) * 512
            if ih + 1 < 32:
                if ih >= 1:
                    p.wait("gpsimd", kvm[ih - 1])
                    p.wait("sync", kvm[ih - 1])
                kvt[ih + 1] = kv_load(ih + 1)
            p.wait("tensor", kvt[ih][0])
            p.wait("tensor", kvt[ih][1])
            for dc in range(8):
                TE(lambda t, ih=ih, dc=dc, wslot=wslot, kslot=kslot: t.matmul(
                    ps[dc],
                    fbb16[:, wslot:wslot + 1024].rearrange("q (dc w) -> q dc w", dc=8)[:, dc, :],
                    fbb16[:, kslot:kslot + 512],
                    start=(ih == 0), stop=(ih == 31)))
            kvm[ih] = p.mark("tensor")
        p.wait("gpsimd", kvm[31])
        p.wait("sync", kvm[31])
        t_te = p.mark("tensor")
        p.wait("vector", t_te)
        for dc in range(8):
            VE(lambda v, dc=dc: v.tensor_copy(
                sb["fa"].rearrange("q (z w) -> q z w", z=16)[:, dc, :], ps[dc]))
        t_ve = p.mark("vector")
        p.wait("tensor", t_ve)
        for dc in range(8):
            bank = dc % 2
            for kh in range(8):
                TE(lambda t, dc=dc, kh=kh, bank=bank, ts512=ts512: t.matmul(
                    ps[bank], frv[:, kh, dc * 128:(dc + 1) * 128],
                    fmv(sb["wb"])[:, kh, ts512], start=(kh == 0), stop=(kh == 7)))
            t_te = p.mark("tensor")
            p.wait("scalar", t_te)
            AC(lambda s, dc=dc, bank=bank: s.activation(
                sb["wc"].rearrange("q (z w) -> q z w", z=32)[:, 16 + dc, :],
                ps[bank], AT.Sigmoid))
            t_ac = p.mark("scalar")
            p.wait("tensor", t_ac)
        l2t = [None] * 32
        l2m = [None] * 32

        def l2_load(ih2):
            tg = p.dma("gpsimd", fbb16[:, (ih2 % 2) * 1024:(ih2 % 2) * 1024 + 16],
                  A["A_projT"].rearrange("(ih q) o -> q ih o", q=128, ih=32)[:, ih2, :])
            return tg, p.dma("sync", fbb16[:, 2048 + (ih2 % 2) * 512:2048 + (ih2 % 2) * 512 + 512],
                         hfimg[:, ih2, ts512])
        l2t[0] = l2_load(0)
        for ih in range(32):
            wslot = (ih % 2) * 1024
            kslot = 2048 + (ih % 2) * 512
            if ih + 1 < 32:
                if ih >= 1:
                    p.wait("gpsimd", l2m[ih - 1])
                    p.wait("sync", l2m[ih - 1])
                l2t[ih + 1] = l2_load(ih + 1)
            p.wait("tensor", l2t[ih][0])
            p.wait("tensor", l2t[ih][1])
            TE(lambda t, ih=ih, wslot=wslot, kslot=kslot: t.matmul(
                ps[2][32:48, :], fbb16[:, wslot:wslot + 16],
                fbb16[:, kslot:kslot + 512], start=(ih == 0), stop=(ih == 31)))
            l2m[ih] = p.mark("tensor")
        p.wait("gpsimd", l2m[31])
        p.wait("sync", l2m[31])
        t_te = p.mark("tensor")
        p.wait("vector", t_te)
        VE(lambda v: v.tensor_scalar(
            sb["gb"][32:48, 6144:6656], ps[2][32:48, :], 2.0, None, OP.mult))
        t_ve = p.mark("vector")
        p.wait("tensor", t_ve)
        p.wait("gpsimd", t_ve)
        pjt = [None] * 32
        pjm = [None] * 32

        def pj_load(ih2):
            tg = p.dma("gpsimd", fbb16[:, (ih2 % 2) * 1024:(ih2 % 2) * 1024 + 1024],
                  A["WqprojT_d"].rearrange("(ih q) o -> q ih o", q=128, ih=32)[:, ih2, :])
            return tg, p.dma("sync", fbb16[:, 2048 + (ih2 % 2) * 512:2048 + (ih2 % 2) * 512 + 512],
                         hfimg[:, ih2, ts512])
        pjt[0] = pj_load(0)
        for ih in range(32):
            wslot = (ih % 2) * 1024
            kslot = 2048 + (ih % 2) * 512
            if ih + 1 < 32:
                if ih >= 1:
                    p.wait("gpsimd", pjm[ih - 1])
                    p.wait("sync", pjm[ih - 1])
                pjt[ih + 1] = pj_load(ih + 1)
            p.wait("tensor", pjt[ih][0])
            p.wait("tensor", pjt[ih][1])
            for dc in range(8):
                TE(lambda t, ih=ih, dc=dc, wslot=wslot, kslot=kslot: t.matmul(
                    ps[dc],
                    fbb16[:, wslot:wslot + 1024].rearrange("q (dc w) -> q dc w", dc=8)[:, dc, :],
                    fbb16[:, kslot:kslot + 512],
                    start=(ih == 0), stop=False))
            pjm[ih] = p.mark("tensor")
        for dc in range(8):
            TE(lambda t, dc=dc: t.matmul(
                ps[dc], sb["gb"][32:48, 4096 + dc * 128:4096 + (dc + 1) * 128],
                sb["gb"][32:48, 6144:6656], start=False, stop=True))
        t_te = p.mark("tensor")
        p.wait("scalar", t_te)
        tok_xm2 = p.dma("sync", sb["fa"][:, 4096:8192], xmimg[:, :, ts512])
        tok_x12 = p.dma("sync", wcf32[:, 0:4096], x1img[:, :, ts512])
        p.wait("scalar", tok_xm2)
        p.wait("vector", tok_x12)
        for dc in range(8):
            AC(lambda s, dc=dc: s.activation(
                fbb16[:, 3072 + dc * 512:3072 + (dc + 1) * 512], ps[dc], AT.Silu,
                bias=sb["b_proj"][:, dc:dc + 1]))
        t_ac = p.mark("scalar")
        p.wait("vector", t_ac)
        for dc in range(8):
            VE(lambda v, dc=dc: v.tensor_tensor(
                sb["fa"][:, 0:4096].rearrange("q (z w) -> q z w", z=8)[:, dc, :],
                sb["fa"][:, 0:4096].rearrange("q (z w) -> q z w", z=8)[:, dc, :],
                sb["wc"].rearrange("q (z w) -> q z w", z=32)[:, 16 + dc, :],
                OP.mult))
            VE(lambda v, dc=dc: v.tensor_tensor(
                sb["fa"][:, 0:4096].rearrange("q (z w) -> q z w", z=8)[:, dc, :],
                sb["fa"][:, 0:4096].rearrange("q (z w) -> q z w", z=8)[:, dc, :],
                sb["fa"][:, 4096:8192].rearrange("q (z w) -> q z w", z=8)[:, dc, :],
                OP.add))
            VE(lambda v, dc=dc: v.tensor_tensor(
                sb["fa"][:, 0:4096].rearrange("q (z w) -> q z w", z=8)[:, dc, :],
                sb["fa"][:, 0:4096].rearrange("q (z w) -> q z w", z=8)[:, dc, :],
                fbb16[:, 3072 + dc * 512:3072 + (dc + 1) * 512], OP.mult))
            VE(lambda v, dc=dc: v.tensor_tensor(
                wcf32[:, 0:4096].rearrange("q (z w) -> q z w", z=8)[:, dc, :],
                wcf32[:, 0:4096].rearrange("q (z w) -> q z w", z=8)[:, dc, :],
                sb["fa"][:, 0:4096].rearrange("q (z w) -> q z w", z=8)[:, dc, :],
                OP.add))
        t_ve = p.mark("vector")
        p.wait("sync", t_ve)
        p.wait("tensor", t_ve)
        tok_out = p.dma("sync", outimg[:, :, ts512], wcf32[:, 0:4096])
        p.wait("vector", tok_out)
        p.wait("scalar", tok_out)
    p.barrier()
    return sb, ps


TAP_SHAPES = {
    "A_cm": ([128, 16], F32),
    "xn_bf": ([128, 8 * T], BF16),
    "xnn_bf": ([128, 8 * T], BF16),
    "x_main": ([128, 8 * T], BF16),
    "siluz": ([128, 8 * T], BF16),
    "x_conv": ([128, 8 * T], BF16),
    "dt": ([128, 8 * T], F16),
    "Bm": ([64, T], BF16),
    "Cm": ([64, T], BF16),
    "aff": ([8, T], F32),
    "spre": ([128, 8 * T], BF16),
    "x1": ([128, 8 * T], F32),
    "xn1": ([128, 8 * T], BF16),
    "xnn1": ([128, 8 * T], BF16),
    "rr": ([128, 8 * T], BF16),
    "kk_": ([128, 8 * T], F16),
    "rw": ([128, 8 * T], BF16),
    "xmid": ([128, 8 * T], F32),
    "xn2": ([128, 8 * T], BF16),
    "slotx": ([8, T], F32),
    "thr": ([8, 1], F32),
    "xin0": ([128, 8 * 512], BF16),
    "xe0": ([128, 4 * 1024], BF16),
    "gmoe": ([128, 8 * T], BF16),
    "slotcol": ([128, 128], F32),
    "xntm": ([T, D], BF16),
    "hi0": ([128, 16384], BF16),
    "p2g": ([128, 8192], BF16),
    "affcol": ([128, 128], F32),
    "vva": ([128, 8192], BF16),
    "vvb": ([128, 8192], BF16),
    "p6a": ([128, 8192], F32),
    "p6b": ([128, 4096], F32),
    "p6c": ([128, 8192], F32),
    "p6d": ([128, 4096], F32),
    "p6e": ([128, 8192], F32),
}


def build_nc(taps=()):
    nc = bass.Bass("TRN2", target_bir_lowering=False, debug=False)
    A = {}

    def inp(name, shape, dtype=F32):
        A[name] = nc.dram_tensor(name, shape, dtype, kind="ExternalInput").ap()

    def scratch(name, shape, dtype):
        A[name] = nc.dram_tensor(name, shape, dtype, kind="Internal").ap()

    def outp(name, shape, dtype=F32):
        A[name] = nc.dram_tensor(name, shape, dtype, kind="ExternalOutput").ap()

    inp("xT", [D, T])
    inp("WinT", [D, 2 * D])
    inp("conv_wT", [4, D])
    inp("WdtT", [D, D])
    inp("WBT", [D, NST])
    inp("WCT", [D, NST])
    inp("WoutT", [D, D])
    inp("gate_wT", [D, NE])
    inp("wgT", [NE * D, FI])
    inp("wuT", [NE * D, FI])
    inp("wdT", [NE * FI, D])
    inp("WrT", [D, D])
    inp("WkT", [D, D])
    inp("WvT", [D, D])
    inp("WoT", [D, D])
    inp("WfkT", [D, FI])
    inp("WfvT", [FI, D])
    inp("WfrT", [D, D])
    inp("WfcT", [D, FI])
    inp("WprojT", [FI, D])
    inp("A_fcT", [D, 16])
    inp("B_fcT", [16, FI])
    inp("A_projT", [FI, 16])
    inp("B_projT", [16, D])
    for n in ["ln_g", "ln_b", "conv_b", "Dp", "b_dt", "ln1_w", "ln1_b",
              "ln2_w", "ln2_b", "decay_v", "u_v", "b_proj"]:
        inp(n, [1, D])
    inp("b_fc", [1, FI])
    inp("A_log", [1, NST])
    inp("S1c", [NST, 128])
    inp("S2c", [NST, 16])
    inp("Ssum", [128, 32], BF16)
    inp("Ssel3", [NST, 2048], BF16)
    inp("eye", [128, 128], BF16)
    inp("eyef", [128, 128])

    scratch("xn_fm", [128, 8 * T], BF16)
    scratch("xn_tm", [T, D], BF16)
    scratch("siluz_fm", [128, 8 * T], BF16)
    scratch("xconv_fm", [128, 8 * T], BF16)
    scratch("dt_fm", [128, 8 * T], F16)
    scratch("spre_fm", [128, 8 * T], BF16)
    scratch("x1_fm", [128, 8 * T], F32)
    scratch("xmid_fm", [128, 8 * T], F32)
    scratch("xnn1_fm", [128, 8 * T], BF16)
    scratch("kk_fm", [128, 32 * 512 * 4], BF16)
    scratch("hf_fm", [128, 32 * 512 * 4], BF16)
    scratch("xe_tm", [NE * 128, 4 * D], BF16)
    scratch("WqfcT_d", [D, FI], BF16)
    scratch("WqprojT_d", [FI, D], BF16)

    outp("out_fm", [128, 8 * T])
    for name in taps:
        shape, dt_ = TAP_SHAPES[name]
        outp("tap_" + name, shape, dt_)

    p = P(nc)
    with contextlib.ExitStack() as stack:
        p.make_sems(stack)
        blk = stack.enter_context(nc.Block())
        emit_program(p, A, set(taps))
        for qeng, sname in (("sync", "d_sy"), ("gpsimd", "d_gp"), ("scalar", "d_ac")):
            h = p.semh[sname]
            tv = p.cnt[sname]
            if tv:
                p.em(qeng, lambda e, h=h, tv=tv: e.wait_ge(h, tv))
        p.finalize_waits()

        @blk.sync
        def _(sync):
            for fn in p.ops["sync"]:
                fn(sync)

        @blk.gpsimd
        def _(g):
            for fn in p.ops["gpsimd"]:
                fn(g)

        @blk.vector
        def _(v):
            for fn in p.ops["vector"]:
                fn(v)

        @blk.scalar
        def _(s):
            for fn in p.ops["scalar"]:
                fn(s)

        @blk.tensor
        def _(t):
            for fn in p.ops["tensor"]:
                fn(t)

    return nc


def prep_inputs(b, kw):
    d = {}
    ct = np.ascontiguousarray
    d["xT"] = ct(kw["L0_attn_x"][b].T)
    d["WinT"] = ct(kw["L0_attn_W_in"].T)
    d["conv_wT"] = ct(kw["L0_attn_conv_w"][:, 0, :].T)
    d["WdtT"] = ct(kw["L0_attn_W_dt"].T)
    d["WBT"] = ct(kw["L0_attn_W_B"].T)
    d["WCT"] = ct(kw["L0_attn_W_C"].T)
    d["WoutT"] = ct(kw["L0_attn_W_out"].T)
    d["gate_wT"] = ct(kw["L0_ffn_gate_w"].T)
    d["wgT"] = ct(kw["L0_ffn_w_gate"].transpose(0, 2, 1)).reshape(NE * D, FI)
    d["wuT"] = ct(kw["L0_ffn_w_up"].transpose(0, 2, 1)).reshape(NE * D, FI)
    d["wdT"] = ct(kw["L0_ffn_w_down"].transpose(0, 2, 1)).reshape(NE * FI, D)
    d["WrT"] = ct(kw["L1_attn_Wr"].T)
    d["WkT"] = ct(kw["L1_attn_Wk"].T)
    d["WvT"] = ct(kw["L1_attn_Wv"].T)
    d["WoT"] = ct(kw["L1_attn_Wo"].T)
    d["WfkT"] = ct(kw["L1_attn_Wfk"].T)
    d["WfvT"] = ct(kw["L1_attn_Wfv"].T)
    d["WfrT"] = ct(kw["L1_attn_Wfr"].T)
    d["WfcT"] = ct(kw["L1_ffn_W_fc"].T)
    d["WprojT"] = ct(kw["L1_ffn_W_proj"].T)
    d["A_fcT"] = ct(kw["L1_ffn_A_fc"].T)
    d["B_fcT"] = ct(kw["L1_ffn_B_fc"].T)
    d["A_projT"] = ct(kw["L1_ffn_A_proj"].T)
    d["B_projT"] = ct(kw["L1_ffn_B_proj"].T)
    d["ln_g"] = kw["L0_attn_ln_g"].reshape(1, D)
    d["ln_b"] = kw["L0_attn_ln_b"].reshape(1, D)
    d["conv_b"] = kw["L0_attn_conv_b"].reshape(1, D)
    d["Dp"] = kw["L0_attn_Dp"].reshape(1, D)
    d["b_dt"] = kw["L0_attn_b_dt"].reshape(1, D)
    d["ln1_w"] = kw["L1_attn_ln1_w"].reshape(1, D)
    d["ln1_b"] = kw["L1_attn_ln1_b"].reshape(1, D)
    d["ln2_w"] = kw["L1_attn_ln2_w"].reshape(1, D)
    d["ln2_b"] = kw["L1_attn_ln2_b"].reshape(1, D)
    d["decay_v"] = kw["L1_attn_decay"].reshape(1, D)
    d["u_v"] = kw["L1_attn_u"].reshape(1, D)
    d["b_proj"] = kw["L1_ffn_b_proj"].reshape(1, D)
    d["b_fc"] = kw["L1_ffn_b_fc"].reshape(1, FI)
    d["A_log"] = kw["L0_attn_A_log"].reshape(1, NST)
    n_ = np.arange(NST)
    pp = np.arange(128)
    d["S1c"] = (n_[:, None] % 4 == pp[None, :] // 32).astype(np.float32)
    d["S2c"] = (n_[:, None] // 4 == np.arange(16)[None, :]).astype(np.float32)
    import ml_dtypes
    d["Ssum"] = (pp[:, None] % 32 == np.arange(32)[None, :]).astype(ml_dtypes.bfloat16)
    sel3 = np.zeros((NST, 16, 128), np.float32)
    for q in range(16):
        sel3[:, q, :] = (n_[:, None] == q * 4 + pp[None, :] // 32)
    d["Ssel3"] = sel3.reshape(NST, 2048).astype(ml_dtypes.bfloat16)
    d["eye"] = np.eye(128).astype(ml_dtypes.bfloat16)
    d["eyef"] = np.eye(128).astype(np.float32)
    return d


def kernel(**kw):
    nc = build_nc(taps=DEBUG_TAPS)
    in_maps = [prep_inputs(b, kw) for b in range(8)]
    res = run_bass_kernel_spmd(nc, in_maps, core_ids=list(range(8)))
    outs = []
    for b in range(8):
        o = res.results[b]["out_fm"]
        outs.append(np.asarray(o).reshape(128, 8, T).transpose(2, 1, 0).reshape(T, D))
    return np.stack(outs, axis=0)


# version 06
# speedup vs baseline: 1.2774x; 1.2387x over previous
import contextlib
import os
import numpy as np
import concourse.bass as bass
import concourse.mybir as mybir
from concourse.bass_utils import run_bass_kernel_spmd

F32 = mybir.dt.float32
F16 = mybir.dt.float16
BF16 = mybir.dt.bfloat16
AT = mybir.ActivationFunctionType
OP = mybir.AluOpType

T = 2048
D = 1024
NST = 64
NE = 8
FI = 4096
RNE_C = 1.5 * 2.0 ** 23
DEBUG_TAPS = ()


class P:

    ENGS = ("sync", "gpsimd", "vector", "scalar", "tensor")

    def __init__(self, nc):
        self.nc = nc
        self.ops = {k: [] for k in self.ENGS}
        self.semh = {}
        self.cnt = {}
        for name in ("s_sy", "s_gp", "s_ve", "s_ac", "s_te",
                     "d_sy", "d_gp", "d_ac", "d_gt", "d_st"):
            self.cnt[name] = 0
        self.echo_sb = {}
        self.echo_dr = None

    def make_sems(self, stack):
        for name in list(self.cnt):
            self.semh[name] = stack.enter_context(self.nc.semaphore(name))

    def em(self, eng, fn, is_dma=False):
        self.ops[eng].append(fn)
        self._last_dma = getattr(self, "_last_dma", {})
        self._last_dma[eng] = is_dma
        self._last_marked = getattr(self, "_last_marked", {})
        self._last_marked[eng] = False

    def inc(self, name, amt=1):
        self.cnt[name] += amt
        return self.cnt[name]

    def mark(self, eng):
        sname = {"sync": "s_sy", "gpsimd": "s_gp", "vector": "s_ve",
                 "scalar": "s_ac", "tensor": "s_te"}[eng]
        tv = self.inc(sname)
        h = self.semh[sname]
        already = getattr(self, "_last_marked", {}).get(eng, False)
        self._last_marked = getattr(self, "_last_marked", {})
        last_dma = getattr(self, "_last_dma", {}).get(eng, False)
        if self.ops[eng] and not already and not last_dma:
            prev = self.ops[eng].pop()
            self.em(eng, lambda e, h=h, prev=prev: prev(e).then_inc(h, 1))
        elif last_dma or not self.ops[eng]:
            self.em(eng, lambda e, h=h: e.nop().then_inc(h, 1))
        else:
            self.em(eng, lambda e, h=h: e.drain().then_inc(h, 1))
        self._last_marked[eng] = True
        return sname, tv

    SEM_ENG = {"s_sy": "sync", "s_gp": "gpsimd", "s_ve": "vector",
               "s_ac": "scalar", "s_te": "tensor",
               "d_sy": "sync", "d_gp": "gpsimd", "d_ac": "scalar"}

    def wait(self, eng, tok):
        sname, tv = tok
        h = self.semh[sname]
        if sname.startswith("d_"):
            self.dwaited = getattr(self, "dwaited", {})
            self.dwaited.setdefault(sname, set()).add(tv)
            self.em(eng, lambda e, h=h, tv=tv: e.wait_ge(h, tv))
            self._flush(eng)
            return
        box = [max(tv, self.cnt[sname])]
        self.pending = getattr(self, "pending", {})
        self.pending.setdefault(self.SEM_ENG[sname], []).append((sname, box))
        self.em(eng, lambda e, h=h, box=box: e.wait_ge(h, box[0]))
        self._flush(eng)

    def _flush(self, eng):
        self.pending = getattr(self, "pending", {})
        for sname, box in self.pending.pop(eng, []):
            box[0] = max(box[0], self.cnt[sname])

    def finalize_waits(self):
        self.pending = getattr(self, "pending", {})
        for eng in list(self.pending):
            self._flush(eng)
        self.dwaited = getattr(self, "dwaited", {})
        for sname, cbefore, box2 in getattr(self, "selfw", []):
            ws = [w for w in self.dwaited.get(sname, ()) if w <= cbefore]
            box2[0] = max(ws, default=0)

    def selfsync(self, eng):
        self.wait(eng, self.mark(eng))

    def barrier(self, engs=None):
        engs = engs or self.ENGS
        for _phase in range(2):
            toks = {eng: self.mark(eng) for eng in engs}
            for eng in engs:
                for other, tok in toks.items():
                    if other != eng:
                        self.wait(eng, tok)

    ECHO_THRESH = 600_000
    SIM = os.environ.get("BASS_SIM_FP64") == "1"

    def dma(self, qeng, dst, src, **kw):
        sname = {"sync": "d_sy", "gpsimd": "d_gp", "scalar": "d_ac"}[qeng]
        h = self.semh[sname]

        def nbytes(ap):
            n = 1
            for s_ in ap.shape:
                n *= s_
            return n * mybir.dt.size(ap.dtype)

        box2 = [0]
        self.selfw = getattr(self, "selfw", [])
        self.selfw.append((sname, self.cnt[sname], box2))
        self.em(qeng, lambda e, h=h, box2=box2: e.wait_ge(h, box2[0]))
        if not self.SIM and max(nbytes(dst), nbytes(src)) > self.ECHO_THRESH:
            ht = self.semh["d_gt" if qeng == "gpsimd" else "d_st"]
            self.em(qeng, lambda e, ht=ht, dst=dst, src=src, kw=kw:
                    e.dma_start(dst, src, **kw).then_inc(ht, 16), is_dma=True)
            tv = self.inc(sname, 16)
            ed, esr = self.echo_sb[qeng], self.echo_dr
            self.em(qeng, lambda e, h=h, ed=ed, esr=esr:
                    e.dma_start(ed, esr).then_inc(h, 16), is_dma=True)
            return sname, tv
        tv = self.inc(sname, 16)
        self.em(qeng, lambda e, h=h, dst=dst, src=src, kw=kw:
                e.dma_start(dst, src, **kw).then_inc(h, 16), is_dma=True)
        return sname, tv


def fmv(ap, dh=8):
    return ap.rearrange("q (dh t) -> q dh t", dh=dh)


def emit_program(p, A, taps):
    nc = p.nc
    sb = {}

    def S(name, shape, dtype):
        sb[name] = nc.alloc_sbuf_tensor("sb_" + name, shape, dtype).ap()
        return sb[name]

    VECS = ["ln_g", "ln_b", "conv_b", "Dp", "b_dt", "ln1_w", "ln1_b",
            "ln2_w", "ln2_b", "decay_v", "u_v", "b_proj"]
    for n in VECS:
        S(n, [128, 8], F32)
    S("b_fc_fm", [128, 32], F32)
    S("cw", [128, 32], F32)
    S("A_row", [1, 64], F32)
    S("Amov", [64, 16], F32)
    S("A_cm", [128, 16], F32)
    S("S1c", [64, 128], F32)
    S("S2c", [64, 16], F32)
    S("Ssum", [128, 32], BF16)
    S("Ssel3", [64, 2048], BF16)
    S("eyef", [128, 128], F32)
    S("eye", [128, 128], BF16)
    S("iota_s", [128, 512], F32)
    S("ones_c", [128, 1], F32)
    S("eps_c", [128, 1], F32)
    S("ones_r1f", [1, 512], F32)
    S("ones_r1b", [1, 16], BF16)
    S("ew", [128, 8], F32)
    S("eu", [128, 8], F32)
    S("row1", [1, 512], F32)
    S("row2", [1, 512], F32)
    S("row3", [1, 512], F32)
    S("aff", [8, 2048], F32)
    S("E8", [8, 512], F32)
    S("lo8", [8, 1], F32)
    S("hi8", [8, 1], F32)
    S("mid8", [8, 1], F32)
    S("cnt8", [8, 1], F32)
    S("tmp8a", [8, 1], F32)
    S("tmp8b", [8, 1], F32)
    S("slotcol", [128, 128], F32)
    S("affcol", [128, 128], F32)
    S("echo_g", [128, 16], F32)
    S("echo_s", [128, 16], F32)
    p.echo_sb = {"gpsimd": sb["echo_g"], "sync": sb["echo_s"],
                 "scalar": sb["echo_s"]}
    p.echo_dr = A["xT"][0:128, 0:16]
    S("carry", [128, 512], F32)
    S("wa", [128, 16384], BF16)
    S("wb", [128, 16384], BF16)
    S("wc", [128, 16384], BF16)
    S("fa", [128, 8192], F32)
    S("fb", [128, 4096], F32)
    S("ga", [128, 8192], BF16)
    S("gb", [128, 8192], BF16)
    sb["msk"] = sb["fa"][0:8, 0:2048]
    sb["slotx"] = sb["fa"][0:8, 2048:4096]
    sb["iot8"] = sb["fa"][0:8, 4096:6144]

    ps = [nc.alloc_psum_tensor(f"ps{i}", [128, 512], F32).ap() for i in range(8)]

    def TE(fn):
        p.em("tensor", fn)

    def VE(fn, nd=False):
        if not nd:
            p.em("vector", lambda v: v.drain())
        p.em("vector", fn)

    def AC(fn, nd=False):
        if not nd:
            p.em("scalar", lambda s: s.drain())
        p.em("scalar", fn)

    def GP(fn):
        p.em("gpsimd", fn)

    def tapout(name, src_ap):
        if name in taps:
            p.barrier()
            tok = p.dma("sync", A["tap_" + name], src_ap)
            for eng in P.ENGS:
                p.wait(eng, tok)

    def tapout_d(name, dram_ap):
        tapout(name, dram_ap)


    toks = []
    for n in VECS:
        toks.append(p.dma("gpsimd", sb[n],
                          A[n].rearrange("a (dh q) -> a q dh", q=128),
                          allow_slow_non_contiguous=True))
    toks.append(p.dma("gpsimd", sb["b_fc_fm"],
                      A["b_fc"].rearrange("a (dh q) -> a q dh", q=128),
                      allow_slow_non_contiguous=True))
    toks.append(p.dma("gpsimd", sb["cw"].rearrange("q (k dh) -> q k dh", k=4),
                      A["conv_wT"].rearrange("k (dh q) -> q k dh", q=128),
                      allow_slow_non_contiguous=True))
    toks.append(p.dma("sync", sb["A_row"], A["A_log"]))
    toks.append(p.dma("sync", sb["S1c"], A["S1c"]))
    toks.append(p.dma("sync", sb["S2c"], A["S2c"]))
    toks.append(p.dma("sync", sb["Ssum"], A["Ssum"]))
    toks.append(p.dma("sync", sb["Ssel3"], A["Ssel3"]))
    toks.append(p.dma("sync", sb["eyef"], A["eyef"]))
    toks.append(p.dma("sync", sb["eye"], A["eye"]))
    for eng in ("vector", "scalar", "tensor", "gpsimd"):
        for tok in toks:
            p.wait(eng, tok)
    VE(lambda v: v.memset(sb["ones_c"], 1.0))
    VE(lambda v: v.memset(sb["eps_c"], 1e-5))
    VE(lambda v: v.memset(sb["ones_r1f"], 1.0))
    VE(lambda v: v.memset(sb["ones_r1b"], 1.0))
    VE(lambda v: v.memset(sb["carry"], 0.0))
    GP(lambda g: g.iota(sb["iota_s"], [[1, 512]], base=0, channel_multiplier=0,
                        allow_small_or_imprecise_dtypes=True))
    GP(lambda g: g.iota(sb["iot8"], [[1, 2048]], base=0, channel_multiplier=0,
                        allow_small_or_imprecise_dtypes=True))
    AC(lambda s: s.activation(sb["A_row"], sb["A_row"], AT.Exp))
    t_ac = p.mark("scalar")
    p.wait("vector", t_ac)
    VE(lambda v: v.tensor_scalar(sb["A_row"], sb["A_row"], -1.0, None, OP.mult))
    AC(lambda s: s.activation(sb["ew"], sb["decay_v"], AT.Exp))
    t_ac = p.mark("scalar")
    p.wait("vector", t_ac)
    VE(lambda v: v.tensor_scalar(sb["ew"], sb["ew"], -1.0, None, OP.mult))
    t_ve = p.mark("vector")
    p.wait("scalar", t_ve)
    AC(lambda s: s.activation(sb["ew"], sb["ew"], AT.Exp))
    AC(lambda s: s.activation(sb["eu"], sb["u_v"], AT.Exp))
    t_ve2 = p.mark("vector")
    p.wait("tensor", t_ve2)
    TE(lambda t: t.matmul(ps[0][0:64, 0:1], sb["A_row"],
                          sb["ones_r1f"][0:1, 0:1], start=True, stop=True))
    t_te = p.mark("tensor")
    p.wait("vector", t_te)
    VE(lambda v: v.tensor_copy(sb["fa"][0:64, 0:1], ps[0][0:64, 0:1]))
    p.selfsync("vector")
    VE(lambda v: v.tensor_scalar(sb["Amov"], sb["S2c"], sb["fa"][0:64, 0:1],
                                 None, OP.mult))
    t_ve = p.mark("vector")
    p.wait("tensor", t_ve)
    TE(lambda t: t.matmul(ps[1][:, 0:16], sb["S1c"], sb["Amov"],
                          start=True, stop=True))
    t_te = p.mark("tensor")
    p.wait("vector", t_te)
    VE(lambda v: v.tensor_copy(sb["A_cm"], ps[1][:, 0:16]))
    p.barrier()
    tapout("A_cm", sb["A_cm"])

    xTv = A["xT"].rearrange("(dh q) t -> q dh t", q=128)

    def ln_stats_apply(tt, src_f32_4096, dst_xn_slice, dst_xnn_slice,
                       gcol, bcol, want_xnn=True):
        xt = src_f32_4096
        xsq = sb["fa"][:, 4096:8192]
        VE(lambda v: v.tensor_tensor(xsq, xt, xt, OP.mult))
        t_ve = p.mark("vector")
        p.wait("tensor", t_ve)
        for dh in range(8):
            TE(lambda t, dh=dh: t.matmul(
                ps[0][0:1, :], sb["ones_c"], fmv(xt)[:, dh, :],
                start=(dh == 0), stop=(dh == 7)))
        for dh in range(8):
            TE(lambda t, dh=dh: t.matmul(
                ps[1][0:1, :], sb["ones_c"], fmv(xsq, 8)[:, dh, :],
                start=(dh == 0), stop=(dh == 7)))
        t_te = p.mark("tensor")
        p.wait("vector", t_te)
        r1, r2, r3 = sb["row1"], sb["row2"], sb["row3"]
        sl = slice(0, 512)
        VE(lambda v: v.tensor_scalar(r1[0:1, sl], ps[0][0:1, :], 1.0 / D, None, OP.mult))
        VE(lambda v: v.tensor_scalar(r2[0:1, sl], ps[1][0:1, :], 1.0 / D, None, OP.mult))
        VE(lambda v: v.tensor_tensor(r3[0:1, sl], r1[0:1, sl], r1[0:1, sl], OP.mult))
        VE(lambda v: v.tensor_tensor(r2[0:1, sl], r2[0:1, sl], r3[0:1, sl], OP.subtract))
        t_ve = p.mark("vector")
        p.wait("scalar", t_ve)
        AC(lambda s: s.activation(r3[0:1, sl], r2[0:1, sl], AT.Sqrt, bias=sb["eps_c"][0:1, :]))
        t_ac = p.mark("scalar")
        p.wait("vector", t_ac)
        VE(lambda v: v.reciprocal(r3[0:1, sl], r3[0:1, sl]))
        if want_xnn:
            VE(lambda v: v.tensor_tensor(r2[0:1, sl], r2[0:1, sl], r3[0:1, sl], OP.mult))
            VE(lambda v: v.tensor_tensor(r2[0:1, sl], r2[0:1, sl], r3[0:1, sl], OP.mult))
            t_ve = p.mark("vector")
            p.wait("scalar", t_ve)
            AC(lambda s: s.activation(r2[0:1, sl], r2[0:1, sl], AT.Sqrt, bias=sb["eps_c"][0:1, :]))
            t_ac = p.mark("scalar")
            p.wait("vector", t_ac)
            VE(lambda v: v.reciprocal(r2[0:1, sl], r2[0:1, sl]))
            VE(lambda v: v.tensor_tensor(r2[0:1, sl], r2[0:1, sl], r3[0:1, sl], OP.mult))
        t_ve = p.mark("vector")
        p.wait("tensor", t_ve)
        TE(lambda t: t.matmul(ps[2], sb["ones_r1f"][0:1, 0:128], r1[0:1, sl],
                              start=True, stop=True))
        TE(lambda t: t.matmul(ps[3], sb["ones_r1f"][0:1, 0:128], r3[0:1, sl],
                              start=True, stop=True))
        if want_xnn:
            TE(lambda t: t.matmul(ps[4], sb["ones_r1f"][0:1, 0:128], r2[0:1, sl],
                                  start=True, stop=True))
        t_te = p.mark("tensor")
        p.wait("vector", t_te)
        xc = sb["fa"]
        for dh in range(8):
            VE(lambda v, dh=dh: v.tensor_tensor(
                fmv(xc, 16)[:, dh, :], fmv(xt)[:, dh, :], ps[2], OP.subtract))
        if dst_xn_slice is not None:
            for dh in range(8):
                VE(lambda v, dh=dh: v.tensor_tensor(
                    dst_xn_slice(dh), fmv(xc, 16)[:, dh, :], ps[3], OP.mult))
        if want_xnn:
            for dh in range(8):
                VE(lambda v, dh=dh: v.tensor_tensor(
                    fmv(xc, 16)[:, 8 + dh, :], fmv(xc, 16)[:, dh, :], ps[4], OP.mult))
            for dh in range(8):
                VE(lambda v, dh=dh: v.tensor_scalar(
                    dst_xnn_slice(dh), fmv(xc, 16)[:, 8 + dh, :],
                    gcol[:, dh:dh + 1], bcol[:, dh:dh + 1], OP.mult, OP.add))
        t_ve = p.mark("vector")
        p.wait("tensor", t_ve)
        p.wait("scalar", t_ve)
        return t_ve

    tok_win = p.dma("gpsimd", sb["wa"],
                    A["WinT"].rearrange("(kh q) o -> q kh o", q=128))
    tok_gw = p.dma("gpsimd", sb["ga"][:, 0:64],
                   A["gate_wT"].rearrange("(kh q) o -> q kh o", q=128))

    prev_mark = None
    for tt in range(4):
        xt = sb["fb"]
        if prev_mark is not None:
            p.wait("sync", prev_mark)
        tok = p.dma("sync", fmv(xt), xTv[:, :, tt * 512:(tt + 1) * 512])
        p.wait("vector", tok)
        prev_mark = ln_stats_apply(
            tt, xt,
            lambda dh, tt=tt: fmv(sb["wb"])[:, dh, tt * 512:(tt + 1) * 512],
            lambda dh, tt=tt: fmv(sb["wc"])[:, dh, tt * 512:(tt + 1) * 512],
            sb["ln_g"], sb["ln_b"])
    p.wait("sync", prev_mark)
    tok_xn = p.dma("sync", A["xn_fm"], sb["wb"])
    tapout("xn_bf", sb["wb"])
    tapout("xnn_bf", sb["wc"])

    p.wait("tensor", tok_gw)
    gwv = sb["ga"][:, 0:64].rearrange("q (kh o) -> q kh o", kh=8)
    for tt in range(4):
        slt = slice(tt * 512, (tt + 1) * 512)
        for kh in range(8):
            TE(lambda t, kh=kh, slt=slt: t.matmul(
                ps[7][0:8, :], gwv[:, kh, :], fmv(sb["wb"])[:, kh, slt],
                start=(kh == 0), stop=(kh == 7)))
        t_te = p.mark("tensor")
        p.wait("scalar", t_te)
        AC(lambda s: s.activation(
            sb["E8"], ps[7][0:8, :], AT.Exp))
        t_ac = p.mark("scalar")
        p.wait("tensor", t_ac)
        p.wait("vector", t_ac)
        TE(lambda t: t.matmul(
            ps[0][0:1, :], sb["ones_c"][0:8, :], sb["E8"],
            start=True, stop=True))
        t_te = p.mark("tensor")
        p.wait("vector", t_te)
        VE(lambda v: v.reciprocal(sb["row1"], ps[0][0:1, :]))
        t_ve = p.mark("vector")
        p.wait("tensor", t_ve)
        TE(lambda t: t.matmul(
            ps[1][0:8, :], sb["ones_r1f"][0:1, 0:8], sb["row1"],
            start=True, stop=True))
        t_te = p.mark("tensor")
        p.wait("vector", t_te)
        VE(lambda v, slt=slt: v.tensor_tensor(
            sb["aff"][:, slt], sb["E8"], ps[1][0:8, :], OP.mult))
        prev_aff = p.mark("vector")
        p.wait("tensor", prev_aff)
    tapout("aff", sb["aff"])

    xntm_img = A["xn_tm"].rearrange("(tc q) d -> tc q d", q=128)
    prev_tm = None
    for tc in range(16):
        if prev_tm is not None:
            p.wait("scalar", prev_tm)
        for dh in range(8):
            TE(lambda t, tc=tc, dh=dh: t.transpose(
                ps[6].bitcast(BF16)[:, 0:128],
                fmv(sb["wb"])[:, dh, tc * 128:(tc + 1) * 128], sb["eye"]))
            t_te = p.mark("tensor")
            p.wait("scalar", t_te)
            AC(lambda s, dh=dh: s.copy(
                sb["ga"][:, 1024 + dh * 128:1024 + (dh + 1) * 128],
                ps[6].bitcast(BF16)[:, 0:128]))
            t_ac = p.mark("scalar")
            p.wait("tensor", t_ac)
        t_ac = p.mark("scalar")
        p.wait("sync", t_ac)
        prev_tm = p.dma("sync", xntm_img[tc], sb["ga"][:, 1024:2048])

    p.wait("tensor", tok_win)
    p.wait("tensor", tok_xn)
    wav = fmv(sb["wa"])
    for tt in range(4):
        slt = slice(tt * 512, (tt + 1) * 512)
        for oc in range(16):
            bank = oc % 2 + 5
            for kh in range(8):
                TE(lambda t, oc=oc, kh=kh, bank=bank, slt=slt: t.matmul(
                    ps[bank], wav[:, kh, oc * 128:(oc + 1) * 128],
                    fmv(sb["wc"])[:, kh, slt], start=(kh == 0), stop=(kh == 7)))
            t_te = p.mark("tensor")
            if oc < 8:
                p.wait("vector", t_te)
                VE(lambda v, oc=oc, tt=tt, bank=bank: v.tensor_copy(
                    fmv(sb["wb"])[:, oc, tt * 512:(tt + 1) * 512], ps[bank]))
                t_ve = p.mark("vector")
                p.wait("tensor", t_ve)
            else:
                p.wait("scalar", t_te)
                dst = sb["ga"] if oc < 12 else sb["gb"]
                AC(lambda s, oc=oc, tt=tt, bank=bank, dst=dst: s.activation(
                    fmv(dst, 4)[:, (oc - 8) % 4, tt * 512:(tt + 1) * 512],
                    ps[bank], AT.Silu))
                t_ac = p.mark("scalar")
                p.wait("tensor", t_ac)
    p.barrier()
    tok_sza = p.dma("sync", fmv(A["siluz_fm"])[:, 0:4, :], fmv(sb["ga"], 4))
    tok_szb = p.dma("sync", fmv(A["siluz_fm"])[:, 4:8, :], fmv(sb["gb"], 4))
    if "siluz" in taps:
        p.barrier()
        ta = p.dma("sync", fmv(A["tap_siluz"])[:, 0:4, :], fmv(sb["ga"], 4))
        tb = p.dma("sync", fmv(A["tap_siluz"])[:, 4:8, :], fmv(sb["gb"], 4))
        for eng in P.ENGS:
            p.wait(eng, tb)
    tapout("x_main", sb["wb"])

    for dh in range(8):
        accs = sb["fa"][:, 0:2048]
        VE(lambda v, dh=dh, accs=accs: v.tensor_scalar(
            accs, fmv(sb["wb"])[:, dh, :], sb["cw"][:, 24 + dh:25 + dh], None, OP.mult))
        for k in range(3):
            sh = 3 - k
            VE(lambda v, dh=dh, k=k, sh=sh, accs=accs: v.scalar_tensor_tensor(
                accs[:, sh:2048], fmv(sb["wb"])[:, dh, 0:2048 - sh],
                sb["cw"][:, k * 8 + dh:k * 8 + dh + 1], accs[:, sh:2048],
                OP.mult, OP.add))
        t_ve = p.mark("vector")
        p.wait("scalar", t_ve)
        AC(lambda s, dh=dh, accs=accs: s.activation(
            fmv(sb["wc"])[:, dh, :], accs,
            AT.Silu, bias=sb["conv_b"][:, dh:dh + 1]))
        t_ac = p.mark("scalar")
        p.wait("vector", t_ac)
    p.barrier()
    tok_xc = p.dma("sync", A["xconv_fm"], sb["wc"])
    tapout("x_conv", sb["wc"])

    tok_wdt = p.dma("gpsimd", sb["wa"][:, 0:8192],
                    A["WdtT"].rearrange("(kh q) o -> q kh o", q=128))
    p.wait("tensor", tok_wdt)
    wdtv = sb["wa"][:, 0:8192].rearrange("q (kh o) -> q kh o", kh=8)
    for tt in range(4):
        slt = slice(tt * 512, (tt + 1) * 512)
        for oc in range(8):
            bank = oc % 2 + 5
            for kh in range(8):
                TE(lambda t, oc=oc, kh=kh, bank=bank, slt=slt: t.matmul(
                    ps[bank], wdtv[:, kh, oc * 128:(oc + 1) * 128],
                    fmv(sb["wc"])[:, kh, slt], start=(kh == 0), stop=(kh == 7)))
            t_te = p.mark("tensor")
            p.wait("scalar", t_te)
            AC(lambda s, oc=oc, bank=bank: s.activation(
                sb["fa"][:, 0:512], ps[bank], AT.Exp,
                bias=sb["b_dt"][:, oc:oc + 1]))
            AC(lambda s, oc=oc, tt=tt: s.activation(
                fmv(sb["wb"].bitcast(F16))[:, oc, tt * 512:(tt + 1) * 512],
                sb["fa"][:, 0:512], AT.Ln, bias=sb["ones_c"]))
            t_ac = p.mark("scalar")
            p.wait("tensor", t_ac)
    p.barrier()
    tok_dt = p.dma("sync", A["dt_fm"], sb["wb"].bitcast(F16))
    tapout("dt", sb["wb"].bitcast(F16))

    p.wait("gpsimd", tok_sza)
    tok_wB = p.dma("gpsimd", sb["ga"][:, 0:512],
                   A["WBT"].rearrange("(kh q) o -> q kh o", q=128))
    tok_wC = p.dma("gpsimd", sb["ga"][:, 512:1024],
                   A["WCT"].rearrange("(kh q) o -> q kh o", q=128))
    p.wait("tensor", tok_wB)
    p.wait("tensor", tok_wC)
    for tt in range(4):
        slt = slice(tt * 512, (tt + 1) * 512)
        for which, base in (("B", 0), ("C", 512)):
            bank = 5 if which == "B" else 6
            stat = sb["ga"][:, base:base + 512].rearrange("q (kh o) -> q kh o", kh=8)
            for kh in range(8):
                TE(lambda t, kh=kh, bank=bank, slt=slt, stat=stat: t.matmul(
                    ps[bank][0:64, :], stat[:, kh, 0:64],
                    fmv(sb["wc"])[:, kh, slt], start=(kh == 0), stop=(kh == 7)))
            t_te = p.mark("tensor")
            p.wait("vector", t_te)
            fbb = sb["fb"].bitcast(BF16)
            dst = fbb[0:64, 0:2048] if which == "B" else fbb[0:64, 2048:4096]
            VE(lambda v, dst=dst, tt=tt, bank=bank: v.tensor_copy(
                dst[:, tt * 512:(tt + 1) * 512], ps[bank][0:64, :]))
            t_ve = p.mark("vector")
            p.wait("tensor", t_ve)
    p.barrier()
    tapout("Bm", sb["fb"].bitcast(BF16)[0:64, 0:2048])
    tapout("Cm", sb["fb"].bitcast(BF16)[0:64, 2048:4096])


    p.barrier()
    msk, slx, io8 = sb["msk"], sb["slotx"], sb["iot8"]
    VE(lambda v: v.memset(sb["lo8"], 0.0))
    VE(lambda v: v.memset(sb["hi8"], 1.0))
    for it in range(30):
        VE(lambda v: v.tensor_tensor(sb["mid8"], sb["lo8"], sb["hi8"], OP.add))
        VE(lambda v: v.tensor_scalar(sb["mid8"], sb["mid8"], 0.5, None, OP.mult))
        VE(lambda v: v.tensor_scalar(msk, sb["aff"], sb["mid8"], None, OP.is_ge))
        VE(lambda v: v.tensor_reduce(sb["cnt8"], msk, mybir.AxisListType.X, OP.add))
        VE(lambda v: v.tensor_scalar(sb["cnt8"], sb["cnt8"], 512.0, None, OP.is_ge))
        VE(lambda v: v.tensor_tensor(sb["tmp8a"], sb["mid8"], sb["lo8"], OP.subtract))
        VE(lambda v: v.tensor_tensor(sb["tmp8a"], sb["tmp8a"], sb["cnt8"], OP.mult))
        VE(lambda v: v.tensor_tensor(sb["lo8"], sb["lo8"], sb["tmp8a"], OP.add))
        VE(lambda v: v.tensor_tensor(sb["tmp8b"], sb["hi8"], sb["mid8"], OP.subtract))
        VE(lambda v: v.tensor_tensor(sb["tmp8b"], sb["tmp8b"], sb["cnt8"], OP.mult))
        VE(lambda v: v.tensor_tensor(sb["hi8"], sb["mid8"], sb["tmp8b"], OP.add))
    VE(lambda v: v.tensor_scalar(msk, sb["aff"], sb["lo8"], None, OP.is_ge))
    GP(lambda g: g.iota(io8, [[1, 2048]], base=0, channel_multiplier=0,
                        allow_small_or_imprecise_dtypes=True))
    t_gp = p.mark("gpsimd")
    p.wait("vector", t_gp)
    VE(lambda v: v.memset(sb["fa"][0:8, 6144:8192], 1.0))
    VE(lambda v: v.tensor_tensor_scan(slx, sb["fa"][0:8, 6144:8192], msk,
                                      0.0, OP.mult, OP.add))
    VE(lambda v: v.tensor_tensor(slx, slx, msk, OP.mult))
    VE(lambda v: v.tensor_scalar(slx, slx, 1.0, None, OP.subtract))
    t_ve = p.mark("vector")
    p.wait("tensor", t_ve)
    for c in range(16):
        TE(lambda t, c=c: t.transpose(ps[6][:, 0:8], slx[:, c * 128:(c + 1) * 128],
                                      sb["eyef"][0:8, 0:8]))
        TE(lambda t, c=c: t.transpose(ps[7][:, 0:8], sb["aff"][:, c * 128:(c + 1) * 128],
                                      sb["eyef"][0:8, 0:8]))
        t_te = p.mark("tensor")
        p.wait("vector", t_te)
        VE(lambda v, c=c: v.tensor_copy(sb["slotcol"][:, c * 8:(c + 1) * 8], ps[6][:, 0:8]))
        VE(lambda v, c=c: v.tensor_copy(sb["affcol"][:, c * 8:(c + 1) * 8], ps[7][:, 0:8]))
        t_ve = p.mark("vector")
        p.wait("tensor", t_ve)
    p.barrier()

    tapout("slotx", sb["slotx"])
    tapout("thr", sb["lo8"])
    tapout("slotcol", sb["slotcol"])
    tapout_d("xntm", A["xn_tm"])

    xe_img = A["xe_tm"].rearrange("(e q) f -> e q f", q=128)
    wgv = A["wgT"].rearrange("(e kh q) o -> e q kh o", q=128, kh=8)
    wuv = A["wuT"].rearrange("(e kh q) o -> e q kh o", q=128, kh=8)
    wdv = A["wdT"].rearrange("(e2 ic q) o -> e2 ic q o", q=128, ic=32)
    xnv = A["xn_tm"].rearrange("(tc q) (dc w) -> q tc dc w", q=128, w=128)
    for e in range(NE):
        for c in range(16):
            VE(lambda v, c=c, e=e: v.tensor_scalar(
                sb["ga"].rearrange("q (c s) -> q c s", c=16)[:, c, :],
                sb["iota_s"], sb["slotcol"][:, c * 8 + e:c * 8 + e + 1], None,
                OP.is_equal))
        t_ve = p.mark("vector")
        p.wait("tensor", t_ve)
        p.wait("gpsimd", t_ve)
        st_slot = [0, 4096]
        toks = [None] * 8
        marks = [None] * 8
        toks[0] = p.dma("gpsimd", sb["gb"][:, 0:2048], xnv[:, :, 0, :])
        for dc in range(8):
            if dc + 1 < 8:
                if dc >= 1:
                    p.wait("gpsimd", marks[dc - 1])
                sl = st_slot[(dc + 1) % 2]
                toks[dc + 1] = p.dma("gpsimd", sb["gb"][:, sl:sl + 2048],
                                     xnv[:, :, dc + 1, :])
            bank = dc % 2
            sl = st_slot[dc % 2]
            p.wait("tensor", toks[dc])
            for tc in range(16):
                TE(lambda t, tc=tc, bank=bank, sl=sl: t.matmul(
                    ps[bank], sb["gb"][:, sl:sl + 2048].rearrange("q (tc w) -> q tc w", tc=16)[:, tc, :],
                    sb["ga"].rearrange("q (c s) -> q c s", c=16)[:, tc, :],
                    start=(tc == 0), stop=(tc == 15)))
            marks[dc] = t_te = p.mark("tensor")
            p.wait("scalar", t_te)
            AC(lambda s, dc=dc, bank=bank: s.copy(
                sb["wc"].rearrange("q (dc s) -> q dc s", dc=32)[:, dc, :], ps[bank]))
            t_ac = p.mark("scalar")
            p.wait("tensor", t_ac)
        p.wait("gpsimd", t_ac)
        if e == 0:
            tapout("xin0", sb["wc"][:, 0:4096])
        gu_slots = [(5120, 6144), (8192, 9216)]
        gtoks = [None] * 32
        gmarks = [None] * 32

        def gu_load(ic):
            sg, su = gu_slots[ic % 2]
            p.dma("gpsimd", sb["wc"][:, sg:sg + 1024],
                  wgv[e, :, :, ic * 128:(ic + 1) * 128])
            gtoks[ic] = p.dma("gpsimd", sb["wc"][:, su:su + 1024],
                              wuv[e, :, :, ic * 128:(ic + 1) * 128])
        gu_load(0)
        for ic in range(32):
            if ic + 1 < 32:
                if ic >= 1:
                    p.wait("gpsimd", gmarks[ic - 1])
                gu_load(ic + 1)
            sg, su = gu_slots[ic % 2]
            p.wait("tensor", gtoks[ic])
            for kh in range(8):
                TE(lambda t, kh=kh, sg=sg: t.matmul(
                    ps[2], sb["wc"][:, sg:sg + 1024].rearrange("q (kh w) -> q kh w", kh=8)[:, kh, :],
                    sb["wc"].rearrange("q (dc s) -> q dc s", dc=32)[:, kh, :],
                    start=(kh == 0), stop=(kh == 7)))
            for kh in range(8):
                TE(lambda t, kh=kh, su=su: t.matmul(
                    ps[3], sb["wc"][:, su:su + 1024].rearrange("q (kh w) -> q kh w", kh=8)[:, kh, :],
                    sb["wc"].rearrange("q (dc s) -> q dc s", dc=32)[:, kh, :],
                    start=(kh == 0), stop=(kh == 7)))
            gmarks[ic] = t_te = p.mark("tensor")
            p.wait("scalar", t_te)
            AC(lambda s: s.activation(sb["gb"][:, 2048:2560], ps[2], AT.Silu))
            t_ac = p.mark("scalar")
            p.wait("vector", t_ac)
            VE(lambda v, ic=ic: v.tensor_tensor(
                sb["wa"].rearrange("q (ic s) -> q ic s", ic=32)[:, ic, :],
                sb["gb"][:, 2048:2560], ps[3], OP.mult))
            t_ve = p.mark("vector")
            p.wait("tensor", t_ve)
        if e == 0:
            tapout("hi0", sb["wa"][:, 0:16384])
        dn_slots = [10240, 11264]
        dtoks = [None] * 32
        dmarks = [None] * 32
        dtoks[0] = p.dma("gpsimd", sb["wc"][:, 10240:11264], wdv[e, 0, :, :])
        for ic in range(32):
            if ic + 1 < 32:
                if ic >= 1:
                    p.wait("gpsimd", dmarks[ic - 1])
                sl = dn_slots[(ic + 1) % 2]
                dtoks[ic + 1] = p.dma("gpsimd", sb["wc"][:, sl:sl + 1024],
                                      wdv[e, ic + 1, :, :])
            sl = dn_slots[ic % 2]
            p.wait("tensor", dtoks[ic])
            for sc in range(4):
                for dh2 in range(2):
                    TE(lambda t, ic=ic, sc=sc, dh2=dh2, sl=sl: t.matmul(
                        ps[sc * 2 + dh2],
                        sb["wa"].rearrange("q (ic s) -> q ic s", ic=32)[:, ic, sc * 128:(sc + 1) * 128],
                        sb["wc"][:, sl + dh2 * 512:sl + (dh2 + 1) * 512],
                        start=(ic == 0), stop=(ic == 31)))
            dmarks[ic] = p.mark("tensor")
        t_te = p.mark("tensor")
        p.wait("scalar", t_te)
        for sc in range(4):
            for dh2 in range(2):
                AC(lambda s, sc=sc, dh2=dh2: s.copy(
                    sb["gb"][:, sc * 1024 + dh2 * 512: sc * 1024 + (dh2 + 1) * 512],
                    ps[sc * 2 + dh2]))
        t_ac = p.mark("scalar")
        p.wait("tensor", t_ac)
        p.wait("gpsimd", t_ac)
        p.wait("sync", t_ac)
        tok = p.dma("sync", xe_img[e], sb["gb"][:, 0:4096])
        p.wait("vector", tok)
        p.wait("scalar", tok)
        p.wait("gpsimd", tok)
        if e == 0:
            tapout("xe0", sb["gb"][:, 0:4096])
    p.barrier()

    wcf16 = sb["wc"].bitcast(F16)
    fbb = sb["fb"].bitcast(BF16)
    dtimg = A["dt_fm"].rearrange("q (dh t) -> q dh t", dh=8)
    xcimg = A["xconv_fm"].rearrange("q (dh t) -> q dh t", dh=8)
    szimg = A["siluz_fm"].rearrange("q (dh t) -> q dh t", dh=8)
    spimg = A["spre_fm"].rearrange("q (dh t) -> q dh t", dh=8)
    DTS = [0, 1024]
    UDS = [2048, 3072]
    BBS = [4096, 5120]
    SZS = 6144
    SPS = 7168
    for th in range(2):
        prev_ac = None
        for q in range(16):
            for hh in range(2):
                lo = th * 1024 + hh * 512
                TE(lambda t, q=q, lo=lo: t.matmul(
                    ps[6], sb["Ssel3"][:, q * 128:(q + 1) * 128],
                    fbb[0:64, lo:lo + 512], start=True, stop=True))
                TE(lambda t, q=q, lo=lo: t.matmul(
                    ps[7], sb["Ssel3"][:, q * 128:(q + 1) * 128],
                    fbb[0:64, 2048 + lo:2048 + lo + 512], start=True, stop=True))
                t_te = p.mark("tensor")
                p.wait("scalar", t_te)
                AC(lambda s, q=q, hh=hh: s.copy(
                    fmv(sb["wa"], 16)[:, q, hh * 512:(hh + 1) * 512], ps[6]))
                AC(lambda s, q=q, hh=hh: s.copy(
                    fmv(sb["wb"], 16)[:, q, hh * 512:(hh + 1) * 512], ps[7]))
                prev_ac = p.mark("scalar")
                p.wait("tensor", prev_ac)
        p.wait("vector", prev_ac)
        p.wait("sync", prev_ac)
        prev_ep = None
        prev_sp_tok = None
        for g in range(32):
            dh, dl0 = g // 4, (g % 4) * 32
            gs = g % 2
            tsl = slice(th * 1024, (th + 1) * 1024)
            if prev_ep is not None:
                p.wait("sync", prev_ep)
            for pg in range(4):
                p.dma("sync", wcf16[pg * 32:(pg + 1) * 32, DTS[gs]:DTS[gs] + 1024],
                      dtimg[dl0:dl0 + 32, dh, tsl])
                p.dma("sync", sb["wc"][pg * 32:(pg + 1) * 32, UDS[gs]:UDS[gs] + 1024],
                      xcimg[dl0:dl0 + 32, dh, tsl])
            tok_dup = p.dma("sync", sb["wc"][0:32, SZS:SZS + 1024],
                            szimg[dl0:dl0 + 32, dh, tsl])
            p.wait("scalar", tok_dup)
            p.wait("vector", tok_dup)
            for qp in range(8):
                q0, q1 = 2 * qp, 2 * qp + 1
                AC(lambda s, q=q0, gs=gs: s.activation(
                    sb["fa"][:, 0:1024], wcf16[:, DTS[gs]:DTS[gs] + 1024],
                    AT.Exp, scale=sb["A_cm"][:, q:q + 1]), nd=(qp > 0))
                AC(lambda s, q=q1, gs=gs: s.activation(
                    sb["fa"][:, 1024:2048], wcf16[:, DTS[gs]:DTS[gs] + 1024],
                    AT.Exp, scale=sb["A_cm"][:, q:q + 1]), nd=True)
                t_ac = p.mark("scalar")
                VE(lambda v, q=q0, gs=gs: v.tensor_tensor(
                    sb["wc"][:, BBS[0]:BBS[0] + 1024],
                    sb["wc"][:, UDS[gs]:UDS[gs] + 1024],
                    fmv(sb["wa"], 16)[:, q, :], OP.mult))
                VE(lambda v, q=q1, gs=gs: v.tensor_tensor(
                    sb["wc"][:, BBS[1]:BBS[1] + 1024],
                    sb["wc"][:, UDS[gs]:UDS[gs] + 1024],
                    fmv(sb["wa"], 16)[:, q, :], OP.mult), nd=True)
                p.wait("vector", t_ac)
                for qq, asl in ((q0, 0), (q1, 1024)):
                    hbuf = sb["ga"] if qq < 8 else sb["gb"]
                    hsl = (qq % 8) * 1024
                    VE(lambda v, q=qq, g=g, asl=asl, hbuf=hbuf, hsl=hsl:
                       v.tensor_tensor_scan(
                           hbuf[:, hsl:hsl + 1024], sb["fa"][:, asl:asl + 1024],
                           sb["wc"][:, BBS[asl // 1024]:BBS[asl // 1024] + 1024],
                           sb["carry"][:, g * 16 + q:g * 16 + q + 1],
                           OP.mult, OP.add), nd=(asl == 1024))
                t_ve = p.mark("vector")
                p.wait("scalar", t_ve)
            VE(lambda v, g=g: v.tensor_copy(
                sb["carry"][:, g * 16:g * 16 + 8],
                sb["ga"].rearrange("q (a w) -> q a w", a=8)[:, :, 1023]))
            VE(lambda v, g=g: v.tensor_copy(
                sb["carry"][:, g * 16 + 8:g * 16 + 16],
                sb["gb"].rearrange("q (a w) -> q a w", a=8)[:, :, 1023]), nd=True)
            for q in range(16):
                hbuf = sb["ga"] if q < 8 else sb["gb"]
                hsl = (q % 8) * 1024
                VE(lambda v, q=q, hbuf=hbuf, hsl=hsl: v.tensor_tensor(
                    hbuf[:, hsl:hsl + 1024], hbuf[:, hsl:hsl + 1024],
                    fmv(sb["wb"], 16)[:, q, :], OP.mult), nd=(q > 0))
            t_ve = p.mark("vector")
            p.wait("tensor", t_ve)
            for ti in range(2):
                for q in range(16):
                    hbuf = sb["ga"] if q < 8 else sb["gb"]
                    hsl = (q % 8) * 1024
                    TE(lambda t, q=q, ti=ti, hbuf=hbuf, hsl=hsl: t.matmul(
                        ps[ti][0:32, :], sb["Ssum"],
                        hbuf[:, hsl + ti * 512: hsl + (ti + 1) * 512],
                        start=(q == 0), stop=(q == 15)))
            t_te = p.mark("tensor")
            p.wait("vector", t_te)
            if prev_sp_tok is not None:
                p.wait("vector", prev_sp_tok)
            for ti in range(2):
                VE(lambda v, ti=ti, gs=gs, dl0=dl0, dh=dh: v.tensor_scalar(
                    sb["fa"][0:32, 2048 + ti * 512: 2048 + (ti + 1) * 512],
                    sb["wc"][0:32, UDS[gs] + ti * 512: UDS[gs] + (ti + 1) * 512],
                    sb["Dp"][dl0:dl0 + 32, dh:dh + 1], None, OP.mult), nd=(ti == 1))
            for ti in range(2):
                VE(lambda v, ti=ti: v.tensor_tensor(
                    sb["fa"][0:32, 2048 + ti * 512: 2048 + (ti + 1) * 512],
                    sb["fa"][0:32, 2048 + ti * 512: 2048 + (ti + 1) * 512],
                    ps[ti][0:32, :], OP.add), nd=(ti == 1))
            for ti in range(2):
                VE(lambda v, ti=ti: v.tensor_tensor(
                    sb["wc"][0:32, SPS + ti * 512: SPS + (ti + 1) * 512],
                    sb["fa"][0:32, 2048 + ti * 512: 2048 + (ti + 1) * 512],
                    sb["wc"][0:32, SZS + ti * 512: SZS + (ti + 1) * 512], OP.mult), nd=(ti == 1))
            prev_ep = p.mark("vector")
            p.wait("tensor", prev_ep)
            p.wait("sync", prev_ep)
            prev_sp_tok = p.dma("sync", spimg[dl0:dl0 + 32, dh, tsl],
                                sb["wc"][0:32, SPS:SPS + 1024])
    p.wait("sync", prev_sp_tok)
    p.barrier()
    tapout_d("spre", A["spre_fm"])

    tok_wo = p.dma("gpsimd", sb["wb"][:, 0:8192],
                   A["WoutT"].rearrange("(kh q) o -> q kh o", q=128))
    wov = sb["wb"][:, 0:8192].rearrange("q (kh o) -> q kh o", kh=8)
    xeimg2 = A["xe_tm"].rearrange("(e q) (sh w) -> e q sh w", q=128, sh=4)
    x1img = A["x1_fm"].rearrange("q (dh t) -> q dh t", dh=8)
    xnnimg = A["xnn1_fm"].rearrange("q (dh t) -> q dh t", dh=8)
    p.wait("tensor", tok_wo)
    for tt in range(4):
        ts512 = slice(tt * 512, (tt + 1) * 512)
        for e in range(8):
            for tc in range(4):
                ci = tt * 4 + tc
                VE(lambda v, ci=ci, e=e: v.tensor_scalar(
                    sb["fa"].bitcast(BF16)[:, 0:512], sb["iota_s"],
                    sb["slotcol"][:, ci * 8 + e:ci * 8 + e + 1],
                    sb["affcol"][:, ci * 8 + e:ci * 8 + e + 1],
                    OP.is_equal, OP.mult))
                t_ve = p.mark("vector")
                p.wait("tensor", t_ve)
                for sh in range(4):
                    TE(lambda t, sh=sh: t.transpose(
                        ps[6].bitcast(BF16)[:, 0:128],
                        sb["fa"].bitcast(BF16)[:, sh * 128:(sh + 1) * 128],
                        sb["eye"]))
                    t_te = p.mark("tensor")
                    p.wait("scalar", t_te)
                    dstb = sb["ga"] if e < 4 else sb["wb"][:, 8192:16384]
                    zi = (e * 4 + sh) % 16
                    AC(lambda s, dstb=dstb, zi=zi, tc=tc: s.copy(
                        dstb.rearrange("q (z w) -> q z w", z=16)
                        [:, zi, tc * 128:(tc + 1) * 128],
                        ps[6].bitcast(BF16)[:, 0:128]))
                    t_ac = p.mark("scalar")
                    p.wait("tensor", t_ac)
                    p.wait("vector", t_ac)
        if tt == 0 and "p2g" in taps:
            tapout("p2g", sb["ga"])
            tapout("affcol", sb["affcol"])
        if tt > 0:
            p.wait("sync", tok_xnn)
        tok_sp = p.dma("sync", sb["wc"][:, 0:4096],
                       A["spre_fm"].rearrange("q (dh t) -> q dh t", dh=8)[:, :, ts512])
        tok_xn2 = p.dma("sync", sb["wc"][:, 4096:8192],
                        A["xn_fm"].rearrange("q (dh t) -> q dh t", dh=8)[:, :, ts512])
        tok_x = p.dma("sync", fmv(sb["fb"]), xTv[:, :, ts512])
        p.wait("tensor", tok_sp)
        p.wait("vector", tok_x)
        dbuf = "gmoe" not in taps

        def st_load(dc2):
            base = 4096 if (dbuf and dc2 % 2) else 0
            tk = None
            for e2 in range(8):
                tk = p.dma("sync", sb["gb"][:, base + e2 * 512:base + (e2 + 1) * 512],
                           xeimg2[e2, :, :, dc2 * 128:(dc2 + 1) * 128])
            return tk
        if tt > 0:
            p.wait("sync", smarks[7])
        stoks = [None] * 8
        smarks = [None] * 8
        if dbuf:
            stoks[0] = st_load(0)
        for dc in range(8):
            bank_g = dc % 2
            bank_s = 2 + dc % 2
            if dbuf:
                if dc + 1 < 8:
                    if dc >= 1:
                        p.wait("sync", smarks[dc - 1])
                    stoks[dc + 1] = st_load(dc + 1)
            else:
                if dc >= 1:
                    p.wait("sync", smarks[dc - 1])
                stoks[dc] = st_load(dc)
            base = 4096 if (dbuf and dc % 2) else 0
            p.wait("tensor", stoks[dc])
            for e in range(8):
                for sh in range(4):
                    srcb = sb["ga"] if e < 4 else sb["wb"][:, 8192:16384]
                    zi = (e * 4 + sh) % 16
                    TE(lambda t, srcb=srcb, zi=zi, bank_g=bank_g, e=e, sh=sh, base=base: t.matmul(
                        ps[bank_g],
                        sb["gb"][:, base + e * 512 + sh * 128:base + e * 512 + (sh + 1) * 128],
                        srcb.rearrange("q (z w) -> q z w", z=16)[:, zi, :],
                        start=(e == 0 and sh == 0), stop=(e == 7 and sh == 3)))
            smarks[dc] = p.mark("tensor")
            for kh in range(8):
                TE(lambda t, kh=kh, dc=dc, bank_s=bank_s: t.matmul(
                    ps[bank_s], wov[:, kh, dc * 128:(dc + 1) * 128],
                    sb["wc"][:, 0:4096].rearrange("q (kh w) -> q kh w", kh=8)[:, kh, :],
                    start=(kh == 0), stop=(kh == 7)))
            t_te = p.mark("tensor")
            p.wait("scalar", t_te)
            p.wait("vector", t_te)
            if "gmoe" in taps:
                AC(lambda s, bank_g=bank_g, dc=dc: s.copy(
                    sb["wc"].rearrange("q (z w) -> q z w", z=32)[:, 16 + dc, :],
                    ps[bank_g]))
            AC(lambda s, bank_g=bank_g: s.activation(
                sb["fa"][:, 0:512], ps[bank_g], AT.Silu))
            t_ac = p.mark("scalar")
            p.wait("vector", t_ac)
            VE(lambda v, dc=dc, bank_s=bank_s: v.tensor_tensor(
                sb["fa"][:, 512:1024], ps[bank_s],
                sb["wc"][:, 4096:8192].rearrange("q (kh w) -> q kh w", kh=8)[:, dc, :],
                OP.add))
            VE(lambda v: v.tensor_tensor(
                sb["fa"][:, 512:1024], sb["fa"][:, 512:1024], sb["fa"][:, 0:512],
                OP.mult))
            VE(lambda v, dc=dc: v.tensor_tensor(
                sb["fa"].rearrange("q (z w) -> q z w", z=16)[:, 8 + dc, :],
                fmv(sb["fb"])[:, dc, :], sb["fa"][:, 512:1024], OP.add))
            VE(lambda v, dc=dc: v.tensor_tensor(
                sb["fa"][:, 0:512],
                sb["fa"].rearrange("q (z w) -> q z w", z=16)[:, 8 + dc, :],
                sb["fa"].rearrange("q (z w) -> q z w", z=16)[:, 8 + dc, :], OP.mult))
            t_ve = p.mark("vector")
            p.wait("tensor", t_ve)
            TE(lambda t, dc=dc: t.matmul(
                ps[4][0:1, :], sb["ones_c"],
                sb["fa"].rearrange("q (z w) -> q z w", z=16)[:, 8 + dc, :],
                start=(dc == 0), stop=(dc == 7)))
            TE(lambda t, dc=dc: t.matmul(
                ps[5][0:1, :], sb["ones_c"], sb["fa"][:, 0:512],
                start=(dc == 0), stop=(dc == 7)))
            t_te = p.mark("tensor")
            p.wait("vector", t_te)
            p.wait("scalar", t_te)
        if "gmoe" in taps:
            t_g = p.mark("scalar")
            p.wait("sync", t_g)
            p.dma("sync",
                  A["tap_gmoe"].rearrange("q (dh t) -> q dh t", dh=8)[:, :, ts512],
                  sb["wc"].rearrange("q (z w) -> q z w", z=32)[:, 16:24, :])
        p.wait("sync", t_ve)
        tok_x1 = p.dma("sync", x1img[:, :, ts512], sb["fa"][:, 4096:8192])
        r1, r2, r3 = sb["row1"], sb["row2"], sb["row3"]
        VE(lambda v: v.tensor_scalar(r1, ps[4][0:1, :], 1.0 / D, None, OP.mult))
        VE(lambda v: v.tensor_scalar(r2, ps[5][0:1, :], 1.0 / D, None, OP.mult))
        VE(lambda v: v.tensor_tensor(r3, r1, r1, OP.mult))
        VE(lambda v: v.tensor_tensor(r2, r2, r3, OP.subtract))
        t_ve = p.mark("vector")
        p.wait("scalar", t_ve)
        AC(lambda s: s.activation(r3, r2, AT.Sqrt, bias=sb["eps_c"][0:1, :]))
        t_ac = p.mark("scalar")
        p.wait("vector", t_ac)
        VE(lambda v: v.reciprocal(r3, r3))
        VE(lambda v: v.tensor_tensor(r2, r2, r3, OP.mult))
        VE(lambda v: v.tensor_tensor(r2, r2, r3, OP.mult))
        t_ve = p.mark("vector")
        p.wait("scalar", t_ve)
        AC(lambda s: s.activation(r2, r2, AT.Sqrt, bias=sb["eps_c"][0:1, :]))
        t_ac = p.mark("scalar")
        p.wait("vector", t_ac)
        VE(lambda v: v.reciprocal(r2, r2))
        VE(lambda v: v.tensor_tensor(r2, r2, r3, OP.mult))
        t_ve = p.mark("vector")
        p.wait("tensor", t_ve)
        TE(lambda t: t.matmul(ps[6], sb["ones_r1f"][0:1, 0:128], r1, start=True, stop=True))
        TE(lambda t: t.matmul(ps[7], sb["ones_r1f"][0:1, 0:128], r3, start=True, stop=True))
        TE(lambda t: t.matmul(ps[4], sb["ones_r1f"][0:1, 0:128], r2, start=True, stop=True))
        t_te = p.mark("tensor")
        p.wait("vector", t_te)
        for dc in range(8):
            x1s = sb["fa"].rearrange("q (z w) -> q z w", z=16)[:, 8 + dc, :]
            VE(lambda v, x1s=x1s: v.tensor_tensor(sb["fa"][:, 0:512], x1s, ps[6], OP.subtract))
            VE(lambda v, dc=dc, ts512=ts512: v.tensor_tensor(
                fmv(sb["wa"])[:, dc, ts512], sb["fa"][:, 0:512], ps[7], OP.mult))
            VE(lambda v: v.tensor_tensor(sb["fa"][:, 512:1024], sb["fa"][:, 0:512], ps[4], OP.mult))
            VE(lambda v, dc=dc: v.tensor_scalar(
                sb["wc"].rearrange("q (z w) -> q z w", z=32)[:, dc, :],
                sb["fa"][:, 512:1024],
                sb["ln1_w"][:, dc:dc + 1], sb["ln1_b"][:, dc:dc + 1], OP.mult, OP.add))
        t_ve = p.mark("vector")
        p.wait("sync", t_ve)
        p.wait("tensor", t_ve)
        tok_xnn = p.dma("sync", xnnimg[:, :, ts512], sb["wc"][:, 0:4096])
        p.wait("vector", tok_xnn)
        p.wait("scalar", tok_xnn)
    p.barrier()
    tapout_d("x1", A["x1_fm"])
    tapout("xn1", sb["wa"])
    tapout_d("xnn1", A["xnn1_fm"])

    fab = sb["fa"].bitcast(BF16)
    fbb16 = sb["fb"].bitcast(BF16)
    for wi, wname in enumerate(["WrT", "WkT", "WvT"]):
        tok_w = p.dma("gpsimd", fab[:, 0:8192],
                      A[wname].rearrange("(kh q) o -> q kh o", q=128))
        p.wait("tensor", tok_w)
        wv = fab[:, 0:8192].rearrange("q (kh o) -> q kh o", kh=8)
        for tt in range(4):
            ts512 = slice(tt * 512, (tt + 1) * 512)
            tok_xnn = p.dma("sync", fbb16[:, 0:4096], xnnimg[:, :, ts512])
            p.wait("tensor", tok_xnn)
            for oc in range(8):
                bank = oc % 2
                for kh in range(8):
                    TE(lambda t, oc=oc, kh=kh, bank=bank: t.matmul(
                        ps[bank], wv[:, kh, oc * 128:(oc + 1) * 128],
                        fbb16[:, 0:4096].rearrange("q (kh w) -> q kh w", kh=8)[:, kh, :],
                        start=(kh == 0), stop=(kh == 7)))
                t_te = p.mark("tensor")
                p.wait("scalar", t_te)
                if wi == 0:
                    AC(lambda s, oc=oc, ts512=ts512, bank=bank: s.activation(
                        fmv(sb["wb"])[:, oc, ts512], ps[bank], AT.Sigmoid))
                elif wi == 1:
                    AC(lambda s, oc=oc, ts512=ts512, bank=bank: s.copy(
                        fmv(sb["wc"].bitcast(F16))[:, oc, ts512], ps[bank]))
                else:
                    AC(lambda s, oc=oc, ts512=ts512, bank=bank: s.copy(
                        (fmv(sb["ga"], 4)[:, oc, ts512] if oc < 4 else
                         fmv(sb["gb"], 4)[:, oc - 4, ts512]), ps[bank]))
                t_ac = p.mark("scalar")
                p.wait("tensor", t_ac)
            t_ac = p.mark("scalar")
            p.wait("sync", t_ac)
        t_mm = p.mark("tensor")
        p.wait("gpsimd", t_mm)
    p.barrier()
    tapout("rr", sb["wb"])
    tapout("kk_", sb["wc"].bitcast(F16))
    tapout("vva", sb["ga"])
    tapout("vvb", sb["gb"])

    VE(lambda v: v.memset(sb["fb"][:, 2048:2049], 0.0))
    VE(lambda v: v.memset(sb["fa"][:, 6144:6145], 0.0))
    for ch in range(8):
        vsl = (fmv(sb["ga"], 4)[:, ch, :] if ch < 4 else
               fmv(sb["gb"], 4)[:, ch - 4, :])
        ksl = fmv(sb["wc"].bitcast(F16))[:, ch, :]
        VE(lambda v: v.memset(sb["fa"][:, 6144:8192], 1.0))
        VE(lambda v, ch=ch: v.tensor_scalar(
            sb["fa"][:, 6144:8192], sb["fa"][:, 6144:8192],
            sb["ew"][:, ch:ch + 1], None, OP.mult))
        for dr in range(2):
            t_vp = p.mark("vector")
            p.wait("scalar", t_vp)
            AC(lambda s, ksl=ksl: s.activation(sb["fa"][:, 0:2048], ksl, AT.Exp))
            t_ac = p.mark("scalar")
            p.wait("vector", t_ac)
            VE(lambda v, vsl=vsl: v.tensor_tensor(
                sb["fa"][:, 2048:4096], sb["fa"][:, 0:2048], vsl, OP.mult))
            ewb = sb["fa"][:, 6144:8192]
            if dr == 0:
                nout = sb["fb"][:, 0:2048]
                dout = sb["fa"][:, 4096:6144]
                bnum = sb["fa"][:, 2048:4096]
                bden = sb["fa"][:, 0:2048]
            else:
                nout = bass.AP(sb["fb"].tensor, sb["fb"].offset + 2047,
                               [list(sb["fb"].ap[0]), [-1, 2048]])
                dout = bass.AP(sb["fa"].tensor, sb["fa"].offset + 4096 + 2047,
                               [list(sb["fa"].ap[0]), [-1, 2048]])
                bnum = bass.AP(sb["fa"].tensor, sb["fa"].offset + 2048 + 2047,
                               [list(sb["fa"].ap[0]), [-1, 2048]])
                bden = bass.AP(sb["fa"].tensor, sb["fa"].offset + 2047,
                               [list(sb["fa"].ap[0]), [-1, 2048]])
            if ch == 0 and dr == 0:
                tapout("p6a", sb["fa"])
            VE(lambda v, nout=nout, bnum=bnum, ewb=ewb: v.tensor_tensor_scan(
                nout, ewb, bnum, 0.0, OP.mult, OP.add))
            VE(lambda v, dout=dout, bden=bden, ewb=ewb: v.tensor_tensor_scan(
                dout, ewb, bden, 0.0, OP.mult, OP.add))
            if ch == 0 and dr == 0:
                tapout("p6b", sb["fb"])
                tapout("p6c", sb["fa"])
            eucol = sb["eu"][:, ch:ch + 1]
            VE(lambda v, eucol=eucol: v.tensor_scalar(
                sb["fa"][:, 2048:4096], sb["fa"][:, 2048:4096], eucol, None,
                OP.mult))
            VE(lambda v, eucol=eucol: v.tensor_scalar(
                sb["fa"][:, 0:2048], sb["fa"][:, 0:2048], eucol, None,
                OP.mult))
            if dr == 0:
                VE(lambda v: v.tensor_tensor(
                    sb["fa"][:, 2049:4096], sb["fa"][:, 2049:4096],
                    sb["fb"][:, 0:2047], OP.add))
                VE(lambda v: v.tensor_tensor(
                    sb["fa"][:, 1:2048], sb["fa"][:, 1:2048],
                    sb["fa"][:, 4096:6143], OP.add))
            else:
                VE(lambda v: v.tensor_tensor(
                    sb["fa"][:, 2048:4095], sb["fa"][:, 2048:4095],
                    sb["fb"][:, 1:2048], OP.add))
                VE(lambda v: v.tensor_tensor(
                    sb["fa"][:, 0:2047], sb["fa"][:, 0:2047],
                    sb["fa"][:, 4097:6144], OP.add))
            VE(lambda v: v.reciprocal(sb["fa"][:, 0:2048], sb["fa"][:, 0:2048]))
            if dr == 0:
                VE(lambda v: v.tensor_tensor(
                    sb["fb"][:, 2048:4096], sb["fa"][:, 2048:4096],
                    sb["fa"][:, 0:2048], OP.mult))
            else:
                VE(lambda v: v.tensor_tensor(
                    sb["fa"][:, 4096:6144], sb["fa"][:, 2048:4096],
                    sb["fa"][:, 0:2048], OP.mult))
        if ch == 0:
            tapout("p6d", sb["fb"])
            tapout("p6e", sb["fa"])
        VE(lambda v: v.tensor_tensor(
            sb["fa"][:, 4096:6144], sb["fa"][:, 4096:6144],
            sb["fb"][:, 2048:4096], OP.add))
        VE(lambda v: v.tensor_scalar(
            sb["fa"][:, 4096:6144], sb["fa"][:, 4096:6144], 0.5, None, OP.mult))
        VE(lambda v, ch=ch: v.tensor_tensor(
            fmv(sb["wb"])[:, ch, :], fmv(sb["wb"])[:, ch, :],
            sb["fa"][:, 4096:6144], OP.mult))
    t_ve = p.mark("vector")
    p.barrier()
    tapout("rw", sb["wb"])

    tok_wo2 = p.dma("gpsimd", fab[:, 0:8192],
                    A["WoT"].rearrange("(kh q) o -> q kh o", q=128))
    p.wait("tensor", tok_wo2)
    wov2 = fab[:, 0:8192].rearrange("q (kh o) -> q kh o", kh=8)
    xmimg = A["xmid_fm"].rearrange("q (dh t) -> q dh t", dh=8)
    for tt in range(4):
        ts512 = slice(tt * 512, (tt + 1) * 512)
        for dc in range(8):
            bank = dc % 2
            for kh in range(8):
                TE(lambda t, dc=dc, kh=kh, bank=bank, ts512=ts512: t.matmul(
                    ps[bank], wov2[:, kh, dc * 128:(dc + 1) * 128],
                    fmv(sb["wb"])[:, kh, ts512], start=(kh == 0), stop=(kh == 7)))
            t_te = p.mark("tensor")
            p.wait("vector", t_te)
            VE(lambda v, dc=dc, ts512=ts512, bank=bank: v.tensor_tensor(
                fmv(sb["fb"])[:, dc, :], ps[bank], fmv(sb["wa"])[:, dc, ts512],
                OP.add))
            VE(lambda v, dc=dc: v.tensor_tensor(
                sb["fa"][:, 4096:4608], fmv(sb["fb"])[:, dc, :],
                fmv(sb["fb"])[:, dc, :], OP.mult))
            t_ve = p.mark("vector")
            p.wait("tensor", t_ve)
            TE(lambda t, dc=dc: t.matmul(
                ps[4][0:1, :], sb["ones_c"], fmv(sb["fb"])[:, dc, :],
                start=(dc == 0), stop=(dc == 7)))
            TE(lambda t, dc=dc: t.matmul(
                ps[5][0:1, :], sb["ones_c"], sb["fa"][:, 4096:4608],
                start=(dc == 0), stop=(dc == 7)))
            t_te = p.mark("tensor")
            p.wait("vector", t_te)
            p.wait("scalar", t_te)
        p.wait("sync", t_ve)
        tok_xm = p.dma("sync", xmimg[:, :, ts512], sb["fb"])
        p.wait("vector", tok_xm)
        r1, r2, r3 = sb["row1"], sb["row2"], sb["row3"]
        VE(lambda v: v.tensor_scalar(r1, ps[4][0:1, :], 1.0 / D, None, OP.mult))
        VE(lambda v: v.tensor_scalar(r2, ps[5][0:1, :], 1.0 / D, None, OP.mult))
        VE(lambda v: v.tensor_tensor(r3, r1, r1, OP.mult))
        VE(lambda v: v.tensor_tensor(r2, r2, r3, OP.subtract))
        t_ve = p.mark("vector")
        p.wait("scalar", t_ve)
        AC(lambda s: s.activation(r3, r2, AT.Sqrt, bias=sb["eps_c"][0:1, :]))
        t_ac = p.mark("scalar")
        p.wait("vector", t_ac)
        VE(lambda v: v.reciprocal(r3, r3))
        t_ve = p.mark("vector")
        p.wait("tensor", t_ve)
        TE(lambda t: t.matmul(ps[6], sb["ones_r1f"][0:1, 0:128], r1, start=True, stop=True))
        TE(lambda t: t.matmul(ps[7], sb["ones_r1f"][0:1, 0:128], r3, start=True, stop=True))
        t_te = p.mark("tensor")
        p.wait("vector", t_te)
        for dc in range(8):
            VE(lambda v, dc=dc: v.tensor_tensor(
                sb["fa"][:, 4096:4608], fmv(sb["fb"])[:, dc, :], ps[6], OP.subtract))
            VE(lambda v: v.tensor_tensor(
                sb["fa"][:, 4608:5120], sb["fa"][:, 4096:4608], ps[7], OP.mult))
            VE(lambda v, dc=dc, ts512=ts512: v.tensor_scalar(
                fmv(sb["wb"])[:, dc, ts512], sb["fa"][:, 4608:5120],
                sb["ln2_w"][:, dc:dc + 1], sb["ln2_b"][:, dc:dc + 1],
                OP.mult, OP.add))
        t_ve = p.mark("vector")
        p.wait("tensor", t_ve)
        p.wait("sync", t_ve)
    p.barrier()
    tapout_d("xmid", A["xmid_fm"])
    tapout("xn2", sb["wb"])

    for wname, dstname, nch, cw in (("WfcT", "WqfcT_d", 8, 4096),
                                    ("WprojT", "WqprojT_d", 32, 1024)):
        VE(lambda v: v.memset(sb["fa"][:, 7168:7169], 0.0))
        t_ve = p.mark("vector")
        p.wait("sync", t_ve)
        wimg = A[wname].rearrange("(n q) o -> n q o", q=128)
        prev = None
        for n in range(nch):
            tok = p.dma("sync", sb["fa"][:, 0:cw], wimg[n])
            p.wait("vector", tok)
            VE(lambda v, cw=cw: v.tensor_reduce(
                sb["fa"][:, 7169:7170], sb["fa"][:, 0:cw], mybir.AxisListType.X,
                OP.max, apply_absolute_value=True))
            VE(lambda v: v.tensor_tensor(
                sb["fa"][:, 7168:7169], sb["fa"][:, 7168:7169],
                sb["fa"][:, 7169:7170], OP.max))
            prev = p.mark("vector")
            p.wait("sync", prev)
        p.wait("tensor", prev)
        TE(lambda t: t.transpose(ps[6][0:1, 0:128], sb["fa"][:, 7168:7169],
                                 sb["eyef"]))
        t_te = p.mark("tensor")
        p.wait("vector", t_te)
        VE(lambda v: v.tensor_reduce(
            sb["fa"][0:1, 7170:7171], ps[6][0:1, 0:128], mybir.AxisListType.X,
            OP.max))
        VE(lambda v: v.tensor_scalar(
            sb["fa"][0:1, 7171:7172], sb["fa"][0:1, 7170:7171], 1.0 / 127.0,
            None, OP.mult))
        VE(lambda v: v.reciprocal(sb["fa"][0:1, 7172:7173], sb["fa"][0:1, 7171:7172]))
        t_ve = p.mark("vector")
        p.wait("tensor", t_ve)
        TE(lambda t: t.matmul(ps[6][:, 0:2], sb["ones_r1f"][0:1, 0:128],
                              sb["fa"][0:1, 7171:7173], start=True, stop=True))
        t_te = p.mark("tensor")
        p.wait("vector", t_te)
        VE(lambda v: v.tensor_copy(sb["fa"][:, 7173:7175], ps[6][:, 0:2]))
        p.selfsync("vector")
        for n in range(nch):
            tok = p.dma("sync", sb["fa"][:, 0:cw], wimg[n])
            p.wait("vector", tok)
            VE(lambda v, cw=cw: v.tensor_scalar(
                sb["fa"][:, 0:cw], sb["fa"][:, 0:cw],
                sb["fa"][:, 7174:7175], RNE_C, OP.mult, OP.add))
            VE(lambda v, cw=cw: v.tensor_scalar(
                sb["fa"][:, 0:cw], sb["fa"][:, 0:cw], -RNE_C, None, OP.add))
            VE(lambda v, cw=cw: v.tensor_scalar(
                sb["fa"][:, 0:cw], sb["fa"][:, 0:cw], 127.0, -128.0,
                OP.min, OP.max))
            VE(lambda v, cw=cw: v.tensor_scalar(
                sb["fb"].bitcast(BF16)[:, 0:cw], sb["fa"][:, 0:cw],
                sb["fa"][:, 7173:7174], None, OP.mult))
            t_ve = p.mark("vector")
            p.wait("sync", t_ve)
            tok2 = p.dma("sync",
                         A[dstname].rearrange("(n q) o -> n q o", q=128)[n],
                         sb["fb"].bitcast(BF16)[:, 0:cw])
            p.wait("vector", tok2)
    p.barrier()

    kkimg = A["kk_fm"].rearrange("q (ic t) -> q ic t", ic=32)
    hfimg = A["hf_fm"].rearrange("q (ic t) -> q ic t", ic=32)
    fktoks = [None] * 32
    fkmarks = [None] * 32

    def fk_load(ic2):
        return p.dma("gpsimd", fab[:, (ic2 % 2) * 1024:(ic2 % 2) * 1024 + 1024],
                     A["WfkT"].rearrange("(kh q) o -> q kh o", q=128)[:, :, ic2 * 128:(ic2 + 1) * 128])
    fktoks[0] = fk_load(0)
    for ic in range(32):
        slot = (ic % 2) * 1024
        if ic + 1 < 32:
            if ic >= 1:
                p.wait("gpsimd", fkmarks[ic - 1])
            fktoks[ic + 1] = fk_load(ic + 1)
        p.wait("tensor", fktoks[ic])
        for tt in range(4):
            ts512 = slice(tt * 512, (tt + 1) * 512)
            bank = tt % 2
            for kh in range(8):
                TE(lambda t, kh=kh, bank=bank, slot=slot, ts512=ts512: t.matmul(
                    ps[bank],
                    fab[:, slot:slot + 1024].rearrange("q (kh w) -> q kh w", kh=8)[:, kh, :],
                    fmv(sb["wb"])[:, kh, ts512], start=(kh == 0), stop=(kh == 7)))
            t_te = p.mark("tensor")
            p.wait("vector", t_te)
            VE(lambda v, bank=bank: v.tensor_scalar(
                sb["fa"][:, 2048:2560], ps[bank], 0.0, None, OP.max))
            VE(lambda v, bank=bank: v.tensor_tensor(
                fbb16[:, 0:512], sb["fa"][:, 2048:2560], ps[bank], OP.mult))
            t_ve = p.mark("vector")
            p.wait("tensor", t_ve)
            p.wait("sync", t_ve)
            tok_kk = p.dma("sync", kkimg[:, ic, ts512], fbb16[:, 0:512])
            p.wait("vector", tok_kk)
        fkmarks[ic] = p.mark("tensor")
    p.barrier()

    tok_afc = p.dma("gpsimd", fab[:, 0:128],
                    A["A_fcT"].rearrange("(kh q) o -> q kh o", q=128))
    tok_bfc = p.dma("gpsimd", sb["gb"][32:48, 0:4096], A["B_fcT"])
    p.wait("tensor", tok_bfc)
    for tt in range(4):
        ts512 = slice(tt * 512, (tt + 1) * 512)
        for kh in range(8):
            TE(lambda t, kh=kh, ts512=ts512: t.matmul(
                ps[2][32:48, :],
                fab[:, 0:128].rearrange("q (kh w) -> q kh w", kh=8)[:, kh, :],
                fmv(sb["wa"])[:, kh, ts512], start=(kh == 0), stop=(kh == 7)))
        t_te = p.mark("tensor")
        p.wait("vector", t_te)
        VE(lambda v, ts512=ts512: v.tensor_scalar(
            sb["gb"][32:48, 4096:6144][:, ts512], ps[2][32:48, :], 2.0, None, OP.mult))
        t_ve = p.mark("vector")
        p.wait("tensor", t_ve)
    fctoks = [None] * 32
    fcmarks = [None] * 32

    def fc_load(ic2):
        return p.dma("gpsimd", fab[:, (ic2 % 2) * 1024:(ic2 % 2) * 1024 + 1024],
                     A["WqfcT_d"].rearrange("(kh q) o -> q kh o", q=128)[:, :, ic2 * 128:(ic2 + 1) * 128])
    fctoks[0] = fc_load(0)
    for ic in range(32):
        slot = (ic % 2) * 1024
        if ic + 1 < 32:
            if ic >= 1:
                p.wait("gpsimd", fcmarks[ic - 1])
            fctoks[ic + 1] = fc_load(ic + 1)
        p.wait("tensor", fctoks[ic])
        for tt in range(4):
            ts512 = slice(tt * 512, (tt + 1) * 512)
            bank = tt % 2
            for kh in range(8):
                TE(lambda t, kh=kh, bank=bank, slot=slot, ts512=ts512: t.matmul(
                    ps[bank],
                    fab[:, slot:slot + 1024].rearrange("q (kh w) -> q kh w", kh=8)[:, kh, :],
                    fmv(sb["wa"])[:, kh, ts512], start=(kh == 0), stop=False))
            TE(lambda t, ic=ic, ts512=ts512, bank=bank: t.matmul(
                ps[bank], sb["gb"][32:48, ic * 128:(ic + 1) * 128],
                sb["gb"][32:48, 4096:6144][:, ts512], start=False, stop=True))
            t_te = p.mark("tensor")
            p.wait("scalar", t_te)
            AC(lambda s, ic=ic, bank=bank: s.activation(
                fbb16[:, 0:512], ps[bank], AT.Gelu,
                bias=sb["b_fc_fm"][:, ic:ic + 1]))
            t_ac = p.mark("scalar")
            p.wait("tensor", t_ac)
            p.wait("sync", t_ac)
            tok_h = p.dma("sync", hfimg[:, ic, ts512], fbb16[:, 0:512])
            p.wait("scalar", tok_h)
        fcmarks[ic] = p.mark("tensor")
    p.barrier()

    tok_fr = p.dma("gpsimd", sb["ga"][:, 0:8192],
                   A["WfrT"].rearrange("(kh q) o -> q kh o", q=128))
    tok_bpj = p.dma("gpsimd", sb["gb"][32:48, 4096:5120], A["B_projT"])
    wcf32 = sb["wc"].bitcast(F32)
    outimg = A["out_fm"].rearrange("q (dh t) -> q dh t", dh=8)
    frv = sb["ga"][:, 0:8192].rearrange("q (kh o) -> q kh o", kh=8)
    p.wait("tensor", tok_bpj)
    for tt in range(4):
        ts512 = slice(tt * 512, (tt + 1) * 512)
        if tt > 0:
            p.wait("gpsimd", pjm[31])
            p.wait("sync", pjm[31])
        kvt = [None] * 32
        kvm = [None] * 32

        def kv_load(ih2):
            tg = p.dma("gpsimd", fbb16[:, (ih2 % 2) * 1024:(ih2 % 2) * 1024 + 1024],
                  A["WfvT"].rearrange("(ih q) o -> q ih o", q=128, ih=32)[:, ih2, :])
            return tg, p.dma("sync", fbb16[:, 2048 + (ih2 % 2) * 512:2048 + (ih2 % 2) * 512 + 512],
                         kkimg[:, ih2, ts512])
        kvt[0] = kv_load(0)
        for ih in range(32):
            wslot = (ih % 2) * 1024
            kslot = 2048 + (ih % 2) * 512
            if ih + 1 < 32:
                if ih >= 1:
                    p.wait("gpsimd", kvm[ih - 1])
                    p.wait("sync", kvm[ih - 1])
                kvt[ih + 1] = kv_load(ih + 1)
            p.wait("tensor", kvt[ih][0])
            p.wait("tensor", kvt[ih][1])
            for dc in range(8):
                TE(lambda t, ih=ih, dc=dc, wslot=wslot, kslot=kslot: t.matmul(
                    ps[dc],
                    fbb16[:, wslot:wslot + 1024].rearrange("q (dc w) -> q dc w", dc=8)[:, dc, :],
                    fbb16[:, kslot:kslot + 512],
                    start=(ih == 0), stop=(ih == 31)))
            kvm[ih] = p.mark("tensor")
        p.wait("gpsimd", kvm[31])
        p.wait("sync", kvm[31])
        t_te = p.mark("tensor")
        p.wait("vector", t_te)
        for dc in range(8):
            VE(lambda v, dc=dc: v.tensor_copy(
                sb["fa"].rearrange("q (z w) -> q z w", z=16)[:, dc, :], ps[dc]))
        t_ve = p.mark("vector")
        p.wait("tensor", t_ve)
        for dc in range(8):
            bank = dc % 2
            for kh in range(8):
                TE(lambda t, dc=dc, kh=kh, bank=bank, ts512=ts512: t.matmul(
                    ps[bank], frv[:, kh, dc * 128:(dc + 1) * 128],
                    fmv(sb["wb"])[:, kh, ts512], start=(kh == 0), stop=(kh == 7)))
            t_te = p.mark("tensor")
            p.wait("scalar", t_te)
            AC(lambda s, dc=dc, bank=bank: s.activation(
                sb["wc"].rearrange("q (z w) -> q z w", z=32)[:, 16 + dc, :],
                ps[bank], AT.Sigmoid))
            t_ac = p.mark("scalar")
            p.wait("tensor", t_ac)
        l2t = [None] * 32
        l2m = [None] * 32

        def l2_load(ih2):
            tg = p.dma("gpsimd", fbb16[:, (ih2 % 2) * 1024:(ih2 % 2) * 1024 + 16],
                  A["A_projT"].rearrange("(ih q) o -> q ih o", q=128, ih=32)[:, ih2, :])
            return tg, p.dma("sync", fbb16[:, 2048 + (ih2 % 2) * 512:2048 + (ih2 % 2) * 512 + 512],
                         hfimg[:, ih2, ts512])
        l2t[0] = l2_load(0)
        for ih in range(32):
            wslot = (ih % 2) * 1024
            kslot = 2048 + (ih % 2) * 512
            if ih + 1 < 32:
                if ih >= 1:
                    p.wait("gpsimd", l2m[ih - 1])
                    p.wait("sync", l2m[ih - 1])
                l2t[ih + 1] = l2_load(ih + 1)
            p.wait("tensor", l2t[ih][0])
            p.wait("tensor", l2t[ih][1])
            TE(lambda t, ih=ih, wslot=wslot, kslot=kslot: t.matmul(
                ps[2][32:48, :], fbb16[:, wslot:wslot + 16],
                fbb16[:, kslot:kslot + 512], start=(ih == 0), stop=(ih == 31)))
            l2m[ih] = p.mark("tensor")
        p.wait("gpsimd", l2m[31])
        p.wait("sync", l2m[31])
        t_te = p.mark("tensor")
        p.wait("vector", t_te)
        VE(lambda v: v.tensor_scalar(
            sb["gb"][32:48, 6144:6656], ps[2][32:48, :], 2.0, None, OP.mult))
        t_ve = p.mark("vector")
        p.wait("tensor", t_ve)
        p.wait("gpsimd", t_ve)
        pjt = [None] * 32
        pjm = [None] * 32

        def pj_load(ih2):
            tg = p.dma("gpsimd", fbb16[:, (ih2 % 2) * 1024:(ih2 % 2) * 1024 + 1024],
                  A["WqprojT_d"].rearrange("(ih q) o -> q ih o", q=128, ih=32)[:, ih2, :])
            return tg, p.dma("sync", fbb16[:, 2048 + (ih2 % 2) * 512:2048 + (ih2 % 2) * 512 + 512],
                         hfimg[:, ih2, ts512])
        pjt[0] = pj_load(0)
        for ih in range(32):
            wslot = (ih % 2) * 1024
            kslot = 2048 + (ih % 2) * 512
            if ih + 1 < 32:
                if ih >= 1:
                    p.wait("gpsimd", pjm[ih - 1])
                    p.wait("sync", pjm[ih - 1])
                pjt[ih + 1] = pj_load(ih + 1)
            p.wait("tensor", pjt[ih][0])
            p.wait("tensor", pjt[ih][1])
            for dc in range(8):
                TE(lambda t, ih=ih, dc=dc, wslot=wslot, kslot=kslot: t.matmul(
                    ps[dc],
                    fbb16[:, wslot:wslot + 1024].rearrange("q (dc w) -> q dc w", dc=8)[:, dc, :],
                    fbb16[:, kslot:kslot + 512],
                    start=(ih == 0), stop=False))
            pjm[ih] = p.mark("tensor")
        for dc in range(8):
            TE(lambda t, dc=dc: t.matmul(
                ps[dc], sb["gb"][32:48, 4096 + dc * 128:4096 + (dc + 1) * 128],
                sb["gb"][32:48, 6144:6656], start=False, stop=True))
        t_te = p.mark("tensor")
        p.wait("scalar", t_te)
        tok_xm2 = p.dma("sync", sb["fa"][:, 4096:8192], xmimg[:, :, ts512])
        tok_x12 = p.dma("sync", wcf32[:, 0:4096], x1img[:, :, ts512])
        p.wait("scalar", tok_xm2)
        p.wait("vector", tok_x12)
        for dc in range(8):
            AC(lambda s, dc=dc: s.activation(
                fbb16[:, 3072 + dc * 512:3072 + (dc + 1) * 512], ps[dc], AT.Silu,
                bias=sb["b_proj"][:, dc:dc + 1]))
        t_ac = p.mark("scalar")
        p.wait("vector", t_ac)
        for dc in range(8):
            VE(lambda v, dc=dc: v.tensor_tensor(
                sb["fa"][:, 0:4096].rearrange("q (z w) -> q z w", z=8)[:, dc, :],
                sb["fa"][:, 0:4096].rearrange("q (z w) -> q z w", z=8)[:, dc, :],
                sb["wc"].rearrange("q (z w) -> q z w", z=32)[:, 16 + dc, :],
                OP.mult))
            VE(lambda v, dc=dc: v.tensor_tensor(
                sb["fa"][:, 0:4096].rearrange("q (z w) -> q z w", z=8)[:, dc, :],
                sb["fa"][:, 0:4096].rearrange("q (z w) -> q z w", z=8)[:, dc, :],
                sb["fa"][:, 4096:8192].rearrange("q (z w) -> q z w", z=8)[:, dc, :],
                OP.add))
            VE(lambda v, dc=dc: v.tensor_tensor(
                sb["fa"][:, 0:4096].rearrange("q (z w) -> q z w", z=8)[:, dc, :],
                sb["fa"][:, 0:4096].rearrange("q (z w) -> q z w", z=8)[:, dc, :],
                fbb16[:, 3072 + dc * 512:3072 + (dc + 1) * 512], OP.mult))
            VE(lambda v, dc=dc: v.tensor_tensor(
                wcf32[:, 0:4096].rearrange("q (z w) -> q z w", z=8)[:, dc, :],
                wcf32[:, 0:4096].rearrange("q (z w) -> q z w", z=8)[:, dc, :],
                sb["fa"][:, 0:4096].rearrange("q (z w) -> q z w", z=8)[:, dc, :],
                OP.add))
        t_ve = p.mark("vector")
        p.wait("sync", t_ve)
        p.wait("tensor", t_ve)
        tok_out = p.dma("sync", outimg[:, :, ts512], wcf32[:, 0:4096])
        p.wait("vector", tok_out)
        p.wait("scalar", tok_out)
        p.wait("sync", tok_out)
    p.barrier()
    return sb, ps


TAP_SHAPES = {
    "A_cm": ([128, 16], F32),
    "xn_bf": ([128, 8 * T], BF16),
    "xnn_bf": ([128, 8 * T], BF16),
    "x_main": ([128, 8 * T], BF16),
    "siluz": ([128, 8 * T], BF16),
    "x_conv": ([128, 8 * T], BF16),
    "dt": ([128, 8 * T], F16),
    "Bm": ([64, T], BF16),
    "Cm": ([64, T], BF16),
    "aff": ([8, T], F32),
    "spre": ([128, 8 * T], BF16),
    "x1": ([128, 8 * T], F32),
    "xn1": ([128, 8 * T], BF16),
    "xnn1": ([128, 8 * T], BF16),
    "rr": ([128, 8 * T], BF16),
    "kk_": ([128, 8 * T], F16),
    "rw": ([128, 8 * T], BF16),
    "xmid": ([128, 8 * T], F32),
    "xn2": ([128, 8 * T], BF16),
    "slotx": ([8, T], F32),
    "thr": ([8, 1], F32),
    "xin0": ([128, 8 * 512], BF16),
    "xe0": ([128, 4 * 1024], BF16),
    "gmoe": ([128, 8 * T], BF16),
    "slotcol": ([128, 128], F32),
    "xntm": ([T, D], BF16),
    "hi0": ([128, 16384], BF16),
    "p2g": ([128, 8192], BF16),
    "affcol": ([128, 128], F32),
    "vva": ([128, 8192], BF16),
    "vvb": ([128, 8192], BF16),
    "p6a": ([128, 8192], F32),
    "p6b": ([128, 4096], F32),
    "p6c": ([128, 8192], F32),
    "p6d": ([128, 4096], F32),
    "p6e": ([128, 8192], F32),
}


def build_nc(taps=()):
    nc = bass.Bass("TRN2", target_bir_lowering=False, debug=False)
    A = {}

    def inp(name, shape, dtype=F32):
        A[name] = nc.dram_tensor(name, shape, dtype, kind="ExternalInput").ap()

    def scratch(name, shape, dtype):
        A[name] = nc.dram_tensor(name, shape, dtype, kind="Internal").ap()

    def outp(name, shape, dtype=F32):
        A[name] = nc.dram_tensor(name, shape, dtype, kind="ExternalOutput").ap()

    inp("xT", [D, T])
    inp("WinT", [D, 2 * D])
    inp("conv_wT", [4, D])
    inp("WdtT", [D, D])
    inp("WBT", [D, NST])
    inp("WCT", [D, NST])
    inp("WoutT", [D, D])
    inp("gate_wT", [D, NE])
    inp("wgT", [NE * D, FI])
    inp("wuT", [NE * D, FI])
    inp("wdT", [NE * FI, D])
    inp("WrT", [D, D])
    inp("WkT", [D, D])
    inp("WvT", [D, D])
    inp("WoT", [D, D])
    inp("WfkT", [D, FI])
    inp("WfvT", [FI, D])
    inp("WfrT", [D, D])
    inp("WfcT", [D, FI])
    inp("WprojT", [FI, D])
    inp("A_fcT", [D, 16])
    inp("B_fcT", [16, FI])
    inp("A_projT", [FI, 16])
    inp("B_projT", [16, D])
    for n in ["ln_g", "ln_b", "conv_b", "Dp", "b_dt", "ln1_w", "ln1_b",
              "ln2_w", "ln2_b", "decay_v", "u_v", "b_proj"]:
        inp(n, [1, D])
    inp("b_fc", [1, FI])
    inp("A_log", [1, NST])
    inp("S1c", [NST, 128])
    inp("S2c", [NST, 16])
    inp("Ssum", [128, 32], BF16)
    inp("Ssel3", [NST, 2048], BF16)
    inp("eye", [128, 128], BF16)
    inp("eyef", [128, 128])

    scratch("xn_fm", [128, 8 * T], BF16)
    scratch("xn_tm", [T, D], BF16)
    scratch("siluz_fm", [128, 8 * T], BF16)
    scratch("xconv_fm", [128, 8 * T], BF16)
    scratch("dt_fm", [128, 8 * T], F16)
    scratch("spre_fm", [128, 8 * T], BF16)
    scratch("x1_fm", [128, 8 * T], F32)
    scratch("xmid_fm", [128, 8 * T], F32)
    scratch("xnn1_fm", [128, 8 * T], BF16)
    scratch("kk_fm", [128, 32 * 512 * 4], BF16)
    scratch("hf_fm", [128, 32 * 512 * 4], BF16)
    scratch("xe_tm", [NE * 128, 4 * D], BF16)
    scratch("WqfcT_d", [D, FI], BF16)
    scratch("WqprojT_d", [FI, D], BF16)

    outp("out_fm", [128, 8 * T])
    for name in taps:
        shape, dt_ = TAP_SHAPES[name]
        outp("tap_" + name, shape, dt_)

    p = P(nc)
    with contextlib.ExitStack() as stack:
        p.make_sems(stack)
        blk = stack.enter_context(nc.Block())
        emit_program(p, A, set(taps))
        for qeng, sname in (("sync", "d_sy"), ("gpsimd", "d_gp"), ("scalar", "d_ac")):
            h = p.semh[sname]
            tv = p.cnt[sname]
            if tv:
                p.em(qeng, lambda e, h=h, tv=tv: e.wait_ge(h, tv))
        p.finalize_waits()

        @blk.sync
        def _(sync):
            for fn in p.ops["sync"]:
                fn(sync)

        @blk.gpsimd
        def _(g):
            for fn in p.ops["gpsimd"]:
                fn(g)

        @blk.vector
        def _(v):
            for fn in p.ops["vector"]:
                fn(v)

        @blk.scalar
        def _(s):
            for fn in p.ops["scalar"]:
                fn(s)

        @blk.tensor
        def _(t):
            for fn in p.ops["tensor"]:
                fn(t)

    return nc


def prep_inputs(b, kw):
    d = {}
    ct = np.ascontiguousarray
    d["xT"] = ct(kw["L0_attn_x"][b].T)
    d["WinT"] = ct(kw["L0_attn_W_in"].T)
    d["conv_wT"] = ct(kw["L0_attn_conv_w"][:, 0, :].T)
    d["WdtT"] = ct(kw["L0_attn_W_dt"].T)
    d["WBT"] = ct(kw["L0_attn_W_B"].T)
    d["WCT"] = ct(kw["L0_attn_W_C"].T)
    d["WoutT"] = ct(kw["L0_attn_W_out"].T)
    d["gate_wT"] = ct(kw["L0_ffn_gate_w"].T)
    d["wgT"] = ct(kw["L0_ffn_w_gate"].transpose(0, 2, 1)).reshape(NE * D, FI)
    d["wuT"] = ct(kw["L0_ffn_w_up"].transpose(0, 2, 1)).reshape(NE * D, FI)
    d["wdT"] = ct(kw["L0_ffn_w_down"].transpose(0, 2, 1)).reshape(NE * FI, D)
    d["WrT"] = ct(kw["L1_attn_Wr"].T)
    d["WkT"] = ct(kw["L1_attn_Wk"].T)
    d["WvT"] = ct(kw["L1_attn_Wv"].T)
    d["WoT"] = ct(kw["L1_attn_Wo"].T)
    d["WfkT"] = ct(kw["L1_attn_Wfk"].T)
    d["WfvT"] = ct(kw["L1_attn_Wfv"].T)
    d["WfrT"] = ct(kw["L1_attn_Wfr"].T)
    d["WfcT"] = ct(kw["L1_ffn_W_fc"].T)
    d["WprojT"] = ct(kw["L1_ffn_W_proj"].T)
    d["A_fcT"] = ct(kw["L1_ffn_A_fc"].T)
    d["B_fcT"] = ct(kw["L1_ffn_B_fc"].T)
    d["A_projT"] = ct(kw["L1_ffn_A_proj"].T)
    d["B_projT"] = ct(kw["L1_ffn_B_proj"].T)
    d["ln_g"] = kw["L0_attn_ln_g"].reshape(1, D)
    d["ln_b"] = kw["L0_attn_ln_b"].reshape(1, D)
    d["conv_b"] = kw["L0_attn_conv_b"].reshape(1, D)
    d["Dp"] = kw["L0_attn_Dp"].reshape(1, D)
    d["b_dt"] = kw["L0_attn_b_dt"].reshape(1, D)
    d["ln1_w"] = kw["L1_attn_ln1_w"].reshape(1, D)
    d["ln1_b"] = kw["L1_attn_ln1_b"].reshape(1, D)
    d["ln2_w"] = kw["L1_attn_ln2_w"].reshape(1, D)
    d["ln2_b"] = kw["L1_attn_ln2_b"].reshape(1, D)
    d["decay_v"] = kw["L1_attn_decay"].reshape(1, D)
    d["u_v"] = kw["L1_attn_u"].reshape(1, D)
    d["b_proj"] = kw["L1_ffn_b_proj"].reshape(1, D)
    d["b_fc"] = kw["L1_ffn_b_fc"].reshape(1, FI)
    d["A_log"] = kw["L0_attn_A_log"].reshape(1, NST)
    n_ = np.arange(NST)
    pp = np.arange(128)
    d["S1c"] = (n_[:, None] % 4 == pp[None, :] // 32).astype(np.float32)
    d["S2c"] = (n_[:, None] // 4 == np.arange(16)[None, :]).astype(np.float32)
    import ml_dtypes
    d["Ssum"] = (pp[:, None] % 32 == np.arange(32)[None, :]).astype(ml_dtypes.bfloat16)
    sel3 = np.zeros((NST, 16, 128), np.float32)
    for q in range(16):
        sel3[:, q, :] = (n_[:, None] == q * 4 + pp[None, :] // 32)
    d["Ssel3"] = sel3.reshape(NST, 2048).astype(ml_dtypes.bfloat16)
    d["eye"] = np.eye(128).astype(ml_dtypes.bfloat16)
    d["eyef"] = np.eye(128).astype(np.float32)
    return d


def kernel(**kw):
    nc = build_nc(taps=DEBUG_TAPS)
    in_maps = [prep_inputs(b, kw) for b in range(8)]
    res = run_bass_kernel_spmd(nc, in_maps, core_ids=list(range(8)))
    outs = []
    for b in range(8):
        o = res.results[b]["out_fm"]
        outs.append(np.asarray(o).reshape(128, 8, T).transpose(2, 1, 0).reshape(T, D))
    return np.stack(outs, axis=0)
